# version 10
# baseline (speedup 1.0000x reference)
import math
import numpy as np
from contextlib import ExitStack

import concourse.bass as bass
import concourse.mybir as mybir
from concourse.bass import IndirectOffsetOnAxis
from concourse.bass_utils import run_bass_kernel_spmd

F32 = mybir.dt.float32
BF16 = mybir.dt.bfloat16
I32 = mybir.dt.int32
ALU = mybir.AluOpType
ACT = mybir.ActivationFunctionType
AXX = mybir.AxisListType.X

NCORES = 8
D = 2048
DC = D // 128
HPC = 2
HD = 128
ESL = 3 * HPC * HD
ISQ = 1.0 / math.sqrt(HD)
LN_EPS = 1e-5


def groups(total, start=0, maxn=512):
    out = []
    lo = start
    while lo < total:
        hi = min(total, (lo // maxn + 1) * maxn)
        out.append((lo, hi))
        lo = hi
    return out


def build_nc(S, HID, dbg=False):
    SW = S // 8
    NSTRIP = S // SW
    SKT = SW // 128
    NB = max(1, SW // 512)
    CPS = SW // (S // 16)
    TOKC = S // 16
    TOK = 2 * TOKC
    HC = HID // 128
    WIN = 512
    NWIN = S // WIN

    nc = bass.Bass("TRN2", target_bir_lowering=False, debug=False)
    es = ExitStack()
    cnt = {}
    sems = {}

    def getsem(name):
        if name not in sems:
            sems[name] = es.enter_context(nc.semaphore(name))
        return sems[name]

    for sn in ("TS", "VS", "SS", "CC"):
        getsem(sn)

    def bump(name, by):
        cnt[name] = cnt.get(name, 0) + by
        return cnt[name]

    def wait(eng, ckv):
        eng.wait_ge(sems[ckv[0]], ckv[1])

    def sbuf(name, shape, dt=F32, st=None, side=None):
        return (st or es).enter_context(
            nc.sbuf_tensor(name, shape, dt, side=side))

    def psum(name, shape, st=None):
        return (st or es).enter_context(nc.psum_tensor(name, shape, F32))

    DI = lambda n, s, d: nc.dram_tensor(n, s, d, kind="ExternalInput").ap()
    xfull = DI("xfull", [128, DC * S], F32)
    xown_i = DI("xown", [128, DC * TOK], F32)
    wm_i = DI("wm", [128, DC * 12 * 128], F32)
    c_i = DI("c_col", [128, DC], F32)
    wqkv_i = DI("wqkv", [128, DC * ESL], F32)
    bqkv_i = DI("bqkv_row", [1, ESL], F32)
    wout_i = DI("wout", [128, DC * D], F32)
    bout_i = DI("bout_col", [128, DC], F32)
    wg_i = DI("wg", [128, HC * DC * 128], I32)
    wu_i = DI("wu", [128, HC * DC * 128], I32)
    wd_i = DI("wd", [128, DC * HC * 128], I32)
    sg_i = DI("sg_col", [128, HC], F32)
    su_i = DI("su_col", [128, HC], F32)
    sd_i = DI("sd_col", [128, DC], F32)
    tri_i = DI("tri", [128, 128], BF16)
    pr16_i = DI("pramp16", [128, DC], F32)
    gbase_i = DI("gbase", [128, 2], F32)
    out_t = nc.dram_tensor("out_t", [128, DC * TOK], F32,
                           kind="ExternalOutput").ap()
    if dbg:
        dbg_mod = nc.dram_tensor("dbg_mod", [128, NCORES * 12], F32,
                                 kind="ExternalOutput").ap()
        dbg_st = nc.dram_tensor("dbg_st", [NCORES, 2 * TOK], F32,
                                kind="ExternalOutput").ap()
        dbg_rows = nc.dram_tensor("dbg_rows", [34, S], BF16,
                                  kind="ExternalOutput").ap()
        dbg_corr = nc.dram_tensor("dbg_corr", [34, ESL], BF16,
                                  kind="ExternalOutput").ap()
        dbg_qf = nc.dram_tensor("dbg_qf", [128, HPC * S], BF16,
                                kind="ExternalOutput").ap()
        dbg_kf = nc.dram_tensor("dbg_kf", [128, HPC * S], BF16,
                                kind="ExternalOutput").ap()
        dbg_vt = nc.dram_tensor("dbg_vt", [128, HPC * S], BF16,
                                kind="ExternalOutput").ap()
        dbg_amg = nc.dram_tensor("dbg_amg", [128, DC * TOKC], BF16,
                                 kind="ExternalOutput").ap()
        dbg_h2 = nc.dram_tensor("dbg_h2", [128, DC * TOK], BF16,
                                kind="ExternalOutput").ap()
        dbg_agi = nc.dram_tensor("dbg_agi", [NSTRIP * HPC * HD, SW], BF16,
                                 kind="ExternalOutput").ap()
        dbg_m0 = nc.dram_tensor("dbg_m0", [128, HPC * 4], F32,
                                kind="ExternalOutput").ap()
        dbg_nrm = nc.dram_tensor("dbg_nrm", [2, HPC * NWIN], F32,
                                 kind="ExternalOutput").ap()
        dbg_pacc = nc.dram_tensor("dbg_pacc", [128, SW], F32,
                                  kind="ExternalOutput").ap()
        dbg_rinv = nc.dram_tensor("dbg_rinv", [1, SW], BF16,
                                  kind="ExternalOutput").ap()
        dbg_agab = nc.dram_tensor("dbg_agab", [NSTRIP // 2 * D, SW], BF16,
                                  kind="ExternalOutput").ap()

    agma_i = nc.dram_tensor("agma_i", [128, 4], F32)
    agma_o = nc.dram_tensor("agma_o", [NCORES * 128, 4], F32)
    agmb_i = nc.dram_tensor("agmb_i", [128, 8], F32)
    agmb_o = nc.dram_tensor("agmb_o", [NCORES * 128, 8], F32)
    agst_i = nc.dram_tensor("agst_i", [1, 2 * TOK], F32)
    agst_o = nc.dram_tensor("agst_o", [NCORES, 2 * TOK], F32)
    ago_in = nc.dram_tensor("ago_in", [NSTRIP * HPC * HD, SW], BF16)
    ago_ab = nc.dram_tensor("ago_ab", [NSTRIP // 2 * D, SW], BF16,
                            addr_space="Shared")
    ago_cd = nc.dram_tensor("ago_cd", [NSTRIP // 2 * D, SW], BF16,
                            addr_space="Shared")
    ago_flat_a = ago_ab.ap().rearrange("a b -> (a b)")[:, None]
    ago_flat_b = ago_cd.ap().rearrange("a b -> (a b)")[:, None]
    x1_sp = nc.dram_tensor("x1_sp", [128, DC * TOK], F32)

    ones_cf = sbuf("ones_cf", [128, 1], F32)
    ones_cb = sbuf("ones_cb", [128, 1], BF16)
    ones2b = sbuf("ones2b", [2, 128], BF16)
    neg128b = sbuf("neg128b", [2, 128], BF16)
    ones_rf = sbuf("ones_rf", [1, 128], F32)
    ones_rb = sbuf("ones_rb", [1, 128], BF16)
    tri01 = sbuf("tri01", [128, 128], BF16)
    pr16 = sbuf("pr16", [128, DC])
    gbase = sbuf("gbase_s", [128, 2])
    c_sb = sbuf("c_sb", [128, DC])
    csilu = sbuf("csilu", [128, DC])
    sa_c = sbuf("sa_c", [128, DC])
    sab_c = sbuf("sab_c", [128, DC], BF16)
    ca1_c = sbuf("ca1_c", [128, DC])
    ga_c = sbuf("ga_c", [128, DC])
    sb_c = sbuf("sb_c", [128, DC])
    cb1_c = sbuf("cb1_c", [128, DC])
    gb_c = sbuf("gb_c", [128, DC])
    gabout_c = sbuf("gabout_c", [128, DC])
    gbsd_c = sbuf("gbsd_c", [128, DC])
    sd_c = sbuf("sd_c", [128, DC])
    bout_c = sbuf("bout_c", [128, DC])
    sg_c = sbuf("sg_c", [128, HC])
    su_c = sbuf("su_c", [128, HC])
    negm0 = sbuf("negm0", [128, HPC * NSTRIP])
    m0s = sbuf("m0s", [1, 4])
    idx16f = sbuf("idx16f", [128, DC])
    idx16 = [sbuf(f"idx16_{g}", [128, DC], I32) for g in range(2)]

    block = es.enter_context(nc.Block())
    ck = {}

    def dma(eng, stream, out, in_, **kw):
        sem = getsem(stream)
        eng.dma_start(out=out, in_=in_, **kw).then_inc(sem, 16)
        return (stream, bump(stream, 16))

    def t_inc(inst):
        inst.then_inc(sems["TS"], 1)
        return ("TS", bump("TS", 1))

    def v_inc(inst):
        inst.then_inc(sems["VS"], 1)
        return ("VS", bump("VS", 1))

    def s_inc(inst):
        inst.then_inc(sems["SS"], 1)
        return ("SS", bump("SS", 1))

    def c_inc(inst):
        inst.then_inc(sems["CC"], 1)
        return ("CC", bump("CC", 1))

    RG = [list(range(NCORES))]

    p23 = ExitStack()
    wq_sb = sbuf("wq_sb", [128, DC * ESL], BF16, st=p23, side="right")
    corr4 = sbuf("corr4", [34, ESL], BF16, st=p23, side="right")
    corr_l = corr4[:][32:34, :]
    rows4 = sbuf("rows4", [34, S], BF16, st=p23, side="right")
    xw = [sbuf(f"xw{i}", [128, DC * WIN], BF16, st=p23, side="right")
          for i in range(2)]
    v_win = [sbuf(f"vwin{i}", [128, HPC * WIN], BF16, st=p23, side="right")
             for i in range(2)]
    sq_scr = sbuf("sq_scr", [128, WIN], BF16, st=p23, side="right")
    rep_sb = sbuf("rep_sb", [128, WIN], st=p23, side="right")
    rrows = rows4[:][0:2, :]
    crows = rows4[:][32:34, :]

    p01 = ExitStack()
    brow_f = sbuf("brow_f", [1, ESL], st=p01, side="right")
    wm_t = [sbuf(f"wm_t{i}", [128, 12 * 128], st=p01, side="right")
            for i in range(2)]
    xo_b = [sbuf(f"xo_b{i}", [128, TOK], BF16, st=p01, side="right")
            for i in range(2)]
    xo_s = [sbuf(f"xo_s{i}", [128, TOK], BF16, st=p01, side="right")
            for i in range(2)]
    stat8 = sbuf("stat8", [NCORES, 2 * TOK], st=p01, side="right")
    r8 = sbuf("r8", [NCORES, TOK], st=p01, side="right")
    m8n = sbuf("m8n", [NCORES, TOK], st=p01, side="right")
    lnscr8 = sbuf("lnscr8", [NCORES, TOK], st=p01, side="right")
    r8hi = sbuf("r8hi", [NCORES, TOK], BF16, st=p01, side="right")
    r8lo = sbuf("r8lo", [NCORES, TOK], BF16, st=p01, side="right")
    st_m = sbuf("st_m", [1, TOK], st=p01, side="right")
    st_v = sbuf("st_v", [1, TOK], st=p01, side="right")
    st_sd = sbuf("st_sd", [1, TOK], st=p01, side="right")
    modc = sbuf("modc", [128, NCORES * 12], st=p01, side="right")
    modacc = sbuf("modacc", [128, 12], st=p01, side="right")
    g_scr = sbuf("g_scr", [128, DC], st=p01, side="right")
    p_mod = psum("p_mod", [128, 12], st=p01)
    SQO = max(TOK, 512)
    p_st = psum("p_st", [1, SQO + TOK], st=p01)
    p_gemv = psum("p_gemv", [64, ESL], st=p01)

    @block.sync
    def _(sync):
        dma(sync, "cst", c_sb[:], c_i)
        dma(sync, "cst", tri01[:], tri_i)
        dma(sync, "cst", pr16[:], pr16_i)
        dma(sync, "cst", gbase[:], gbase_i)
        dma(sync, "cst", sg_c[:], sg_i)
        dma(sync, "cst", su_c[:], su_i)
        dma(sync, "cst", sd_c[:], sd_i)
        dma(sync, "cst", bout_c[:], bout_i)
        ck["ld_consts"] = dma(sync, "cst", brow_f[0:1, :], bqkv_i)

    @block.vector
    def _(v):
        v.memset(ones_cf[:], 1.0)
        v.memset(ones_cb[:], 1.0)
        v.memset(ones2b[:], 1.0)
        v.memset(neg128b[:], -128.0)
        v.memset(ones_rf[:], 1.0)
        ck["v_consts"] = v_inc(v.memset(ones_rb[:], 1.0))

    @block.scalar
    def _(s):
        wait(s, ck["ld_consts"])
        ck["s_csilu"] = s_inc(s.activation(csilu[:], c_sb[:], ACT.Silu))

    ck["xo_ld"] = {}
    ck["xo_sq"] = {}
    ck["xo_mm"] = {}
    for cd in range(DC):
        @block.gpsimd
        def _(g, cd=cd):
            if cd >= 2:
                wait(g, ck["xo_mm"][cd - 2])
            ck["xo_ld"][cd] = dma(g, f"xo{cd % 2}", xo_b[cd % 2][:],
                                  xown_i[:, cd * TOK:(cd + 1) * TOK])

        @block.vector
        def _(v, cd=cd):
            wait(v, ck["xo_ld"][cd])
            ck["xo_sq"][cd] = v_inc(v.tensor_mul(
                xo_s[cd % 2][:], xo_b[cd % 2][:], xo_b[cd % 2][:]))

        @block.tensor
        def _(t, cd=cd):
            wait(t, ck["xo_ld"][cd])
            if cd == 0:
                wait(t, ck["v_consts"])
            last = None
            for (lo, hi) in groups(TOK):
                last = t.matmul(p_st[0:1, lo:hi], ones_cb[:],
                                xo_b[cd % 2][:, lo:hi],
                                start=(cd == 0), stop=(cd == DC - 1),
                                skip_group_check=True)
            wait(t, ck["xo_sq"][cd])
            for (lo, hi) in groups(TOK):
                last = t.matmul(p_st[0:1, SQO + lo:SQO + hi], ones_cb[:],
                                xo_s[cd % 2][:, lo:hi],
                                start=(cd == 0), stop=(cd == DC - 1),
                                skip_group_check=True)
            ck["xo_mm"][cd] = t_inc(last)
    ck["mm_sq"] = ck["xo_mm"][DC - 1]

    ck["ld_wm"] = {}


    ck["mm_mod"] = []
    ck["v_macc"] = {}

    @block.vector
    def _(v):
        ck["v_macc0"] = v_inc(v.memset(modacc[:], 0.0))

    for cd in range(DC):
        @block.sync
        def _(sync, cd=cd):
            if cd >= 2:
                wait(sync, ck["mm_mod"][cd - 2])
            ck["ld_wm"][cd] = dma(sync, f"wm{cd % 2}", wm_t[cd % 2][:],
                                  wm_i[:, cd * 1536:(cd + 1) * 1536])

        @block.tensor
        def _(t, cd=cd):
            if cd == 0:
                wait(t, ck["s_csilu"])
            wait(t, ck["ld_wm"][cd])
            if cd >= 1:
                wait(t, ck["v_macc"][cd - 1])
            last = None
            for j in range(12):
                last = t.matmul(p_mod[:, j:j + 1],
                                wm_t[cd % 2][:, j * 128:(j + 1) * 128],
                                csilu[:, cd:cd + 1],
                                start=True, stop=True,
                                skip_group_check=True)
            ck["mm_mod"].append(t_inc(last))

        @block.vector
        def _(v, cd=cd):
            wait(v, ck["mm_mod"][cd])
            if cd == 0:
                wait(v, ck["v_macc0"])
            ck["v_macc"][cd] = v_inc(v.tensor_add(modacc[:], modacc[:],
                                                   p_mod[:]))
            v.drain()

    @block.vector
    def _(v):
        wait(v, ck["v_macc"][DC - 1])
        ck["v_moda"] = v_inc(v.tensor_copy(out=modc[:, 0:4],
                                           in_=modacc[:, 0:4]))
        ck["v_modb"] = v_inc(v.tensor_copy(out=modc[:, 4:12],
                                           in_=modacc[:, 4:12]))

    @block.sync
    def _(sync):
        wait(sync, ck["v_moda"])
        ck["st_moda"] = dma(sync, "stma", agma_i.ap(), modc[:, 0:4])
        wait(sync, ck["v_modb"])
        ck["st_modb"] = dma(sync, "stmb", agmb_i.ap(), modc[:, 4:12])

    @block.vector
    def _(v):
        wait(v, ck["mm_sq"])
        v.tensor_scalar_mul(st_m[:], p_st[0:1, 0:TOK], 1.0 / D)
        v.tensor_scalar_mul(st_v[:], p_st[0:1, SQO:SQO + TOK], 1.0 / D)
        v.drain()
        v.tensor_mul(lnscr8[0:1, :], st_m[:], st_m[:])
        v.drain()
        v.tensor_sub(st_v[:], st_v[:], lnscr8[0:1, :])
        v.drain()
        ck["v_var"] = v_inc(v.tensor_scalar_add(st_v[:], st_v[:], LN_EPS))

    @block.scalar
    def _(s):
        wait(s, ck["v_var"])
        ck["s_sd"] = s_inc(s.activation(st_sd[:], st_v[:], ACT.Sqrt))

    @block.sync
    def _(sync):
        wait(sync, ck["v_var"])
        dma(sync, "stst", agst_i.ap()[0:1, 0:TOK], st_m[:])
        wait(sync, ck["s_sd"])
        ck["st_stats"] = dma(sync, "stst", agst_i.ap()[0:1, TOK:2 * TOK],
                             st_sd[:])

    def emit_xwin(g, w):
        r = None
        for cd in range(DC):
            r = dma(g, f"xw{w % 2}", xw[w % 2][:, cd * WIN:(cd + 1) * WIN],
                    xfull[:, cd * S + w * WIN:cd * S + (w + 1) * WIN])
        return r

    ck["xwin"] = {}

    @block.gpsimd
    def _(g):
        ck["xwin"][0] = emit_xwin(g, 0)
        ck["ld_wq"] = []
        for cd in range(DC):
            ck["ld_wq"].append(
                dma(g, "wqc", wq_sb[:, cd * ESL:(cd + 1) * ESL],
                    wqkv_i[:, cd * ESL:(cd + 1) * ESL]))
        ck["xwin"][1] = emit_xwin(g, 1)
        wait(g, ck["st_moda"])
        ck["cc_moda"] = c_inc(g.collective_compute(
            "AllGather", ALU.bypass, replica_groups=RG,
            ins=[agma_i.ap().opt()], outs=[agma_o.ap().opt()]))
        wait(g, ck["st_modb"])
        ck["cc_modb"] = c_inc(g.collective_compute(
            "AllGather", ALU.bypass, replica_groups=RG,
            ins=[agmb_i.ap().opt()], outs=[agmb_o.ap().opt()]))
        wait(g, ck["st_stats"])
        ck["cc_stats"] = c_inc(g.collective_compute(
            "AllGather", ALU.bypass, replica_groups=RG,
            ins=[agst_i.ap().opt()], outs=[agst_o.ap().opt()]))

    @block.sync
    def _(sync):
        wait(sync, ck["cc_moda"])
        for b in range(NCORES):
            ck["ld_mca"] = dma(sync, "mbk", modc[:, b * 12:b * 12 + 4],
                               agma_o.ap()[b * 128:(b + 1) * 128, :])
        wait(sync, ck["cc_modb"])
        for b in range(NCORES):
            ck["ld_mcb"] = dma(sync, "mbk", modc[:, b * 12 + 4:b * 12 + 12],
                               agmb_o.ap()[b * 128:(b + 1) * 128, :])
        wait(sync, ck["cc_stats"])
        ck["ld_rows"] = dma(sync, "mbk", stat8[:], agst_o.ap())

    @block.vector
    def _(v):
        wait(v, ck["ld_rows"])
        mv = modc[:].rearrange("p (b j) -> p b j", j=12)
        v.tensor_copy(out=sa_c[:].rearrange("p (b j) -> p b j", j=2),
                      in_=mv[:, :, 0:2])
        v.tensor_copy(out=ca1_c[:].rearrange("p (b j) -> p b j", j=2),
                      in_=mv[:, :, 2:4])
        v.drain()
        v.tensor_copy(out=sab_c[:], in_=sa_c[:])
        ck["v_saca"] = v_inc(v.tensor_scalar_add(ca1_c[:], ca1_c[:], 1.0))
        v.tensor_copy(out=ga_c[:].rearrange("p (b j) -> p b j", j=2),
                      in_=mv[:, :, 4:6])
        v.tensor_copy(out=sb_c[:].rearrange("p (b j) -> p b j", j=2),
                      in_=mv[:, :, 6:8])
        v.tensor_copy(out=cb1_c[:].rearrange("p (b j) -> p b j", j=2),
                      in_=mv[:, :, 8:10])
        v.tensor_copy(out=gb_c[:].rearrange("p (b j) -> p b j", j=2),
                      in_=mv[:, :, 10:12])
        v.drain()
        v.tensor_scalar_add(cb1_c[:], cb1_c[:], 1.0)
        v.drain()
        for gt in (ga_c, gb_c):
            v.tensor_scalar(g_scr[:], gt[:], 0.0, 2.0, ALU.is_ge, ALU.mult)
            v.drain()
            v.tensor_scalar_add(g_scr[:], g_scr[:], -1.0)
            v.drain()
            v.tensor_mul(gt[:], gt[:], g_scr[:])
            v.drain()
            v.tensor_scalar_max(gt[:], gt[:], 0.25)
            v.drain()
            v.tensor_mul(gt[:], gt[:], g_scr[:])
            v.drain()
        v.tensor_mul(gabout_c[:], ga_c[:], bout_c[:])
        ck["v_mod"] = v_inc(v.tensor_mul(gbsd_c[:], gb_c[:], sd_c[:]))
        v.tensor_scalar(idx16f[:], pr16[:], gbase[:, 0:1], None, ALU.add)
        v.drain()
        v.tensor_copy(out=idx16[0][:], in_=idx16f[:])
        v.drain()
        v.tensor_scalar(idx16f[:], pr16[:], gbase[:, 1:2], None, ALU.add)
        v.drain()
        ck["v_idx"] = v_inc(v.tensor_copy(out=idx16[1][:], in_=idx16f[:]))

    @block.scalar
    def _(s):
        wait(s, ck["ld_rows"])
        s.activation(lnscr8[:], stat8[:, TOK:2 * TOK], ACT.Ln)
        s.drain()
        ck["s_rall"] = s_inc(s.activation(r8[:], lnscr8[:], ACT.Exp,
                                          scale=-1.0))

    @block.vector
    def _(v):
        wait(v, ck["s_rall"])
        v.tensor_copy(out=r8hi[:], in_=r8[:])
        v.drain()
        v.tensor_sub(lnscr8[:], r8[:], r8hi[:])
        v.drain()
        v.tensor_copy(out=r8lo[:], in_=lnscr8[:])
        ck["v_rows"] = v_inc(v.tensor_scalar_mul(m8n[:], stat8[:, 0:TOK],
                                                 -1.0))

    @block.sync
    def _(sync):
        wait(sync, ck["v_rows"])
        for K in range(16):
            sK = K if K <= 7 else 15 - K
            off = 0 if K <= 7 else TOKC
            cs = slice(K * TOKC, (K + 1) * TOKC)
            dma(sync, "rw2", rows4[0:1, cs], r8hi[sK:sK + 1, off:off + TOKC])
            dma(sync, "rw2", rows4[1:2, cs], r8lo[sK:sK + 1, off:off + TOKC])
        ck["d_rw2"] = ("rw2", cnt["rw2"])

    @block.gpsimd
    def _(g):
        wait(g, ck["v_rows"])
        for K in range(16):
            sK = K if K <= 7 else 15 - K
            off = 0 if K <= 7 else TOKC
            cs = slice(K * TOKC, (K + 1) * TOKC)
            dma(g, "rwd", rows4[32:33, cs], m8n[sK:sK + 1, off:off + TOKC])
            ck["d_sd"] = dma(g, "rwd", rows4[33:34, cs],
                             stat8[sK:sK + 1, TOK + off:TOK + off + TOKC])

    ck["wp_gemv"] = []
    ck["wp_scale"] = []
    ck["wp_sum"] = []
    for cd in range(DC):
        @block.tensor
        def _(t, cd=cd):
            if cd == 0:
                wait(t, ck["ld_wq"][DC - 1])
                wait(t, ck["v_saca"])
            last = None
            for (lo, hi) in groups(ESL):
                last = t.matmul(p_gemv[0:1, lo:hi], sab_c[:, cd:cd + 1],
                                wq_sb[:, cd * ESL + lo:cd * ESL + hi],
                                start=(cd == 0), stop=(cd == DC - 1))
            ck["wp_gemv"].append(t_inc(last))

        @block.vector
        def _(v, cd=cd):
            wait(v, ck["wp_gemv"][cd])
            ck["wp_scale"].append(v_inc(v.tensor_scalar(
                wq_sb[:, cd * ESL:(cd + 1) * ESL],
                wq_sb[:, cd * ESL:(cd + 1) * ESL],
                ca1_c[:, cd:cd + 1], None, ALU.mult)))

        @block.tensor
        def _(t, cd=cd):
            wait(t, ck["wp_scale"][cd])
            last = None
            for (lo, hi) in groups(ESL):
                last = t.matmul(p_gemv[32:33, lo:hi], ones_cb[:],
                                wq_sb[:, cd * ESL + lo:cd * ESL + hi],
                                start=(cd == 0), stop=(cd == DC - 1))
            ck["wp_sum"].append(t_inc(last))

    @block.vector
    def _(v):
        wait(v, ck["wp_sum"][-1])
        v.tensor_add(brow_f[0:1, :], brow_f[0:1, :], p_gemv[0:1, :])
        v.drain()
        ck["v_corr"] = v_inc(v.tensor_copy(out=corr4[32:33, :],
                                           in_=p_gemv[32:33, :]))

    @block.gpsimd
    def _(g):
        wait(g, ck["v_corr"])
        ck["d_corr"] = dma(g, "rwd", corr4[33:34, :], brow_f[0:1, :])

    p01.close()

    p34 = ExitStack()
    q_fm = sbuf("q_fm", [128, HPC * S], BF16, st=p34)
    k_fm = sbuf("k_fm", [128, HPC * S], BF16, st=p34)
    v_tok = sbuf("v_tok", [128, HPC * S], BF16, st=p34)
    nrm_q = sbuf("nrm_q", [1, HPC * NWIN], st=p34)
    nrm_k = sbuf("nrm_k", [1, HPC * NWIN], st=p34)
    p3p = ExitStack()
    p_qkv = [psum(f"p_qkv{i}", [128, WIN], st=p3p) for i in range(2)]
    p_rep = psum("p_rep", [128, WIN], st=p3p)
    p_nrm = psum("p_nrm", [1, WIN], st=p3p)

    def qkv_dst(o, w):
        hl = o % 2
        if o < 2:
            return q_fm[:, hl * S + w * WIN:hl * S + (w + 1) * WIN]
        if o < 4:
            return k_fm[:, hl * S + w * WIN:hl * S + (w + 1) * WIN]
        return v_win[w % 2][:, hl * WIN:(hl + 1) * WIN]

    ck["win_done"] = {}
    ck["win_evac"] = {}
    ck["rep_cp"] = {}
    ck["win_tr"] = {}
    last_evac_buf = {0: None, 1: None}
    last_vwin_tr = {0: None, 1: None}
    last_nrm_v = None

    for w in range(NWIN):
        if w >= 2:
            @block.gpsimd
            def _(g, w=w):
                wait(g, ck["win_done"][w - 2])
                ck["xwin"][w] = emit_xwin(g, w)

        @block.tensor
        def _(t, w=w):
            wait(t, ck["xwin"][w])
            if w == 0:
                wait(t, ck["v_corr"])
                wait(t, ck["v_rows"])
                wait(t, ck["d_corr"])
                wait(t, ck["d_rw2"])
            if w >= 1:
                wait(t, ck["rep_cp"][w - 1])
            ck[("rep_mm", w)] = t_inc(t.matmul(
                p_rep[:], ones2b[:],
                rrows[:, w * WIN:(w + 1) * WIN], start=True, stop=True))

        @block.vector
        def _(v, w=w):
            wait(v, ck[("rep_mm", w)])
            ck["rep_cp"][w] = v_inc(v.tensor_copy(out=rep_sb[:], in_=p_rep[:]))

        for o in range(6):
            buf = o % 2

            @block.tensor
            def _(t, w=w, o=o, buf=buf):
                if last_evac_buf[buf] is not None:
                    wait(t, last_evac_buf[buf])
                for cd in range(DC):
                    t.matmul(p_qkv[buf][:],
                             wq_sb[:, cd * ESL + o * 128:cd * ESL + (o + 1) * 128],
                             xw[w % 2][:, cd * WIN:(cd + 1) * WIN],
                             start=(cd == 0), stop=False)
                last = t.matmul(p_qkv[buf][:],
                                corr_l[:, o * 128:(o + 1) * 128],
                                crows[:, w * WIN:(w + 1) * WIN],
                                start=False, stop=True)
                ck[("qkv_t", w, o)] = t_inc(last)

            @block.vector
            def _(v, w=w, o=o, buf=buf):
                wait(v, ck[("qkv_t", w, o)])
                if o == 0:
                    wait(v, ck["rep_cp"][w])
                if o >= 4 and last_vwin_tr[w % 2] is not None:
                    wait(v, last_vwin_tr[w % 2])
                last_evac_buf[buf] = v_inc(v.tensor_mul(
                    qkv_dst(o, w), p_qkv[buf][:], rep_sb[:]))

        ck["win_done"][w] = ("TS", cnt["TS"])
        ck["win_evac"][w] = ("VS", cnt["VS"])

        for i, (tens, nrm) in enumerate(((q_fm, nrm_q), (k_fm, nrm_k))):
            for hl in range(HPC):
                @block.vector
                def _(v, w=w, i=i, hl=hl, tens=tens):
                    wait(v, ck["win_evac"][w])
                    ck[("nrm_sq", w, i, hl)] = v_inc(v.tensor_mul(
                        sq_scr[:],
                        tens[:, hl * S + w * WIN:hl * S + (w + 1) * WIN],
                        tens[:, hl * S + w * WIN:hl * S + (w + 1) * WIN]))

                @block.tensor
                def _(t, w=w, i=i, hl=hl):
                    wait(t, ck[("nrm_sq", w, i, hl)])
                    if last_nrm_v is not None:
                        wait(t, last_nrm_v)
                    ck[("nrm_t", w, i, hl)] = t_inc(t.matmul(
                        p_nrm[:], ones_cb[:], sq_scr[:],
                        start=True, stop=True))

                @block.vector
                def _(v, w=w, i=i, hl=hl, nrm=nrm):
                    wait(v, ck[("nrm_t", w, i, hl)])
                    ck[("nrm_v", w, i, hl)] = v_inc(v.reduce_max(
                        nrm[0:1, hl * NWIN + w:hl * NWIN + w + 1],
                        p_nrm[:], AXX))
                last_nrm_v = ck[("nrm_v", w, i, hl)]

        @block.sync
        def _(sync, w=w):
            wait(sync, ck["win_evac"][w])
            r = None
            for hl in range(HPC):
                for j in range(WIN // 128):
                    tok0 = w * WIN + j * 128
                    r = dma(sync, f"vt{w % 2}",
                            v_tok[:, hl * S + tok0:hl * S + tok0 + 128],
                            v_win[w % 2][:, hl * WIN + j * 128:hl * WIN + (j + 1) * 128],
                            transpose=True)
            last_vwin_tr[w % 2] = r
            ck["win_tr"][w] = r

    ck["vt_all"] = [("vt0", cnt.get("vt0", 0)), ("vt1", cnt.get("vt1", 0))]

    if dbg:
        @block.sync
        def _(sync):
            wait(sync, ck["d_corr"])
            wait(sync, ck["d_rw2"])
            dma(sync, "dbgq", dbg_mod, modc[:])
            dma(sync, "dbgq", dbg_st, stat8[:])
            dma(sync, "dbgq", dbg_rows, rows4[:])
            dma(sync, "dbgq", dbg_corr, corr4[:])
            wait(sync, ck["win_evac"][NWIN - 1])
            wait(sync, ck["win_tr"][NWIN - 1])
            dma(sync, "dbgq", dbg_qf, q_fm[:])
            dma(sync, "dbgq", dbg_kf, k_fm[:])
            ck["dbg1"] = dma(sync, "dbgq", dbg_vt, v_tok[:])

    p23.close()
    p3p.close()

    pW = ExitStack()
    amg = sbuf("amg", [128, DC * TOKC], BF16, st=pW, side="right")
    wo_raw = [sbuf(f"wo_raw{i}", [128, DC * 128], F32, st=pW, side="right")
              for i in range(2)]
    wo_buf = [sbuf(f"wo_buf{i}", [128, DC * 128], BF16, st=pW, side="right")
              for i in range(2)]

    p4s = ExitStack()
    ptile = [sbuf(f"pt{i}", [128, SW], BF16, st=p4s, side="right")
             for i in range(3)]
    pacc = sbuf("pacc", [128, SW], st=p4s, side="right")
    sum_row = sbuf("sum_row", [1, SW], st=p4s, side="right")
    lnrow = sbuf("lnrow", [1, SW], st=p4s, side="right")
    rinv_row = sbuf("rinv_row", [1, SW], BF16, st=p4s, side="right")
    astrip = [sbuf(f"astrip{i}", [128, SW], BF16, st=p4s, side="right")
              for i in range(2)]
    nrep_sb = sbuf("nrep_sb", [128, SW], st=p4s, side="right")
    p_sc = [psum(f"p_sc{i}", [128, SW], st=p4s) for i in range(3)]
    p_out = psum("p_out", [128, SW], st=p4s)

    @block.vector
    def _(v):
        ck["v_pacc0"] = v_inc(v.memset(pacc[:], 0.0))

    last_strip_evac = None
    last_pacc_reset = ck["v_pacc0"]
    ck["cc_strip"] = {}

    for s in range(NSTRIP):
        for hl in range(HPC):
            sidx = s * HPC + hl
            nkt = (s + 1) * SKT
            wq0 = (s * SW) // WIN
            wq1 = ((s + 1) * SW + WIN - 1) // WIN

            @block.vector
            def _(v, hl=hl, wq0=wq0, wq1=wq1):
                v.reduce_max(m0s[0:1, 0:1],
                             nrm_q[0:1, hl * NWIN + wq0:hl * NWIN + wq1], AXX)
                v.reduce_max(m0s[0:1, 1:2],
                             nrm_k[0:1, hl * NWIN:hl * NWIN + wq1], AXX)
                v.drain()
                ck["m0a"] = v_inc(v.tensor_mul(m0s[0:1, 2:3], m0s[0:1, 0:1],
                                               m0s[0:1, 1:2]))

            @block.scalar
            def _(s_):
                wait(s_, ck["m0a"])
                s_.activation(m0s[0:1, 3:4], m0s[0:1, 2:3], ACT.Ln)
                s_.drain()
                ck["m0b"] = s_inc(s_.activation(m0s[0:1, 2:3], m0s[0:1, 3:4],
                                                ACT.Exp, scale=0.5))

            @block.vector
            def _(v):
                wait(v, ck["m0b"])
                ck["m0c"] = v_inc(v.tensor_scalar(
                    m0s[0:1, 0:1], m0s[0:1, 2:3], -ISQ, -1.0,
                    ALU.mult, ALU.add))

            @block.tensor
            def _(t):
                wait(t, ck["m0c"])
                if last_strip_evac is not None:
                    wait(t, last_strip_evac)
                ck["m0t"] = t_inc(t.matmul(p_sc[0][:, 0:1], ones_rf[:],
                                           m0s[0:1, 0:1],
                                           start=True, stop=True))

            @block.vector
            def _(v, sidx=sidx):
                wait(v, ck["m0t"])
                ck[("m0", sidx)] = v_inc(v.tensor_copy(
                    out=negm0[:, sidx:sidx + 1], in_=p_sc[0][:, 0:1]))

            exp_ck = {}
            mask_ck = {}
            pacc_ck = {}
            attnv_ck = {}
            for kt in range(nkt):
                qlo = max(0, 128 * kt - s * SW)
                diag = 128 * kt >= s * SW
                buf = kt % 3

                @block.tensor
                def _(t, s=s, hl=hl, kt=kt, qlo=qlo, buf=buf, sidx=sidx,
                      wq1=wq1):
                    if kt == 0:
                        wait(t, ck[("m0", sidx)])
                        wait(t, ck["vt_all"][0])
                        wait(t, ck["vt_all"][1])
                    if kt >= 3:
                        wait(t, exp_ck[kt - 3])
                    last = None
                    for (lo, hi) in groups(SW, qlo):
                        last = t.matmul(
                            p_sc[buf][:, lo:hi],
                            k_fm[:, hl * S + kt * 128:hl * S + (kt + 1) * 128],
                            q_fm[:, hl * S + s * SW + lo:hl * S + s * SW + hi],
                            start=True, stop=True)
                    ck[("sc", sidx, kt)] = t_inc(last)

                @block.scalar
                def _(s_, kt=kt, qlo=qlo, buf=buf, sidx=sidx):
                    wait(s_, ck[("sc", sidx, kt)])
                    if kt >= 3:
                        wait(s_, attnv_ck[kt - 3])
                        wait(s_, pacc_ck[kt - 3])
                    exp_ck[kt] = s_inc(s_.activation(
                        ptile[buf][:, qlo:SW], p_sc[buf][:, qlo:SW], ACT.Exp,
                        bias=negm0[:, sidx:sidx + 1], scale=ISQ))

                @block.vector
                def _(v, kt=kt, qlo=qlo, buf=buf, diag=diag):
                    wait(v, exp_ck[kt])
                    if diag:
                        mask_ck[kt] = v_inc(v.tensor_mul(
                            ptile[buf][:, qlo:qlo + 128],
                            ptile[buf][:, qlo:qlo + 128], tri01[:]))
                        v.drain()
                    if kt == 0:
                        wait(v, last_pacc_reset)
                    pacc_ck[kt] = v_inc(v.tensor_add(
                        pacc[:, qlo:SW], pacc[:, qlo:SW],
                        ptile[buf][:, qlo:SW]))
                    v.drain()

                @block.tensor
                def _(t, s=s, hl=hl, kt=kt, qlo=qlo, buf=buf, nkt=nkt,
                      diag=diag):
                    if diag:
                        wait(t, mask_ck[kt])
                    else:
                        wait(t, exp_ck[kt])
                    last = None
                    for b in range(qlo // 512, NB):
                        lo = max(qlo, 512 * b)
                        hi = min(SW, 512 * (b + 1))
                        ktl = min(nkt - 1, s * SKT + 4 * (b + 1) - 1)
                        last = t.matmul(
                            p_out[:, lo:hi],
                            v_tok[:, hl * S + kt * 128:hl * S + (kt + 1) * 128],
                            ptile[buf][:, lo:hi],
                            start=(kt == 0), stop=(kt == ktl))
                    attnv_ck[kt] = t_inc(last)

            @block.tensor
            def _(t, nkt=nkt, sidx=sidx):
                wait(t, pacc_ck[nkt - 1])
                for back in range(1, min(3, nkt) + 1):
                    wait(t, exp_ck[nkt - back])
                last = None
                for (lo, hi) in groups(SW):
                    last = t.matmul(p_sc[0][0:1, lo:hi], ones_cf[:],
                                    pacc[:, lo:hi], start=True, stop=True)
                ck[("sums", sidx)] = t_inc(last)

            @block.vector
            def _(v, sidx=sidx):
                wait(v, ck[("sums", sidx)])
                ck[("sumr", sidx)] = v_inc(v.tensor_copy(
                    out=sum_row[:], in_=p_sc[0][0:1, 0:SW]))
                v.drain()
                ck[("paccr", sidx)] = v_inc(v.memset(pacc[:], 0.0))

            if dbg and sidx == 0:
                @block.sync
                def _(sync):
                    wait(sync, ck[("sumr", 0)])
                    dma(sync, "dbgq", dbg_pacc, pacc[:])

            last_pacc_reset = ck[("paccr", sidx)]

            @block.tensor
            def _(t, sidx=sidx, nkt=nkt):
                wait(t, ck[("sumr", sidx)])
                wait(t, attnv_ck[nkt - 1])
                last = None
                for (lo, hi) in groups(SW):
                    last = t.matmul(p_sc[0][:, lo:hi], ones_rf[:],
                                    sum_row[0:1, lo:hi],
                                    start=True, stop=True)
                ck[("repn", sidx)] = t_inc(last)

            @block.vector
            def _(v, sidx=sidx):
                wait(v, ck[("repn", sidx)])
                v.reciprocal(out=nrep_sb[:], in_=p_sc[0][:])
                v.drain()
                if sidx >= 2:
                    wait(v, ck[("agst", sidx - 2)])
                ck[("evac", sidx)] = v_inc(v.tensor_mul(
                    astrip[sidx % 2][:], p_out[:], nrep_sb[:]))

            last_strip_evac = ck[("evac", sidx)]

            @block.gpsimd
            def _(g, s=s, hl=hl, sidx=sidx):
                wait(g, ck[("evac", sidx)])
                ck[("agst", sidx)] = dma(
                    g, f"ag{sidx % 2}",
                    ago_in.ap()[sidx * 128:(sidx + 1) * 128, :],
                    astrip[sidx % 2][:])

        @block.gpsimd
        def _(g, s=s):
            for hl in range(HPC):
                wait(g, ck[("agst", s * HPC + hl)])
            ck["cc_strip"][s] = c_inc(g.collective_compute(
                "AllGather", ALU.bypass, replica_groups=RG,
                ins=[ago_in.ap()[s * HPC * 128:(s + 1) * HPC * 128, :].opt()],
                outs=[(ago_ab if s < NSTRIP // 2 else ago_cd).ap()
                      [(s % (NSTRIP // 2)) * D:
                       (s % (NSTRIP // 2) + 1) * D, :].opt()]))

        if s == NSTRIP // 2 - 1:
            @block.gpsimd
            def _(g):
                wait(g, ck["cc_strip"][NSTRIP // 2 - 1])
                wait(g, ck["v_idx"])
                r = None
                for rr in range(DC):
                    g.indirect_dma_start(
                        amg[:, rr * TOKC:(rr + 1) * TOKC], None,
                        ago_flat_a,
                        IndirectOffsetOnAxis(idx16[0][:, rr:rr + 1], 0),
                    ).then_inc(getsem("gth"), 16)
                    r = ("gth", bump("gth", 16))
                ck[("gather", 0)] = r

    p4s.close()
    p34.close()

    pH = ExitStack()
    h2 = sbuf("h2", [128, DC * TOK], BF16, st=pH)
    p67 = ExitStack()
    xr_roll = [sbuf(f"xr{i}", [128, TOKC], st=p67, side="right")
               for i in range(2)]
    x1_roll = [sbuf(f"x1r{i}", [128, TOKC], st=p67, side="right")
               for i in range(2)]
    x1c_bf = sbuf("x1c_bf", [128, DC * TOKC], BF16, st=p67, side="right")
    x1s_bf = sbuf("x1s_bf", [128, DC * TOKC], BF16, st=p67, side="right")
    ln2row = sbuf("ln2row", [1, 2 * TOKC], st=p67, side="right")
    ln2m = sbuf("ln2m", [1, TOKC], st=p67, side="right")
    ln2v = sbuf("ln2v", [1, TOKC], st=p67, side="right")
    ln2sd = sbuf("ln2sd", [1, TOKC], st=p67, side="right")
    ln2r = sbuf("ln2r", [1, TOKC], st=p67, side="right")
    apply_scr = sbuf("apply_scr", [128, TOKC], st=p67, side="right")
    p_op = [psum(f"p_op{i}", [128, TOKC], st=p67) for i in range(2)]
    SQO2 = max(TOKC, 512)
    p_st2 = psum("p_st2", [1, SQO2 + TOKC], st=p67)
    p_r2m = psum("p_r2m", [128, TOKC], st=p67)
    p_r2r = psum("p_r2r", [128, TOKC], st=p67)

    wov = wout_i.rearrange("p (cd e) -> p cd e", e=D)
    h2_ck = []
    prv = {k: {0: None, 1: None} for k in
           ("woc", "opt", "opx1", "opev", "x1sp")}

    for g_ in range(2):
        if g_ == 1:
            @block.gpsimd
            def _(g):
                wait(g, ck["cc_strip"][NSTRIP - 1])
                wait(g, ck[("opt", 0, DC - 1)])
                r = None
                for rr in range(DC):
                    g.indirect_dma_start(
                        amg[:, rr * TOKC:(rr + 1) * TOKC], None,
                        ago_flat_b,
                        IndirectOffsetOnAxis(idx16[1][:, rr:rr + 1], 0),
                    ).then_inc(getsem("gth"), 16)
                    r = ("gth", bump("gth", 16))
                ck[("gather", 1)] = r

        for o in range(DC):
            bf = o % 2

            @block.sync
            def _(sync, g_=g_, o=o, bf=bf):
                if prv["woc"][bf] is not None:
                    wait(sync, prv["woc"][bf])
                else:
                    wait(sync, ck["win_evac"][NWIN - 1])
                ck[("ld_wor", g_, o)] = dma(
                    sync, f"wor{bf}",
                    wo_raw[bf][:].rearrange("p (cd e) -> p cd e", e=128),
                    wov[:, :, o * 128:(o + 1) * 128])
                if prv["opx1"][bf] is not None:
                    wait(sync, prv["opx1"][bf])
                else:
                    wait(sync, ck[("agst", NSTRIP * HPC - 2)])
                    wait(sync, ck[("agst", NSTRIP * HPC - 1)])
                ck[("ld_xr", g_, o)] = dma(
                    sync, f"xr{bf}", xr_roll[bf][:],
                    xown_i[:, o * TOK + g_ * TOKC:o * TOK + g_ * TOKC + TOKC])

            @block.vector
            def _(v, g_=g_, o=o, bf=bf):
                wait(v, ck[("ld_wor", g_, o)])
                if prv["opt"][bf] is not None:
                    wait(v, prv["opt"][bf])
                ck[("woc", g_, o)] = v_inc(v.tensor_copy(
                    out=wo_buf[bf][:], in_=wo_raw[bf][:]))
            prv["woc"][bf] = ck[("woc", g_, o)]

            @block.tensor
            def _(t, g_=g_, o=o, bf=bf):
                wait(t, ck[("woc", g_, o)])
                if o == 0:
                    wait(t, ck[("gather", g_)])
                if prv["opx1"][bf] is not None:
                    wait(t, prv["opx1"][bf])
                last = None
                for cd in range(DC):
                    last = t.matmul(p_op[bf][:],
                                    wo_buf[bf][:, cd * 128:(cd + 1) * 128],
                                    amg[:, cd * TOKC:(cd + 1) * TOKC],
                                    start=(cd == 0), stop=(cd == DC - 1))
                ck[("opt", g_, o)] = t_inc(last)
            prv["opt"][bf] = ck[("opt", g_, o)]

            @block.vector
            def _(v, g_=g_, o=o, bf=bf):
                wait(v, ck[("opt", g_, o)])
                wait(v, ck[("ld_xr", g_, o)])
                if g_ == 0 and o == 0:
                    wait(v, ck["v_mod"])
                if prv["x1sp"][bf] is not None:
                    wait(v, prv["x1sp"][bf])
                v.tensor_scalar(x1_roll[bf][:], p_op[bf][:],
                                ga_c[:, o:o + 1], gabout_c[:, o:o + 1],
                                ALU.mult, ALU.add)
                v.drain()
                ck[("opx1", g_, o)] = v_inc(v.tensor_add(
                    x1_roll[bf][:], x1_roll[bf][:], xr_roll[bf][:]))
                v.drain()
                ck[("opev", g_, o)] = v_inc(v.tensor_copy(
                    out=x1c_bf[:, o * TOKC:(o + 1) * TOKC],
                    in_=x1_roll[bf][:]))
            prv["opx1"][bf] = ck[("opx1", g_, o)]
            prv["opev"][bf] = ck[("opev", g_, o)]

            @block.sync
            def _(sync, g_=g_, o=o, bf=bf):
                wait(sync, ck[("opev", g_, o)])
                ck[("x1sp", g_, o)] = dma(
                    sync, f"x1s{bf}",
                    x1_sp.ap()[:, o * TOK + g_ * TOKC:o * TOK + g_ * TOKC + TOKC],
                    x1_roll[bf][:])
            prv["x1sp"][bf] = ck[("x1sp", g_, o)]

        if dbg and g_ == 0:
            @block.sync
            def _(sync):
                wait(sync, ck[("m0", NSTRIP * HPC - 1)])
                dma(sync, "dbgq", dbg_m0, negm0[:])
                dma(sync, "dbgq", dbg_nrm[0:1, :], nrm_q[:])
                dma(sync, "dbgq", dbg_nrm[1:2, :], nrm_k[:])
                wait(sync, ck["cc_strip"][NSTRIP - 1])
                dma(sync, "dbgq", dbg_agi, ago_in.ap())
                dma(sync, "dbgq", dbg_agab, ago_ab.ap())
                wait(sync, ck[("opt", 0, DC - 1)])
                ck["dbg2"] = dma(sync, "dbgq", dbg_amg, amg[:])

        @block.vector
        def _(v, g_=g_):
            wait(v, ck[("opev", g_, DC - 1)])
            v.drain()
            ck[("l2sq", g_)] = v_inc(v.tensor_mul(x1s_bf[:], x1c_bf[:],
                                                  x1c_bf[:]))

        @block.tensor
        def _(t, g_=g_):
            wait(t, ck[("l2sq", g_)])
            last = None
            for cd in range(DC):
                for (lo, hi) in groups(TOKC):
                    last = t.matmul(p_st2[0:1, lo:hi], ones_cb[:],
                                    x1c_bf[:, cd * TOKC + lo:cd * TOKC + hi],
                                    start=(cd == 0), stop=(cd == DC - 1),
                                    skip_group_check=True)
            for cd in range(DC):
                for (lo, hi) in groups(TOKC):
                    last = t.matmul(p_st2[0:1, SQO2 + lo:SQO2 + hi], ones_cb[:],
                                    x1s_bf[:, cd * TOKC + lo:cd * TOKC + hi],
                                    start=(cd == 0), stop=(cd == DC - 1),
                                    skip_group_check=True)
            ck[("l2mm", g_)] = t_inc(last)

        @block.vector
        def _(v, g_=g_):
            wait(v, ck[("l2mm", g_)])
            v.tensor_scalar_mul(ln2m[:], p_st2[0:1, 0:TOKC], 1.0 / D)
            v.tensor_scalar_mul(ln2v[:], p_st2[0:1, SQO2:SQO2 + TOKC], 1.0 / D)
            v.drain()
            v.tensor_mul(ln2row[0:1, 0:TOKC], ln2m[:], ln2m[:])
            v.drain()
            v.tensor_sub(ln2v[:], ln2v[:], ln2row[0:1, 0:TOKC])
            v.drain()
            ck[("l2var", g_)] = v_inc(v.tensor_scalar_add(ln2v[:], ln2v[:],
                                                          LN_EPS))

        @block.scalar
        def _(s_, g_=g_):
            wait(s_, ck[("l2var", g_)])
            s_.activation(ln2sd[:], ln2v[:], ACT.Sqrt)
            s_.drain()
            s_.activation(ln2row[0:1, TOKC:2 * TOKC], ln2sd[:], ACT.Ln)
            s_.drain()
            ck[("l2r", g_)] = s_inc(s_.activation(
                ln2r[:], ln2row[0:1, TOKC:2 * TOKC], ACT.Exp, scale=-1.0))

        @block.tensor
        def _(t, g_=g_):
            wait(t, ck[("l2r", g_)])
            if g_ == 1:
                wait(t, ck[("l2apply", 0)])
            last = None
            for (lo, hi) in groups(TOKC):
                t.matmul(p_r2m[:, lo:hi], ones_rf[:], ln2m[0:1, lo:hi],
                         start=True, stop=True)
                last = t.matmul(p_r2r[:, lo:hi], ones_rf[:], ln2r[0:1, lo:hi],
                                start=True, stop=True)
            ck[("l2rep", g_)] = t_inc(last)

        @block.vector
        def _(v, g_=g_):
            wait(v, ck[("l2rep", g_)])
            r = None
            for cd in range(DC):
                v.tensor_sub(apply_scr[:], x1c_bf[:, cd * TOKC:(cd + 1) * TOKC],
                             p_r2m[:])
                v.drain()
                v.tensor_mul(apply_scr[:], apply_scr[:], p_r2r[:])
                v.drain()
                r = v_inc(v.tensor_scalar(
                    h2[:, cd * TOK + g_ * TOKC:cd * TOK + g_ * TOKC + TOKC],
                    apply_scr[:], cb1_c[:, cd:cd + 1], sb_c[:, cd:cd + 1],
                    ALU.mult, ALU.add))
                v.drain()
            ck[("l2apply", g_)] = r

        h2_ck.append(ck[("l2apply", g_)])

    p67.close()
    pW.close()

    pG = ExitStack()
    gu = sbuf("gu", [128, HC * TOK], BF16, st=pG, side="right")
    cs2_rows = sbuf("cs2_rows", [2, TOK], BF16, st=pG, side="right")
    csscr = sbuf("csscr", [1, TOK], st=pG, side="right")
    amr_scr = sbuf("amr_scr", [128, 1], st=pG, side="right")
    up_scr = sbuf("up_scr", [128, TOK], st=pG, side="right")
    p8 = ExitStack()
    gate_b = sbuf("gate_b", [128, TOK], BF16, st=p8, side="right")
    wgi_b = [sbuf(f"wgi{i}", [128, DC * 128], I32, st=p8, side="right")
             for i in range(2)]
    wui_b = [sbuf(f"wui{i}", [128, DC * 128], I32, st=p8, side="right")
             for i in range(2)]
    wgt_b = [sbuf(f"wgt{i}", [128, DC * 128], BF16, st=p8, side="right")
             for i in range(2)]
    wut_b = [sbuf(f"wut{i}", [128, DC * 128], BF16, st=p8, side="right")
             for i in range(2)]
    cs_rows = sbuf("cs_rows", [2, TOK], BF16, st=p8, side="right")
    p_g = [psum(f"p_g{i}", [128, TOK], st=p8) for i in range(2)]
    p_u = [psum(f"p_u{i}", [128, TOK], st=p8) for i in range(2)]

    if dbg:
        @block.sync
        def _(sync):
            wait(sync, h2_ck[1])
            ck["dbg3"] = dma(sync, "dbgq", dbg_h2, h2[:])

    ck["x1s_fin"] = {0: ("x1s0", cnt.get("x1s0", 0)),
                     1: ("x1s1", cnt.get("x1s1", 0))}
    p8d = ExitStack()

    @block.tensor
    def _(t):
        wait(t, h2_ck[1])
        last = None
        for cd in range(DC):
            for (lo, hi) in groups(TOK):
                last = t.matmul(p_g[0][0:1, lo:hi], ones_cb[:],
                                h2[:, cd * TOK + lo:cd * TOK + hi],
                                start=(cd == 0), stop=(cd == DC - 1))
        ck["csh_t"] = t_inc(last)

    @block.vector
    def _(v):
        wait(v, ck["csh_t"])
        v.tensor_copy(out=cs_rows[0:1, :], in_=p_g[0][0:1, 0:TOK])
        v.drain()
        ck["csh_v"] = v_inc(v.tensor_sub(csscr[:], p_g[0][0:1, 0:TOK],
                                         cs_rows[0:1, :]))

    @block.gpsimd
    def _(g):
        wait(g, ck["csh_v"])
        ck["d_csh"] = dma(g, "csd", cs_rows[1:2, :], csscr[:])

    for hh in range(HC):
        @block.scalar
        def _(s_, hh=hh):
            if hh >= 2:
                wait(s_, ck[("wgc", hh - 2)])
            else:
                wait(s_, h2_ck[1])
                wait(s_, ck["x1s_fin"][0])
                wait(s_, ck["x1s_fin"][1])
            ck[("ldwg", hh)] = dma(
                s_, f"wg{hh % 2}", wgi_b[hh % 2][:],
                wg_i[:, hh * DC * 128:(hh + 1) * DC * 128])

        @block.sync
        def _(sync, hh=hh):
            if hh >= 2:
                wait(sync, ck[("wuc", hh - 2)])
            else:
                wait(sync, h2_ck[1])
                wait(sync, ck["x1s_fin"][0])
                wait(sync, ck["x1s_fin"][1])
            ck[("ldwu", hh)] = dma(
                sync, f"wu{hh % 2}", wui_b[hh % 2][:],
                wu_i[:, hh * DC * 128:(hh + 1) * DC * 128])

        @block.vector
        def _(v, hh=hh):
            wait(v, ck[("ldwg", hh)])
            if hh >= 2:
                wait(v, ck[("gmm", hh - 2)])
            ck[("wgc", hh)] = v_inc(v.tensor_copy(out=wgt_b[hh % 2][:],
                                                  in_=wgi_b[hh % 2][:]))
            wait(v, ck[("ldwu", hh)])
            if hh >= 2:
                wait(v, ck[("umm", hh - 2)])
            ck[("wuc", hh)] = v_inc(v.tensor_copy(out=wut_b[hh % 2][:],
                                                  in_=wui_b[hh % 2][:]))

        @block.tensor
        def _(t, hh=hh):
            wait(t, ck[("wgc", hh)])
            if hh == 0:
                wait(t, ck["d_csh"])
            if hh >= 2:
                wait(t, ck[("silu", hh - 2)])
            last = None
            for cd in range(DC):
                for (lo, hi) in groups(TOK):
                    last = t.matmul(
                        p_g[hh % 2][:, lo:hi],
                        wgt_b[hh % 2][:, cd * 128:(cd + 1) * 128],
                        h2[:, cd * TOK + lo:cd * TOK + hi],
                        start=(cd == 0), stop=False)
            for (lo, hi) in groups(TOK):
                last = t.matmul(p_g[hh % 2][:, lo:hi], neg128b[:],
                                cs_rows[:, lo:hi], start=False, stop=True)
            ck[("gmm", hh)] = t_inc(last)
            wait(t, ck[("wuc", hh)])
            if hh >= 2:
                wait(t, ck[("gu", hh - 2)])
            for cd in range(DC):
                for (lo, hi) in groups(TOK):
                    last = t.matmul(
                        p_u[hh % 2][:, lo:hi],
                        wut_b[hh % 2][:, cd * 128:(cd + 1) * 128],
                        h2[:, cd * TOK + lo:cd * TOK + hi],
                        start=(cd == 0), stop=False)
            for (lo, hi) in groups(TOK):
                last = t.matmul(p_u[hh % 2][:, lo:hi], neg128b[:],
                                cs_rows[:, lo:hi], start=False, stop=True)
            ck[("umm", hh)] = t_inc(last)

        @block.scalar
        def _(s_, hh=hh):
            wait(s_, ck[("gmm", hh)])
            if hh >= 2:
                wait(s_, ck[("gu", hh - 2)])
            ck[("silu", hh)] = s_inc(s_.activation(
                gate_b[:], p_g[hh % 2][:], ACT.Silu,
                scale=sg_c[:, hh:hh + 1]))

        @block.vector
        def _(v, hh=hh):
            wait(v, ck[("silu", hh)])
            wait(v, ck[("umm", hh)])
            v.tensor_scalar(up_scr[:], p_u[hh % 2][:], su_c[:, hh:hh + 1],
                            None, ALU.mult)
            v.drain()
            ck[("gu", hh)] = v_inc(v.tensor_mul(
                gu[:, hh * TOK:(hh + 1) * TOK], up_scr[:], gate_b[:]))

    @block.tensor
    def _(t):
        wait(t, ck[("gu", HC - 1)])
        wait(t, ck[("gu", HC - 2)])
        last = None
        for hh in range(HC):
            for (lo, hi) in groups(TOK):
                last = t.matmul(p_u[0][0:1, lo:hi], ones_cb[:],
                                gu[:, hh * TOK + lo:hh * TOK + hi],
                                start=(hh == 0), stop=(hh == HC - 1))
        ck["csg_t"] = t_inc(last)

    @block.vector
    def _(v):
        wait(v, ck["csg_t"])
        v.tensor_copy(out=cs2_rows[0:1, :], in_=p_u[0][0:1, 0:TOK])
        v.drain()
        ck["csg_v"] = v_inc(v.tensor_sub(csscr[:], p_u[0][0:1, 0:TOK],
                                         cs2_rows[0:1, :]))

    @block.gpsimd
    def _(g):
        wait(g, ck["csg_v"])
        ck["d_csg"] = dma(g, "csd", cs2_rows[1:2, :], csscr[:])

    p8.close()
    pH.close()
    wdi_b = [sbuf(f"wdi{i}", [128, HC * 128], I32, st=p8d, side="right")
             for i in range(2)]
    wdt_b = [sbuf(f"wdt{i}", [128, HC * 128], BF16, st=p8d, side="right")
             for i in range(2)]
    x1l = [sbuf(f"x1l{i}", [128, TOK], st=p8d, side="right")
           for i in range(2)]
    x2r = [sbuf(f"x2r{i}", [128, TOK], st=p8d, side="right")
           for i in range(2)]

    for o in range(DC):
        @block.sync
        def _(sync, o=o):
            if o >= 2:
                wait(sync, ck[("wdc", o - 2)])
            else:
                wait(sync, ck[("gu", HC - 1)])
                wait(sync, ck["x1s_fin"][0])
                wait(sync, ck["x1s_fin"][1])
            ck[("ldwd", o)] = dma(
                sync, f"wdl{o % 2}", wdi_b[o % 2][:],
                wd_i[:, o * HC * 128:(o + 1) * HC * 128])
            wait(sync, ck["x1s_fin"][0])
            wait(sync, ck["x1s_fin"][1])
            if o >= 2:
                wait(sync, ck[("dnev", o - 2)])
            ck[("ldx1", o)] = dma(sync, f"lx{o % 2}", x1l[o % 2][:],
                                  x1_sp.ap()[:, o * TOK:(o + 1) * TOK])

        @block.vector
        def _(v, o=o):
            wait(v, ck[("ldwd", o)])
            if o >= 2:
                wait(v, ck[("dmm", o - 2)])
            ck[("wdc", o)] = v_inc(v.tensor_copy(out=wdt_b[o % 2][:],
                                                 in_=wdi_b[o % 2][:]))

        @block.tensor
        def _(t, o=o):
            wait(t, ck[("wdc", o)])
            if o == 0:
                wait(t, ck["d_csg"])
            if o >= 2:
                wait(t, ck[("dnev", o - 2)])
            last = None
            for hh in range(HC):
                for (lo, hi) in groups(TOK):
                    last = t.matmul(
                        p_g[o % 2][:, lo:hi],
                        wdt_b[o % 2][:, hh * 128:(hh + 1) * 128],
                        gu[:, hh * TOK + lo:hh * TOK + hi],
                        start=(hh == 0), stop=False)
            for (lo, hi) in groups(TOK):
                last = t.matmul(p_g[o % 2][:, lo:hi], neg128b[:],
                                cs2_rows[:, lo:hi], start=False, stop=True)
            ck[("dmm", o)] = t_inc(last)

        @block.vector
        def _(v, o=o):
            wait(v, ck[("dmm", o)])
            wait(v, ck[("ldx1", o)])
            if o >= 2:
                wait(v, ck[("ost", o - 2)])
            v.tensor_scalar(x2r[o % 2][:], p_g[o % 2][:],
                            gbsd_c[:, o:o + 1], None, ALU.mult)
            v.drain()
            ck[("dnev", o)] = v_inc(v.tensor_add(
                x2r[o % 2][:], x2r[o % 2][:], x1l[o % 2][:]))

        @block.sync
        def _(sync, o=o):
            wait(sync, ck[("dnev", o)])
            ck[("ost", o)] = dma(sync, f"st{o % 2}",
                                 out_t[:, o * TOK:(o + 1) * TOK],
                                 x2r[o % 2][:])

    p8d.close()
    pG.close()

    @block.sync
    def _(sync):
        sync.wait_ge(sems["st0"], cnt["st0"])
        sync.wait_ge(sems["st1"], cnt["st1"])
        if dbg:
            sync.wait_ge(sems["dbgq"], cnt["dbgq"])

    es.close()
    return nc


S_FULL = 8192
HID_FULL = 5632


def _chunk_major(a):
    Dn, N = a.shape
    return np.ascontiguousarray(
        a.reshape(Dn // 128, 128, N).transpose(1, 0, 2).reshape(128, -1))


def prep_inputs(x, w_qkv, b_qkv, w_out, b_out, wq_g, s_g, wq_u, s_u,
                wq_d, s_d, cvec, wmod, S, HID):
    import ml_dtypes
    HC = HID // 128
    TOKC = S // 16
    xf = np.ascontiguousarray(x[0].T.astype(np.float32))
    xfull = _chunk_major(xf)
    woutT = _chunk_major(np.ascontiguousarray(w_out.T.astype(np.float32)))
    bout_col = np.ascontiguousarray(
        b_out.astype(np.float32).reshape(DC, 128).T)
    wg = np.ascontiguousarray(
        wq_g.reshape(HC, 128, DC, 128).transpose(3, 0, 2, 1)
        .reshape(128, -1).astype(np.int32))
    wu = np.ascontiguousarray(
        wq_u.reshape(HC, 128, DC, 128).transpose(3, 0, 2, 1)
        .reshape(128, -1).astype(np.int32))
    wd = np.ascontiguousarray(
        wq_d.reshape(DC, 128, HC, 128).transpose(3, 0, 2, 1)
        .reshape(128, -1).astype(np.int32))
    sg_col = np.ascontiguousarray(s_g.astype(np.float32).reshape(HC, 128).T)
    su_col = np.ascontiguousarray(s_u.astype(np.float32).reshape(HC, 128).T)
    sd_col = np.ascontiguousarray(s_d.astype(np.float32).reshape(DC, 128).T)
    c_col = np.ascontiguousarray(cvec[0].astype(np.float32).reshape(DC, 128).T)
    tri = (np.arange(128)[:, None] <= np.arange(128)[None, :]
           ).astype(ml_dtypes.bfloat16)
    SW = S // 8
    pr16 = np.empty((128, DC), np.float32)
    for rr in range(DC):
        pr16[:, rr] = (128 * rr + np.arange(128)) * SW

    ins = []
    for c in range(NCORES):
        Ka, Kb = c, 15 - c
        xo = np.empty((128, DC * 2 * TOKC), np.float32)
        for cd in range(DC):
            blk = np.empty((128, 2 * TOKC), np.float32)
            blk[:, :TOKC] = xfull[:, cd * S + Ka * TOKC:cd * S + (Ka + 1) * TOKC]
            blk[:, TOKC:] = xfull[:, cd * S + Kb * TOKC:cd * S + (Kb + 1) * TOKC]
            xo[:, cd * 2 * TOKC:(cd + 1) * 2 * TOKC] = blk
        wsl = np.concatenate([
            w_qkv[256 * c:256 * (c + 1)],
            w_qkv[D + 256 * c:D + 256 * (c + 1)],
            w_qkv[2 * D + 256 * c:2 * D + 256 * (c + 1)]], axis=0)
        wqkv = _chunk_major(np.ascontiguousarray(wsl.T.astype(np.float32)))
        bql = np.concatenate([
            b_qkv[256 * c:256 * (c + 1)],
            b_qkv[D + 256 * c:D + 256 * (c + 1)],
            b_qkv[2 * D + 256 * c:2 * D + 256 * (c + 1)]])[None, :]
        wmc = np.empty((D, 12, 128), np.float32)
        for j in range(12):
            vv, ii = j // 2, j % 2
            col0 = 2048 * vv + 128 * (2 * c + ii)
            wmc[:, j, :] = wmod[:, col0:col0 + 128]
        wm = np.ascontiguousarray(
            wmc.reshape(DC, 128, 12 * 128).transpose(1, 0, 2).reshape(128, -1))
        gb = np.empty((128, 2), np.float32)
        CPS = SW // TOKC
        NSTRIP = S // SW
        for gi, K in enumerate((Ka, Kb)):
            srel = (K // CPS) - (NSTRIP // 2 * gi)
            gb[:, gi] = srel * D * SW + (K % CPS) * TOKC
        ins.append(dict(
            xfull=xfull, xown=np.ascontiguousarray(xo), wm=wm,
            c_col=c_col, wqkv=wqkv,
            bqkv_row=np.ascontiguousarray(bql.astype(np.float32)),
            wout=woutT, bout_col=bout_col, wg=wg, wu=wu, wd=wd,
            sg_col=sg_col, su_col=su_col, sd_col=sd_col, tri=tri,
            pramp16=pr16, gbase=gb))
    return ins


def assemble_out(results, S):
    TOKC = S // 16
    out = np.empty((1, S, D), np.float32)
    for c in range(NCORES):
        r = results[c]["out_t"].reshape(128, DC, 2 * TOKC)
        for gi, K in enumerate((c, 15 - c)):
            blk = r[:, :, gi * TOKC:(gi + 1) * TOKC]
            out[0, K * TOKC:(K + 1) * TOKC, :] = (
                blk.transpose(2, 1, 0).reshape(TOKC, D))
    return out


def kernel(L0_attn_x, L0_attn_w_qkv, L0_attn_b_qkv, L0_attn_w_out,
           L0_attn_b_out, L0_ffn_wq_gate, L0_ffn_s_gate, L0_ffn_wq_up,
           L0_ffn_s_up, L0_ffn_wq_down, L0_ffn_s_down, L0_c, L0_wmod):
    S = int(np.asarray(L0_attn_x).shape[1])
    HID = int(np.asarray(L0_ffn_wq_gate).shape[0])
    nc = build_nc(S, HID)
    ins = prep_inputs(
        np.asarray(L0_attn_x), np.asarray(L0_attn_w_qkv),
        np.asarray(L0_attn_b_qkv), np.asarray(L0_attn_w_out),
        np.asarray(L0_attn_b_out), np.asarray(L0_ffn_wq_gate),
        np.asarray(L0_ffn_s_gate), np.asarray(L0_ffn_wq_up),
        np.asarray(L0_ffn_s_up), np.asarray(L0_ffn_wq_down),
        np.asarray(L0_ffn_s_down), np.asarray(L0_c), np.asarray(L0_wmod),
        S, HID)
    res = run_bass_kernel_spmd(nc, ins, core_ids=list(range(NCORES)))
    return assemble_out(res.results, S)


# version 11
# speedup vs baseline: 1.1032x; 1.1032x over previous
import math
import numpy as np
from contextlib import ExitStack

import concourse.bass as bass
import concourse.mybir as mybir
from concourse.bass import IndirectOffsetOnAxis
from concourse.bass_utils import run_bass_kernel_spmd

F32 = mybir.dt.float32
BF16 = mybir.dt.bfloat16
I32 = mybir.dt.int32
ALU = mybir.AluOpType
ACT = mybir.ActivationFunctionType
AXX = mybir.AxisListType.X

NCORES = 8
D = 2048
DC = D // 128
HPC = 2
HD = 128
ESL = 3 * HPC * HD
ISQ = 1.0 / math.sqrt(HD)
LN_EPS = 1e-5


def groups(total, start=0, maxn=512):
    out = []
    lo = start
    while lo < total:
        hi = min(total, (lo // maxn + 1) * maxn)
        out.append((lo, hi))
        lo = hi
    return out


def build_nc(S, HID, dbg=False):
    SW = S // 8
    NSTRIP = S // SW
    SKT = SW // 128
    NB = max(1, SW // 512)
    CPS = SW // (S // 16)
    TOKC = S // 16
    TOK = 2 * TOKC
    HC = HID // 128
    WIN = 512
    NWIN = S // WIN

    nc = bass.Bass("TRN2", target_bir_lowering=False, debug=False)
    es = ExitStack()
    cnt = {}
    sems = {}

    def getsem(name):
        if name not in sems:
            sems[name] = es.enter_context(nc.semaphore(name))
        return sems[name]

    for sn in ("TS", "VS", "SS", "CC"):
        getsem(sn)

    def bump(name, by):
        cnt[name] = cnt.get(name, 0) + by
        return cnt[name]

    def wait(eng, ckv):
        eng.wait_ge(sems[ckv[0]], ckv[1])

    def sbuf(name, shape, dt=F32, st=None, side=None):
        return (st or es).enter_context(
            nc.sbuf_tensor(name, shape, dt, side=side))

    def psum(name, shape, st=None):
        return (st or es).enter_context(nc.psum_tensor(name, shape, F32))

    DI = lambda n, s, d: nc.dram_tensor(n, s, d, kind="ExternalInput").ap()
    xfull = DI("xfull", [128, DC * S], F32)
    xown_i = DI("xown", [128, DC * TOK], F32)
    wm_i = DI("wm", [128, DC * 12 * 128], F32)
    c_i = DI("c_col", [128, DC], F32)
    wqkv_i = DI("wqkv", [128, DC * ESL], F32)
    bqkv_i = DI("bqkv_row", [1, ESL], F32)
    wout_i = DI("wout", [128, DC * D], F32)
    bout_i = DI("bout_col", [128, DC], F32)
    wg_i = DI("wg", [128, HC * DC * 128], I32)
    wu_i = DI("wu", [128, HC * DC * 128], I32)
    wd_i = DI("wd", [128, DC * HC * 128], I32)
    sg_i = DI("sg_col", [128, HC], F32)
    su_i = DI("su_col", [128, HC], F32)
    sd_i = DI("sd_col", [128, DC], F32)
    tri_i = DI("tri", [128, 128], BF16)
    pr16_i = DI("pramp16", [128, DC], F32)
    gbase_i = DI("gbase", [128, 2], F32)
    out_t = nc.dram_tensor("out_t", [128, DC * TOK], F32,
                           kind="ExternalOutput").ap()
    if dbg:
        dbg_mod = nc.dram_tensor("dbg_mod", [128, NCORES * 12], F32,
                                 kind="ExternalOutput").ap()
        dbg_st = nc.dram_tensor("dbg_st", [NCORES, 2 * TOK], F32,
                                kind="ExternalOutput").ap()
        dbg_rows = nc.dram_tensor("dbg_rows", [34, S], BF16,
                                  kind="ExternalOutput").ap()
        dbg_corr = nc.dram_tensor("dbg_corr", [34, ESL], BF16,
                                  kind="ExternalOutput").ap()
        dbg_qf = nc.dram_tensor("dbg_qf", [128, HPC * S], BF16,
                                kind="ExternalOutput").ap()
        dbg_kf = nc.dram_tensor("dbg_kf", [128, HPC * S], BF16,
                                kind="ExternalOutput").ap()
        dbg_vt = nc.dram_tensor("dbg_vt", [128, HPC * S], BF16,
                                kind="ExternalOutput").ap()
        dbg_amg = nc.dram_tensor("dbg_amg", [128, DC * TOKC], BF16,
                                 kind="ExternalOutput").ap()
        dbg_h2 = nc.dram_tensor("dbg_h2", [128, DC * TOK], BF16,
                                kind="ExternalOutput").ap()
        dbg_agi = nc.dram_tensor("dbg_agi", [NSTRIP * HPC * HD, SW], BF16,
                                 kind="ExternalOutput").ap()
        dbg_m0 = nc.dram_tensor("dbg_m0", [128, HPC * 4], F32,
                                kind="ExternalOutput").ap()
        dbg_nrm = nc.dram_tensor("dbg_nrm", [2, HPC * NWIN], F32,
                                 kind="ExternalOutput").ap()
        dbg_pacc = nc.dram_tensor("dbg_pacc", [128, SW], F32,
                                  kind="ExternalOutput").ap()
        dbg_rinv = nc.dram_tensor("dbg_rinv", [1, SW], BF16,
                                  kind="ExternalOutput").ap()
        dbg_agab = nc.dram_tensor("dbg_agab", [NSTRIP // 2 * D, SW], BF16,
                                  kind="ExternalOutput").ap()

    agma_i = nc.dram_tensor("agma_i", [128, 4], F32)
    agma_o = nc.dram_tensor("agma_o", [NCORES * 128, 4], F32)
    agmb_i = nc.dram_tensor("agmb_i", [128, 8], F32)
    agmb_o = nc.dram_tensor("agmb_o", [NCORES * 128, 8], F32)
    agst_i = nc.dram_tensor("agst_i", [1, 2 * TOK], F32)
    agst_o = nc.dram_tensor("agst_o", [NCORES, 2 * TOK], F32)
    ago_in = nc.dram_tensor("ago_in", [NSTRIP * HPC * HD, SW], BF16)
    ago_ab = nc.dram_tensor("ago_ab", [NSTRIP // 2 * D, SW], BF16,
                            addr_space="Shared")
    ago_cd = nc.dram_tensor("ago_cd", [NSTRIP // 2 * D, SW], BF16,
                            addr_space="Shared")
    ago_flat_a = ago_ab.ap().rearrange("a b -> (a b)")[:, None]
    ago_flat_b = ago_cd.ap().rearrange("a b -> (a b)")[:, None]
    x1_sp = nc.dram_tensor("x1_sp", [128, DC * TOK], F32)

    ones_cf = sbuf("ones_cf", [128, 1], F32)
    ones_cb = sbuf("ones_cb", [128, 1], BF16)
    ones2b = sbuf("ones2b", [2, 128], BF16)
    neg128b = sbuf("neg128b", [2, 128], BF16)
    ones_rf = sbuf("ones_rf", [1, 128], F32)
    ones_rb = sbuf("ones_rb", [1, 128], BF16)
    tri01 = sbuf("tri01", [128, 128], BF16)
    pr16 = sbuf("pr16", [128, DC])
    gbase = sbuf("gbase_s", [128, 2])
    c_sb = sbuf("c_sb", [128, DC])
    csilu = sbuf("csilu", [128, DC])
    sa_c = sbuf("sa_c", [128, DC])
    sab_c = sbuf("sab_c", [128, DC], BF16)
    ca1_c = sbuf("ca1_c", [128, DC])
    ga_c = sbuf("ga_c", [128, DC])
    sb_c = sbuf("sb_c", [128, DC])
    cb1_c = sbuf("cb1_c", [128, DC])
    gb_c = sbuf("gb_c", [128, DC])
    gabout_c = sbuf("gabout_c", [128, DC])
    gbsd_c = sbuf("gbsd_c", [128, DC])
    sd_c = sbuf("sd_c", [128, DC])
    bout_c = sbuf("bout_c", [128, DC])
    sg_c = sbuf("sg_c", [128, HC])
    su_c = sbuf("su_c", [128, HC])
    negm0 = sbuf("negm0", [128, HPC * NSTRIP])
    m0s = sbuf("m0s", [1, 4])
    idx16f = sbuf("idx16f", [128, DC])
    idx16 = [sbuf(f"idx16_{g}", [128, DC], I32) for g in range(2)]

    block = es.enter_context(nc.Block())
    ck = {}

    def dma(eng, stream, out, in_, **kw):
        sem = getsem(stream)
        eng.dma_start(out=out, in_=in_, **kw).then_inc(sem, 16)
        return (stream, bump(stream, 16))

    def t_inc(inst):
        inst.then_inc(sems["TS"], 1)
        return ("TS", bump("TS", 1))

    def v_inc(inst):
        inst.then_inc(sems["VS"], 1)
        return ("VS", bump("VS", 1))

    def s_inc(inst):
        inst.then_inc(sems["SS"], 1)
        return ("SS", bump("SS", 1))

    def c_inc(inst):
        inst.then_inc(sems["CC"], 1)
        return ("CC", bump("CC", 1))

    RG = [list(range(NCORES))]

    p23 = ExitStack()
    wq_sb = sbuf("wq_sb", [128, DC * ESL], BF16, st=p23, side="right")
    corr4 = sbuf("corr4", [34, ESL], BF16, st=p23, side="right")
    corr_l = corr4[:][32:34, :]
    rows4 = sbuf("rows4", [34, S], BF16, st=p23, side="right")
    xw = [sbuf(f"xw{i}", [128, DC * WIN], BF16, st=p23, side="right")
          for i in range(2)]
    v_win = [sbuf(f"vwin{i}", [128, HPC * WIN], BF16, st=p23, side="right")
             for i in range(2)]
    sq_scr = sbuf("sq_scr", [128, WIN], BF16, st=p23, side="right")
    rep_sb = sbuf("rep_sb", [128, WIN], st=p23, side="right")
    rrows = rows4[:][0:2, :]
    crows = rows4[:][32:34, :]

    p01 = ExitStack()
    brow_f = sbuf("brow_f", [1, ESL], st=p01, side="right")
    wm_t = [sbuf(f"wm_t{i}", [128, 12 * 128], st=p01, side="right")
            for i in range(2)]
    xo_b = [sbuf(f"xo_b{i}", [128, TOK], BF16, st=p01, side="right")
            for i in range(2)]
    xo_s = [sbuf(f"xo_s{i}", [128, TOK], BF16, st=p01, side="right")
            for i in range(2)]
    stat8 = sbuf("stat8", [NCORES, 2 * TOK], st=p01, side="right")
    r8 = sbuf("r8", [NCORES, TOK], st=p01, side="right")
    m8n = sbuf("m8n", [NCORES, TOK], st=p01, side="right")
    lnscr8 = sbuf("lnscr8", [NCORES, TOK], st=p01, side="right")
    r8hi = sbuf("r8hi", [NCORES, TOK], BF16, st=p01, side="right")
    r8lo = sbuf("r8lo", [NCORES, TOK], BF16, st=p01, side="right")
    st_m = sbuf("st_m", [1, TOK], st=p01, side="right")
    st_v = sbuf("st_v", [1, TOK], st=p01, side="right")
    st_sd = sbuf("st_sd", [1, TOK], st=p01, side="right")
    modc = sbuf("modc", [128, NCORES * 12], st=p01, side="right")
    modacc = sbuf("modacc", [128, 12], st=p01, side="right")
    g_scr = sbuf("g_scr", [128, DC], st=p01, side="right")
    p_mod = psum("p_mod", [128, 12], st=p01)
    SQO = max(TOK, 512)
    p_st = psum("p_st", [1, SQO + TOK], st=p01)
    p_gemv = psum("p_gemv", [64, ESL], st=p01)

    @block.sync
    def _(sync):
        dma(sync, "cst", c_sb[:], c_i)
        dma(sync, "cst", tri01[:], tri_i)
        dma(sync, "cst", pr16[:], pr16_i)
        dma(sync, "cst", gbase[:], gbase_i)
        dma(sync, "cst", sg_c[:], sg_i)
        dma(sync, "cst", su_c[:], su_i)
        dma(sync, "cst", sd_c[:], sd_i)
        dma(sync, "cst", bout_c[:], bout_i)
        ck["ld_consts"] = dma(sync, "cst", brow_f[0:1, :], bqkv_i)

    @block.vector
    def _(v):
        v.memset(ones_cf[:], 1.0)
        v.memset(ones_cb[:], 1.0)
        v.memset(ones2b[:], 1.0)
        v.memset(neg128b[:], -128.0)
        v.memset(ones_rf[:], 1.0)
        ck["v_consts"] = v_inc(v.memset(ones_rb[:], 1.0))

    @block.scalar
    def _(s):
        wait(s, ck["ld_consts"])
        ck["s_csilu"] = s_inc(s.activation(csilu[:], c_sb[:], ACT.Silu))

    ck["xo_ld"] = {}
    ck["xo_sq"] = {}
    ck["xo_mm"] = {}
    for cd in range(DC):
        @block.gpsimd
        def _(g, cd=cd):
            if cd >= 2:
                wait(g, ck["xo_mm"][cd - 2])
            ck["xo_ld"][cd] = dma(g, f"xo{cd % 2}", xo_b[cd % 2][:],
                                  xown_i[:, cd * TOK:(cd + 1) * TOK])

        @block.vector
        def _(v, cd=cd):
            wait(v, ck["xo_ld"][cd])
            ck["xo_sq"][cd] = v_inc(v.tensor_mul(
                xo_s[cd % 2][:], xo_b[cd % 2][:], xo_b[cd % 2][:]))

        @block.tensor
        def _(t, cd=cd):
            wait(t, ck["xo_ld"][cd])
            if cd == 0:
                wait(t, ck["v_consts"])
            last = None
            for (lo, hi) in groups(TOK):
                last = t.matmul(p_st[0:1, lo:hi], ones_cb[:],
                                xo_b[cd % 2][:, lo:hi],
                                start=(cd == 0), stop=(cd == DC - 1),
                                skip_group_check=True)
            wait(t, ck["xo_sq"][cd])
            for (lo, hi) in groups(TOK):
                last = t.matmul(p_st[0:1, SQO + lo:SQO + hi], ones_cb[:],
                                xo_s[cd % 2][:, lo:hi],
                                start=(cd == 0), stop=(cd == DC - 1),
                                skip_group_check=True)
            ck["xo_mm"][cd] = t_inc(last)
    ck["mm_sq"] = ck["xo_mm"][DC - 1]

    ck["ld_wm"] = {}


    ck["mm_mod"] = []
    ck["v_macc"] = {}

    @block.vector
    def _(v):
        ck["v_macc0"] = v_inc(v.memset(modacc[:], 0.0))

    for cd in range(DC):
        @block.sync
        def _(sync, cd=cd):
            if cd >= 2:
                wait(sync, ck["mm_mod"][cd - 2])
            ck["ld_wm"][cd] = dma(sync, f"wm{cd % 2}", wm_t[cd % 2][:],
                                  wm_i[:, cd * 1536:(cd + 1) * 1536])

        @block.tensor
        def _(t, cd=cd):
            if cd == 0:
                wait(t, ck["s_csilu"])
            wait(t, ck["ld_wm"][cd])
            if cd >= 1:
                wait(t, ck["v_macc"][cd - 1])
            last = None
            for j in range(12):
                last = t.matmul(p_mod[:, j:j + 1],
                                wm_t[cd % 2][:, j * 128:(j + 1) * 128],
                                csilu[:, cd:cd + 1],
                                start=True, stop=True,
                                skip_group_check=True)
            ck["mm_mod"].append(t_inc(last))

        @block.vector
        def _(v, cd=cd):
            wait(v, ck["mm_mod"][cd])
            if cd == 0:
                wait(v, ck["v_macc0"])
            ck["v_macc"][cd] = v_inc(v.tensor_add(modacc[:], modacc[:],
                                                   p_mod[:]))
            v.drain()

    @block.vector
    def _(v):
        wait(v, ck["v_macc"][DC - 1])
        ck["v_moda"] = v_inc(v.tensor_copy(out=modc[:, 0:4],
                                           in_=modacc[:, 0:4]))
        ck["v_modb"] = v_inc(v.tensor_copy(out=modc[:, 4:12],
                                           in_=modacc[:, 4:12]))

    @block.sync
    def _(sync):
        wait(sync, ck["v_moda"])
        ck["st_moda"] = dma(sync, "stma", agma_i.ap(), modc[:, 0:4])
        wait(sync, ck["v_modb"])
        ck["st_modb"] = dma(sync, "stmb", agmb_i.ap(), modc[:, 4:12])

    @block.vector
    def _(v):
        wait(v, ck["mm_sq"])
        v.tensor_scalar_mul(st_m[:], p_st[0:1, 0:TOK], 1.0 / D)
        v.tensor_scalar_mul(st_v[:], p_st[0:1, SQO:SQO + TOK], 1.0 / D)
        v.drain()
        v.tensor_mul(lnscr8[0:1, :], st_m[:], st_m[:])
        v.drain()
        v.tensor_sub(st_v[:], st_v[:], lnscr8[0:1, :])
        v.drain()
        ck["v_var"] = v_inc(v.tensor_scalar_add(st_v[:], st_v[:], LN_EPS))

    @block.scalar
    def _(s):
        wait(s, ck["v_var"])
        ck["s_sd"] = s_inc(s.activation(st_sd[:], st_v[:], ACT.Sqrt))

    @block.sync
    def _(sync):
        wait(sync, ck["v_var"])
        dma(sync, "stst", agst_i.ap()[0:1, 0:TOK], st_m[:])
        wait(sync, ck["s_sd"])
        ck["st_stats"] = dma(sync, "stst", agst_i.ap()[0:1, TOK:2 * TOK],
                             st_sd[:])

    def emit_xwin(g, w):
        r = None
        for cd in range(DC):
            r = dma(g, f"xw{w % 2}", xw[w % 2][:, cd * WIN:(cd + 1) * WIN],
                    xfull[:, cd * S + w * WIN:cd * S + (w + 1) * WIN])
        return r

    ck["xwin"] = {}

    @block.gpsimd
    def _(g):
        ck["xwin"][0] = emit_xwin(g, 0)
        ck["ld_wq"] = []
        for cd in range(DC):
            ck["ld_wq"].append(
                dma(g, "wqc", wq_sb[:, cd * ESL:(cd + 1) * ESL],
                    wqkv_i[:, cd * ESL:(cd + 1) * ESL]))
        ck["xwin"][1] = emit_xwin(g, 1)
        wait(g, ck["st_moda"])
        ck["cc_moda"] = c_inc(g.collective_compute(
            "AllGather", ALU.bypass, replica_groups=RG,
            ins=[agma_i.ap().opt()], outs=[agma_o.ap().opt()]))
        wait(g, ck["st_modb"])
        ck["cc_modb"] = c_inc(g.collective_compute(
            "AllGather", ALU.bypass, replica_groups=RG,
            ins=[agmb_i.ap().opt()], outs=[agmb_o.ap().opt()]))
        wait(g, ck["st_stats"])
        ck["cc_stats"] = c_inc(g.collective_compute(
            "AllGather", ALU.bypass, replica_groups=RG,
            ins=[agst_i.ap().opt()], outs=[agst_o.ap().opt()]))

    @block.sync
    def _(sync):
        wait(sync, ck["cc_moda"])
        for b in range(NCORES):
            ck["ld_mca"] = dma(sync, "mbk", modc[:, b * 12:b * 12 + 4],
                               agma_o.ap()[b * 128:(b + 1) * 128, :])
        wait(sync, ck["cc_modb"])
        for b in range(NCORES):
            ck["ld_mcb"] = dma(sync, "mbk", modc[:, b * 12 + 4:b * 12 + 12],
                               agmb_o.ap()[b * 128:(b + 1) * 128, :])
        wait(sync, ck["cc_stats"])
        ck["ld_rows"] = dma(sync, "mbk", stat8[:], agst_o.ap())

    @block.vector
    def _(v):
        wait(v, ck["ld_rows"])
        mv = modc[:].rearrange("p (b j) -> p b j", j=12)
        v.tensor_copy(out=sa_c[:].rearrange("p (b j) -> p b j", j=2),
                      in_=mv[:, :, 0:2])
        v.tensor_copy(out=ca1_c[:].rearrange("p (b j) -> p b j", j=2),
                      in_=mv[:, :, 2:4])
        v.drain()
        v.tensor_copy(out=sab_c[:], in_=sa_c[:])
        ck["v_saca"] = v_inc(v.tensor_scalar_add(ca1_c[:], ca1_c[:], 1.0))
        v.tensor_copy(out=ga_c[:].rearrange("p (b j) -> p b j", j=2),
                      in_=mv[:, :, 4:6])
        v.tensor_copy(out=sb_c[:].rearrange("p (b j) -> p b j", j=2),
                      in_=mv[:, :, 6:8])
        v.tensor_copy(out=cb1_c[:].rearrange("p (b j) -> p b j", j=2),
                      in_=mv[:, :, 8:10])
        v.tensor_copy(out=gb_c[:].rearrange("p (b j) -> p b j", j=2),
                      in_=mv[:, :, 10:12])
        v.drain()
        v.tensor_scalar_add(cb1_c[:], cb1_c[:], 1.0)
        v.drain()
        for gt in (ga_c, gb_c):
            v.tensor_scalar(g_scr[:], gt[:], 0.0, 2.0, ALU.is_ge, ALU.mult)
            v.drain()
            v.tensor_scalar_add(g_scr[:], g_scr[:], -1.0)
            v.drain()
            v.tensor_mul(gt[:], gt[:], g_scr[:])
            v.drain()
            v.tensor_scalar_max(gt[:], gt[:], 0.25)
            v.drain()
            v.tensor_mul(gt[:], gt[:], g_scr[:])
            v.drain()
        v.tensor_mul(gabout_c[:], ga_c[:], bout_c[:])
        ck["v_mod"] = v_inc(v.tensor_mul(gbsd_c[:], gb_c[:], sd_c[:]))
        v.tensor_scalar(idx16f[:], pr16[:], gbase[:, 0:1], None, ALU.add)
        v.drain()
        v.tensor_copy(out=idx16[0][:], in_=idx16f[:])
        v.drain()
        v.tensor_scalar(idx16f[:], pr16[:], gbase[:, 1:2], None, ALU.add)
        v.drain()
        ck["v_idx"] = v_inc(v.tensor_copy(out=idx16[1][:], in_=idx16f[:]))

    @block.scalar
    def _(s):
        wait(s, ck["ld_rows"])
        s.activation(lnscr8[:], stat8[:, TOK:2 * TOK], ACT.Ln)
        s.drain()
        ck["s_rall"] = s_inc(s.activation(r8[:], lnscr8[:], ACT.Exp,
                                          scale=-1.0))

    @block.vector
    def _(v):
        wait(v, ck["s_rall"])
        v.tensor_copy(out=r8hi[:], in_=r8[:])
        v.drain()
        v.tensor_sub(lnscr8[:], r8[:], r8hi[:])
        v.drain()
        v.tensor_copy(out=r8lo[:], in_=lnscr8[:])
        ck["v_rows"] = v_inc(v.tensor_scalar_mul(m8n[:], stat8[:, 0:TOK],
                                                 -1.0))

    @block.sync
    def _(sync):
        wait(sync, ck["v_rows"])
        for K in range(16):
            sK = K if K <= 7 else 15 - K
            off = 0 if K <= 7 else TOKC
            cs = slice(K * TOKC, (K + 1) * TOKC)
            dma(sync, "rw2", rows4[0:1, cs], r8hi[sK:sK + 1, off:off + TOKC])
            dma(sync, "rw2", rows4[1:2, cs], r8lo[sK:sK + 1, off:off + TOKC])
        ck["d_rw2"] = ("rw2", cnt["rw2"])

    @block.gpsimd
    def _(g):
        wait(g, ck["v_rows"])
        for K in range(16):
            sK = K if K <= 7 else 15 - K
            off = 0 if K <= 7 else TOKC
            cs = slice(K * TOKC, (K + 1) * TOKC)
            dma(g, "rwd", rows4[32:33, cs], m8n[sK:sK + 1, off:off + TOKC])
            ck["d_sd"] = dma(g, "rwd", rows4[33:34, cs],
                             stat8[sK:sK + 1, TOK + off:TOK + off + TOKC])

    ck["wp_gemv"] = []
    ck["wp_scale"] = []
    ck["wp_sum"] = []
    for cd in range(DC):
        @block.tensor
        def _(t, cd=cd):
            if cd == 0:
                wait(t, ck["ld_wq"][DC - 1])
                wait(t, ck["v_saca"])
            last = None
            for (lo, hi) in groups(ESL):
                last = t.matmul(p_gemv[0:1, lo:hi], sab_c[:, cd:cd + 1],
                                wq_sb[:, cd * ESL + lo:cd * ESL + hi],
                                start=(cd == 0), stop=(cd == DC - 1))
            ck["wp_gemv"].append(t_inc(last))

        @block.vector
        def _(v, cd=cd):
            wait(v, ck["wp_gemv"][cd])
            ck["wp_scale"].append(v_inc(v.tensor_scalar(
                wq_sb[:, cd * ESL:(cd + 1) * ESL],
                wq_sb[:, cd * ESL:(cd + 1) * ESL],
                ca1_c[:, cd:cd + 1], None, ALU.mult)))

        @block.tensor
        def _(t, cd=cd):
            wait(t, ck["wp_scale"][cd])
            last = None
            for (lo, hi) in groups(ESL):
                last = t.matmul(p_gemv[32:33, lo:hi], ones_cb[:],
                                wq_sb[:, cd * ESL + lo:cd * ESL + hi],
                                start=(cd == 0), stop=(cd == DC - 1))
            ck["wp_sum"].append(t_inc(last))

    @block.vector
    def _(v):
        wait(v, ck["wp_sum"][-1])
        v.tensor_add(brow_f[0:1, :], brow_f[0:1, :], p_gemv[0:1, :])
        v.drain()
        ck["v_corr"] = v_inc(v.tensor_copy(out=corr4[32:33, :],
                                           in_=p_gemv[32:33, :]))

    @block.gpsimd
    def _(g):
        wait(g, ck["v_corr"])
        ck["d_corr"] = dma(g, "rwd", corr4[33:34, :], brow_f[0:1, :])

    p01.close()

    p34 = ExitStack()
    q_fm = sbuf("q_fm", [128, HPC * S], BF16, st=p34)
    k_fm = sbuf("k_fm", [128, HPC * S], BF16, st=p34)
    v_tok = sbuf("v_tok", [128, HPC * S], BF16, st=p34)
    nrm_q = sbuf("nrm_q", [1, HPC * NWIN], st=p34)
    nrm_k = sbuf("nrm_k", [1, HPC * NWIN], st=p34)
    p3p = ExitStack()
    p_qkv = [psum(f"p_qkv{i}", [128, WIN], st=p3p) for i in range(2)]
    p_rep = psum("p_rep", [128, WIN], st=p3p)
    p_nrm = psum("p_nrm", [1, WIN], st=p3p)

    def qkv_dst(o, w):
        hl = o % 2
        if o < 2:
            return q_fm[:, hl * S + w * WIN:hl * S + (w + 1) * WIN]
        if o < 4:
            return k_fm[:, hl * S + w * WIN:hl * S + (w + 1) * WIN]
        return v_win[w % 2][:, hl * WIN:(hl + 1) * WIN]

    ck["win_done"] = {}
    ck["win_evac"] = {}
    ck["rep_cp"] = {}
    ck["win_tr"] = {}
    last_evac_buf = {0: None, 1: None}
    last_vwin_tr = {0: None, 1: None}
    last_nrm_v = None

    for w in range(NWIN):
        if w >= 2:
            @block.gpsimd
            def _(g, w=w):
                wait(g, ck["win_done"][w - 2])
                ck["xwin"][w] = emit_xwin(g, w)

        @block.tensor
        def _(t, w=w):
            wait(t, ck["xwin"][w])
            if w == 0:
                wait(t, ck["v_corr"])
                wait(t, ck["v_rows"])
                wait(t, ck["d_corr"])
                wait(t, ck["d_rw2"])
            if w >= 1:
                wait(t, ck["rep_cp"][w - 1])
            ck[("rep_mm", w)] = t_inc(t.matmul(
                p_rep[:], ones2b[:],
                rrows[:, w * WIN:(w + 1) * WIN], start=True, stop=True))

        @block.vector
        def _(v, w=w):
            wait(v, ck[("rep_mm", w)])
            ck["rep_cp"][w] = v_inc(v.tensor_copy(out=rep_sb[:], in_=p_rep[:]))

        for o in range(6):
            buf = o % 2

            @block.tensor
            def _(t, w=w, o=o, buf=buf):
                if last_evac_buf[buf] is not None:
                    wait(t, last_evac_buf[buf])
                for cd in range(DC):
                    t.matmul(p_qkv[buf][:],
                             wq_sb[:, cd * ESL + o * 128:cd * ESL + (o + 1) * 128],
                             xw[w % 2][:, cd * WIN:(cd + 1) * WIN],
                             start=(cd == 0), stop=False)
                last = t.matmul(p_qkv[buf][:],
                                corr_l[:, o * 128:(o + 1) * 128],
                                crows[:, w * WIN:(w + 1) * WIN],
                                start=False, stop=True)
                ck[("qkv_t", w, o)] = t_inc(last)

            @block.vector
            def _(v, w=w, o=o, buf=buf):
                wait(v, ck[("qkv_t", w, o)])
                if o == 0:
                    wait(v, ck["rep_cp"][w])
                if o >= 4 and last_vwin_tr[w % 2] is not None:
                    wait(v, last_vwin_tr[w % 2])
                last_evac_buf[buf] = v_inc(v.tensor_mul(
                    qkv_dst(o, w), p_qkv[buf][:], rep_sb[:]))

        ck["win_done"][w] = ("TS", cnt["TS"])
        ck["win_evac"][w] = ("VS", cnt["VS"])

        for i, (tens, nrm) in enumerate(((q_fm, nrm_q), (k_fm, nrm_k))):
            for hl in range(HPC):
                @block.vector
                def _(v, w=w, i=i, hl=hl, tens=tens):
                    wait(v, ck["win_evac"][w])
                    ck[("nrm_sq", w, i, hl)] = v_inc(v.tensor_mul(
                        sq_scr[:],
                        tens[:, hl * S + w * WIN:hl * S + (w + 1) * WIN],
                        tens[:, hl * S + w * WIN:hl * S + (w + 1) * WIN]))

                @block.tensor
                def _(t, w=w, i=i, hl=hl):
                    wait(t, ck[("nrm_sq", w, i, hl)])
                    if last_nrm_v is not None:
                        wait(t, last_nrm_v)
                    ck[("nrm_t", w, i, hl)] = t_inc(t.matmul(
                        p_nrm[:], ones_cb[:], sq_scr[:],
                        start=True, stop=True))

                @block.vector
                def _(v, w=w, i=i, hl=hl, nrm=nrm):
                    wait(v, ck[("nrm_t", w, i, hl)])
                    ck[("nrm_v", w, i, hl)] = v_inc(v.reduce_max(
                        nrm[0:1, hl * NWIN + w:hl * NWIN + w + 1],
                        p_nrm[:], AXX))
                last_nrm_v = ck[("nrm_v", w, i, hl)]

        @block.sync
        def _(sync, w=w):
            wait(sync, ck["win_evac"][w])
            r = None
            for hl in range(HPC):
                for j in range(WIN // 128):
                    tok0 = w * WIN + j * 128
                    r = dma(sync, f"vt{w % 2}",
                            v_tok[:, hl * S + tok0:hl * S + tok0 + 128],
                            v_win[w % 2][:, hl * WIN + j * 128:hl * WIN + (j + 1) * 128],
                            transpose=True)
            last_vwin_tr[w % 2] = r
            ck["win_tr"][w] = r

    ck["vt_all"] = [("vt0", cnt.get("vt0", 0)), ("vt1", cnt.get("vt1", 0))]

    if dbg:
        @block.sync
        def _(sync):
            wait(sync, ck["d_corr"])
            wait(sync, ck["d_rw2"])
            dma(sync, "dbgq", dbg_mod, modc[:])
            dma(sync, "dbgq", dbg_st, stat8[:])
            dma(sync, "dbgq", dbg_rows, rows4[:])
            dma(sync, "dbgq", dbg_corr, corr4[:])
            wait(sync, ck["win_evac"][NWIN - 1])
            wait(sync, ck["win_tr"][NWIN - 1])
            dma(sync, "dbgq", dbg_qf, q_fm[:])
            dma(sync, "dbgq", dbg_kf, k_fm[:])
            ck["dbg1"] = dma(sync, "dbgq", dbg_vt, v_tok[:])

    p23.close()
    p3p.close()

    pW = ExitStack()
    amg = sbuf("amg", [128, DC * TOKC], BF16, st=pW, side="right")
    wo_raw = [sbuf(f"wo_raw{i}", [128, DC * 128], F32, st=pW, side="right")
              for i in range(2)]
    wo_buf = [sbuf(f"wo_buf{i}", [128, DC * 128], BF16, st=pW, side="right")
              for i in range(2)]

    p4s = ExitStack()
    ptile = [sbuf(f"pt{i}", [128, SW], BF16, st=p4s, side="right")
             for i in range(3)]
    pacc = sbuf("pacc", [128, SW], st=p4s, side="right")
    sum_row = sbuf("sum_row", [1, SW], st=p4s, side="right")
    lnrow = sbuf("lnrow", [1, SW], st=p4s, side="right")
    rinv_row = sbuf("rinv_row", [1, SW], BF16, st=p4s, side="right")
    astrip = [sbuf(f"astrip{i}", [128, SW], BF16, st=p4s, side="right")
              for i in range(2)]
    nrep_sb = sbuf("nrep_sb", [128, SW], st=p4s, side="right")
    p_sc = [psum(f"p_sc{i}", [128, SW], st=p4s) for i in range(3)]
    p_out = psum("p_out", [128, SW], st=p4s)

    @block.vector
    def _(v):
        ck["v_pacc0"] = v_inc(v.memset(pacc[:], 0.0))

    last_strip_evac = None
    last_pacc_reset = ck["v_pacc0"]
    ck["cc_strip"] = {}

    for s in range(NSTRIP):
        for hl in range(HPC):
            sidx = s * HPC + hl
            nkt = (s + 1) * SKT
            wq0 = (s * SW) // WIN
            wq1 = ((s + 1) * SW + WIN - 1) // WIN

            @block.vector
            def _(v, hl=hl, wq0=wq0, wq1=wq1):
                v.reduce_max(m0s[0:1, 0:1],
                             nrm_q[0:1, hl * NWIN + wq0:hl * NWIN + wq1], AXX)
                v.reduce_max(m0s[0:1, 1:2],
                             nrm_k[0:1, hl * NWIN:hl * NWIN + wq1], AXX)
                v.drain()
                ck["m0a"] = v_inc(v.tensor_mul(m0s[0:1, 2:3], m0s[0:1, 0:1],
                                               m0s[0:1, 1:2]))

            @block.scalar
            def _(s_):
                wait(s_, ck["m0a"])
                s_.activation(m0s[0:1, 3:4], m0s[0:1, 2:3], ACT.Ln)
                s_.drain()
                ck["m0b"] = s_inc(s_.activation(m0s[0:1, 2:3], m0s[0:1, 3:4],
                                                ACT.Exp, scale=0.5))

            @block.vector
            def _(v):
                wait(v, ck["m0b"])
                ck["m0c"] = v_inc(v.tensor_scalar(
                    m0s[0:1, 0:1], m0s[0:1, 2:3], -ISQ, -1.0,
                    ALU.mult, ALU.add))

            @block.tensor
            def _(t):
                wait(t, ck["m0c"])
                if last_strip_evac is not None:
                    wait(t, last_strip_evac)
                ck["m0t"] = t_inc(t.matmul(p_sc[0][:, 0:1], ones_rf[:],
                                           m0s[0:1, 0:1],
                                           start=True, stop=True))

            @block.vector
            def _(v, sidx=sidx):
                wait(v, ck["m0t"])
                ck[("m0", sidx)] = v_inc(v.tensor_copy(
                    out=negm0[:, sidx:sidx + 1], in_=p_sc[0][:, 0:1]))

            exp_ck = {}
            mask_ck = {}
            pacc_ck = {}
            attnv_ck = {}

            def emit_attnv(kt):
                qlo = max(0, 128 * kt - s * SW)
                diag = 128 * kt >= s * SW
                buf = kt % 3

                @block.tensor
                def _(t, s=s, hl=hl, kt=kt, qlo=qlo, buf=buf, nkt=nkt,
                      diag=diag):
                    if diag:
                        wait(t, mask_ck[kt])
                    else:
                        wait(t, exp_ck[kt])
                    last = None
                    for b in range(qlo // 512, NB):
                        lo = max(qlo, 512 * b)
                        hi = min(SW, 512 * (b + 1))
                        ktl = min(nkt - 1, s * SKT + 4 * (b + 1) - 1)
                        last = t.matmul(
                            p_out[:, lo:hi],
                            v_tok[:, hl * S + kt * 128:hl * S + (kt + 1) * 128],
                            ptile[buf][:, lo:hi],
                            start=(kt == 0), stop=(kt == ktl))
                    attnv_ck[kt] = t_inc(last)

            for kt in range(nkt):
                qlo = max(0, 128 * kt - s * SW)
                diag = 128 * kt >= s * SW
                buf = kt % 3

                @block.tensor
                def _(t, s=s, hl=hl, kt=kt, qlo=qlo, buf=buf, sidx=sidx,
                      wq1=wq1):
                    if kt == 0:
                        wait(t, ck[("m0", sidx)])
                        wait(t, ck["vt_all"][0])
                        wait(t, ck["vt_all"][1])
                    if kt >= 3:
                        wait(t, exp_ck[kt - 3])
                    last = None
                    for (lo, hi) in groups(SW, qlo):
                        last = t.matmul(
                            p_sc[buf][:, lo:hi],
                            k_fm[:, hl * S + kt * 128:hl * S + (kt + 1) * 128],
                            q_fm[:, hl * S + s * SW + lo:hl * S + s * SW + hi],
                            start=True, stop=True)
                    ck[("sc", sidx, kt)] = t_inc(last)

                @block.scalar
                def _(s_, kt=kt, qlo=qlo, buf=buf, sidx=sidx):
                    wait(s_, ck[("sc", sidx, kt)])
                    if kt >= 3:
                        wait(s_, attnv_ck[kt - 3])
                        wait(s_, pacc_ck[kt - 3])
                    exp_ck[kt] = s_inc(s_.activation(
                        ptile[buf][:, qlo:SW], p_sc[buf][:, qlo:SW], ACT.Exp,
                        bias=negm0[:, sidx:sidx + 1], scale=ISQ))

                @block.vector
                def _(v, kt=kt, qlo=qlo, buf=buf, diag=diag):
                    wait(v, exp_ck[kt])
                    if diag:
                        mask_ck[kt] = v_inc(v.tensor_mul(
                            ptile[buf][:, qlo:qlo + 128],
                            ptile[buf][:, qlo:qlo + 128], tri01[:]))
                        v.drain()
                    if kt == 0:
                        wait(v, last_pacc_reset)
                    pacc_ck[kt] = v_inc(v.tensor_add(
                        pacc[:, qlo:SW], pacc[:, qlo:SW],
                        ptile[buf][:, qlo:SW]))
                    v.drain()

                if kt >= 1:
                    emit_attnv(kt - 1)
            emit_attnv(nkt - 1)

            @block.tensor
            def _(t, nkt=nkt, sidx=sidx):
                wait(t, pacc_ck[nkt - 1])
                for back in range(1, min(3, nkt) + 1):
                    wait(t, exp_ck[nkt - back])
                last = None
                for (lo, hi) in groups(SW):
                    last = t.matmul(p_sc[0][0:1, lo:hi], ones_cf[:],
                                    pacc[:, lo:hi], start=True, stop=True)
                ck[("sums", sidx)] = t_inc(last)

            @block.vector
            def _(v, sidx=sidx):
                wait(v, ck[("sums", sidx)])
                ck[("sumr", sidx)] = v_inc(v.tensor_copy(
                    out=sum_row[:], in_=p_sc[0][0:1, 0:SW]))
                v.drain()
                ck[("paccr", sidx)] = v_inc(v.memset(pacc[:], 0.0))

            if dbg and sidx == 0:
                @block.sync
                def _(sync):
                    wait(sync, ck[("sumr", 0)])
                    dma(sync, "dbgq", dbg_pacc, pacc[:])

            last_pacc_reset = ck[("paccr", sidx)]

            @block.tensor
            def _(t, sidx=sidx, nkt=nkt):
                wait(t, ck[("sumr", sidx)])
                wait(t, attnv_ck[nkt - 1])
                last = None
                for (lo, hi) in groups(SW):
                    last = t.matmul(p_sc[0][:, lo:hi], ones_rf[:],
                                    sum_row[0:1, lo:hi],
                                    start=True, stop=True)
                ck[("repn", sidx)] = t_inc(last)

            @block.vector
            def _(v, sidx=sidx):
                wait(v, ck[("repn", sidx)])
                v.reciprocal(out=nrep_sb[:], in_=p_sc[0][:])
                v.drain()
                if sidx >= 2:
                    wait(v, ck[("agst", sidx - 2)])
                ck[("evac", sidx)] = v_inc(v.tensor_mul(
                    astrip[sidx % 2][:], p_out[:], nrep_sb[:]))

            last_strip_evac = ck[("evac", sidx)]

            @block.gpsimd
            def _(g, s=s, hl=hl, sidx=sidx):
                wait(g, ck[("evac", sidx)])
                ck[("agst", sidx)] = dma(
                    g, f"ag{sidx % 2}",
                    ago_in.ap()[sidx * 128:(sidx + 1) * 128, :],
                    astrip[sidx % 2][:])

        @block.gpsimd
        def _(g, s=s):
            for hl in range(HPC):
                wait(g, ck[("agst", s * HPC + hl)])
            ck["cc_strip"][s] = c_inc(g.collective_compute(
                "AllGather", ALU.bypass, replica_groups=RG,
                ins=[ago_in.ap()[s * HPC * 128:(s + 1) * HPC * 128, :].opt()],
                outs=[(ago_ab if s < NSTRIP // 2 else ago_cd).ap()
                      [(s % (NSTRIP // 2)) * D:
                       (s % (NSTRIP // 2) + 1) * D, :].opt()]))

        if s == NSTRIP // 2 - 1:
            @block.gpsimd
            def _(g):
                wait(g, ck["cc_strip"][NSTRIP // 2 - 1])
                wait(g, ck["v_idx"])
                r = None
                for rr in range(DC):
                    g.indirect_dma_start(
                        amg[:, rr * TOKC:(rr + 1) * TOKC], None,
                        ago_flat_a,
                        IndirectOffsetOnAxis(idx16[0][:, rr:rr + 1], 0),
                    ).then_inc(getsem("gth"), 16)
                    r = ("gth", bump("gth", 16))
                ck[("gather", 0)] = r

    p4s.close()
    p34.close()

    pH = ExitStack()
    h2 = sbuf("h2", [128, DC * TOK], BF16, st=pH)
    p67 = ExitStack()
    xr_roll = [sbuf(f"xr{i}", [128, TOKC], st=p67, side="right")
               for i in range(2)]
    x1_roll = [sbuf(f"x1r{i}", [128, TOKC], st=p67, side="right")
               for i in range(2)]
    x1c_bf = sbuf("x1c_bf", [128, DC * TOKC], BF16, st=p67, side="right")
    x1s_bf = sbuf("x1s_bf", [128, DC * TOKC], BF16, st=p67, side="right")
    ln2row = sbuf("ln2row", [1, 2 * TOKC], st=p67, side="right")
    ln2m = sbuf("ln2m", [1, TOKC], st=p67, side="right")
    ln2v = sbuf("ln2v", [1, TOKC], st=p67, side="right")
    ln2sd = sbuf("ln2sd", [1, TOKC], st=p67, side="right")
    ln2r = sbuf("ln2r", [1, TOKC], st=p67, side="right")
    apply_scr = sbuf("apply_scr", [128, TOKC], st=p67, side="right")
    p_op = [psum(f"p_op{i}", [128, TOKC], st=p67) for i in range(2)]
    SQO2 = max(TOKC, 512)
    p_st2 = psum("p_st2", [1, SQO2 + TOKC], st=p67)
    p_r2m = psum("p_r2m", [128, TOKC], st=p67)
    p_r2r = psum("p_r2r", [128, TOKC], st=p67)

    wov = wout_i.rearrange("p (cd e) -> p cd e", e=D)
    h2_ck = []
    prv = {k: {0: None, 1: None} for k in
           ("woc", "opt", "opx1", "opev", "x1sp")}

    for g_ in range(2):
        if g_ == 1:
            @block.gpsimd
            def _(g):
                wait(g, ck["cc_strip"][NSTRIP - 1])
                wait(g, ck[("opt", 0, DC - 1)])
                r = None
                for rr in range(DC):
                    g.indirect_dma_start(
                        amg[:, rr * TOKC:(rr + 1) * TOKC], None,
                        ago_flat_b,
                        IndirectOffsetOnAxis(idx16[1][:, rr:rr + 1], 0),
                    ).then_inc(getsem("gth"), 16)
                    r = ("gth", bump("gth", 16))
                ck[("gather", 1)] = r

        for o in range(DC):
            bf = o % 2

            @block.sync
            def _(sync, g_=g_, o=o, bf=bf):
                if prv["woc"][bf] is not None:
                    wait(sync, prv["woc"][bf])
                else:
                    wait(sync, ck["win_evac"][NWIN - 1])
                ck[("ld_wor", g_, o)] = dma(
                    sync, f"wor{bf}",
                    wo_raw[bf][:].rearrange("p (cd e) -> p cd e", e=128),
                    wov[:, :, o * 128:(o + 1) * 128])
                if prv["opx1"][bf] is not None:
                    wait(sync, prv["opx1"][bf])
                else:
                    wait(sync, ck[("agst", NSTRIP * HPC - 2)])
                    wait(sync, ck[("agst", NSTRIP * HPC - 1)])
                ck[("ld_xr", g_, o)] = dma(
                    sync, f"xr{bf}", xr_roll[bf][:],
                    xown_i[:, o * TOK + g_ * TOKC:o * TOK + g_ * TOKC + TOKC])

            @block.vector
            def _(v, g_=g_, o=o, bf=bf):
                wait(v, ck[("ld_wor", g_, o)])
                if prv["opt"][bf] is not None:
                    wait(v, prv["opt"][bf])
                ck[("woc", g_, o)] = v_inc(v.tensor_copy(
                    out=wo_buf[bf][:], in_=wo_raw[bf][:]))
            prv["woc"][bf] = ck[("woc", g_, o)]

            @block.tensor
            def _(t, g_=g_, o=o, bf=bf):
                wait(t, ck[("woc", g_, o)])
                if o == 0:
                    wait(t, ck[("gather", g_)])
                if prv["opx1"][bf] is not None:
                    wait(t, prv["opx1"][bf])
                last = None
                for cd in range(DC):
                    last = t.matmul(p_op[bf][:],
                                    wo_buf[bf][:, cd * 128:(cd + 1) * 128],
                                    amg[:, cd * TOKC:(cd + 1) * TOKC],
                                    start=(cd == 0), stop=(cd == DC - 1))
                ck[("opt", g_, o)] = t_inc(last)
            prv["opt"][bf] = ck[("opt", g_, o)]

            @block.vector
            def _(v, g_=g_, o=o, bf=bf):
                wait(v, ck[("opt", g_, o)])
                wait(v, ck[("ld_xr", g_, o)])
                if g_ == 0 and o == 0:
                    wait(v, ck["v_mod"])
                if prv["x1sp"][bf] is not None:
                    wait(v, prv["x1sp"][bf])
                v.tensor_scalar(x1_roll[bf][:], p_op[bf][:],
                                ga_c[:, o:o + 1], gabout_c[:, o:o + 1],
                                ALU.mult, ALU.add)
                v.drain()
                ck[("opx1", g_, o)] = v_inc(v.tensor_add(
                    x1_roll[bf][:], x1_roll[bf][:], xr_roll[bf][:]))
                v.drain()
                ck[("opev", g_, o)] = v_inc(v.tensor_copy(
                    out=x1c_bf[:, o * TOKC:(o + 1) * TOKC],
                    in_=x1_roll[bf][:]))
            prv["opx1"][bf] = ck[("opx1", g_, o)]
            prv["opev"][bf] = ck[("opev", g_, o)]

            @block.sync
            def _(sync, g_=g_, o=o, bf=bf):
                wait(sync, ck[("opev", g_, o)])
                ck[("x1sp", g_, o)] = dma(
                    sync, f"x1s{bf}",
                    x1_sp.ap()[:, o * TOK + g_ * TOKC:o * TOK + g_ * TOKC + TOKC],
                    x1_roll[bf][:])
            prv["x1sp"][bf] = ck[("x1sp", g_, o)]

        if dbg and g_ == 0:
            @block.sync
            def _(sync):
                wait(sync, ck[("m0", NSTRIP * HPC - 1)])
                dma(sync, "dbgq", dbg_m0, negm0[:])
                dma(sync, "dbgq", dbg_nrm[0:1, :], nrm_q[:])
                dma(sync, "dbgq", dbg_nrm[1:2, :], nrm_k[:])
                wait(sync, ck["cc_strip"][NSTRIP - 1])
                dma(sync, "dbgq", dbg_agi, ago_in.ap())
                dma(sync, "dbgq", dbg_agab, ago_ab.ap())
                wait(sync, ck[("opt", 0, DC - 1)])
                ck["dbg2"] = dma(sync, "dbgq", dbg_amg, amg[:])

        @block.vector
        def _(v, g_=g_):
            wait(v, ck[("opev", g_, DC - 1)])
            v.drain()
            ck[("l2sq", g_)] = v_inc(v.tensor_mul(x1s_bf[:], x1c_bf[:],
                                                  x1c_bf[:]))

        @block.tensor
        def _(t, g_=g_):
            wait(t, ck[("l2sq", g_)])
            last = None
            for cd in range(DC):
                for (lo, hi) in groups(TOKC):
                    last = t.matmul(p_st2[0:1, lo:hi], ones_cb[:],
                                    x1c_bf[:, cd * TOKC + lo:cd * TOKC + hi],
                                    start=(cd == 0), stop=(cd == DC - 1),
                                    skip_group_check=True)
            for cd in range(DC):
                for (lo, hi) in groups(TOKC):
                    last = t.matmul(p_st2[0:1, SQO2 + lo:SQO2 + hi], ones_cb[:],
                                    x1s_bf[:, cd * TOKC + lo:cd * TOKC + hi],
                                    start=(cd == 0), stop=(cd == DC - 1),
                                    skip_group_check=True)
            ck[("l2mm", g_)] = t_inc(last)

        @block.vector
        def _(v, g_=g_):
            wait(v, ck[("l2mm", g_)])
            v.tensor_scalar_mul(ln2m[:], p_st2[0:1, 0:TOKC], 1.0 / D)
            v.tensor_scalar_mul(ln2v[:], p_st2[0:1, SQO2:SQO2 + TOKC], 1.0 / D)
            v.drain()
            v.tensor_mul(ln2row[0:1, 0:TOKC], ln2m[:], ln2m[:])
            v.drain()
            v.tensor_sub(ln2v[:], ln2v[:], ln2row[0:1, 0:TOKC])
            v.drain()
            ck[("l2var", g_)] = v_inc(v.tensor_scalar_add(ln2v[:], ln2v[:],
                                                          LN_EPS))

        @block.scalar
        def _(s_, g_=g_):
            wait(s_, ck[("l2var", g_)])
            s_.activation(ln2sd[:], ln2v[:], ACT.Sqrt)
            s_.drain()
            s_.activation(ln2row[0:1, TOKC:2 * TOKC], ln2sd[:], ACT.Ln)
            s_.drain()
            ck[("l2r", g_)] = s_inc(s_.activation(
                ln2r[:], ln2row[0:1, TOKC:2 * TOKC], ACT.Exp, scale=-1.0))

        @block.tensor
        def _(t, g_=g_):
            wait(t, ck[("l2r", g_)])
            if g_ == 1:
                wait(t, ck[("l2apply", 0)])
            last = None
            for (lo, hi) in groups(TOKC):
                t.matmul(p_r2m[:, lo:hi], ones_rf[:], ln2m[0:1, lo:hi],
                         start=True, stop=True)
                last = t.matmul(p_r2r[:, lo:hi], ones_rf[:], ln2r[0:1, lo:hi],
                                start=True, stop=True)
            ck[("l2rep", g_)] = t_inc(last)

        @block.vector
        def _(v, g_=g_):
            wait(v, ck[("l2rep", g_)])
            r = None
            for cd in range(DC):
                v.tensor_sub(apply_scr[:], x1c_bf[:, cd * TOKC:(cd + 1) * TOKC],
                             p_r2m[:])
                v.drain()
                v.tensor_mul(apply_scr[:], apply_scr[:], p_r2r[:])
                v.drain()
                r = v_inc(v.tensor_scalar(
                    h2[:, cd * TOK + g_ * TOKC:cd * TOK + g_ * TOKC + TOKC],
                    apply_scr[:], cb1_c[:, cd:cd + 1], sb_c[:, cd:cd + 1],
                    ALU.mult, ALU.add))
                v.drain()
            ck[("l2apply", g_)] = r

        h2_ck.append(ck[("l2apply", g_)])

    p67.close()
    pW.close()

    pG = ExitStack()
    gu = sbuf("gu", [128, HC * TOK], BF16, st=pG, side="right")
    cs2_rows = sbuf("cs2_rows", [2, TOK], BF16, st=pG, side="right")
    csscr = sbuf("csscr", [1, TOK], st=pG, side="right")
    amr_scr = sbuf("amr_scr", [128, 1], st=pG, side="right")
    up_scr = sbuf("up_scr", [128, TOK], st=pG, side="right")
    p8 = ExitStack()
    gate_b = sbuf("gate_b", [128, TOK], BF16, st=p8, side="right")
    wgi_b = [sbuf(f"wgi{i}", [128, DC * 128], I32, st=p8, side="right")
             for i in range(2)]
    wui_b = [sbuf(f"wui{i}", [128, DC * 128], I32, st=p8, side="right")
             for i in range(2)]
    wgt_b = [sbuf(f"wgt{i}", [128, DC * 128], BF16, st=p8, side="right")
             for i in range(2)]
    wut_b = [sbuf(f"wut{i}", [128, DC * 128], BF16, st=p8, side="right")
             for i in range(2)]
    cs_rows = sbuf("cs_rows", [2, TOK], BF16, st=p8, side="right")
    p_g = [psum(f"p_g{i}", [128, TOK], st=p8) for i in range(2)]
    p_u = [psum(f"p_u{i}", [128, TOK], st=p8) for i in range(2)]

    if dbg:
        @block.sync
        def _(sync):
            wait(sync, h2_ck[1])
            ck["dbg3"] = dma(sync, "dbgq", dbg_h2, h2[:])

    ck["x1s_fin"] = {0: ("x1s0", cnt.get("x1s0", 0)),
                     1: ("x1s1", cnt.get("x1s1", 0))}
    p8d = ExitStack()

    @block.tensor
    def _(t):
        wait(t, h2_ck[1])
        last = None
        for cd in range(DC):
            for (lo, hi) in groups(TOK):
                last = t.matmul(p_g[0][0:1, lo:hi], ones_cb[:],
                                h2[:, cd * TOK + lo:cd * TOK + hi],
                                start=(cd == 0), stop=(cd == DC - 1))
        ck["csh_t"] = t_inc(last)

    @block.vector
    def _(v):
        wait(v, ck["csh_t"])
        v.tensor_copy(out=cs_rows[0:1, :], in_=p_g[0][0:1, 0:TOK])
        v.drain()
        ck["csh_v"] = v_inc(v.tensor_sub(csscr[:], p_g[0][0:1, 0:TOK],
                                         cs_rows[0:1, :]))

    @block.gpsimd
    def _(g):
        wait(g, ck["csh_v"])
        ck["d_csh"] = dma(g, "csd", cs_rows[1:2, :], csscr[:])

    for hh in range(HC):
        @block.scalar
        def _(s_, hh=hh):
            if hh >= 2:
                wait(s_, ck[("wgc", hh - 2)])
            else:
                wait(s_, h2_ck[1])
                wait(s_, ck["x1s_fin"][0])
                wait(s_, ck["x1s_fin"][1])
            ck[("ldwg", hh)] = dma(
                s_, f"wg{hh % 2}", wgi_b[hh % 2][:],
                wg_i[:, hh * DC * 128:(hh + 1) * DC * 128])

        @block.sync
        def _(sync, hh=hh):
            if hh >= 2:
                wait(sync, ck[("wuc", hh - 2)])
            else:
                wait(sync, h2_ck[1])
                wait(sync, ck["x1s_fin"][0])
                wait(sync, ck["x1s_fin"][1])
            ck[("ldwu", hh)] = dma(
                sync, f"wu{hh % 2}", wui_b[hh % 2][:],
                wu_i[:, hh * DC * 128:(hh + 1) * DC * 128])

        @block.vector
        def _(v, hh=hh):
            wait(v, ck[("ldwg", hh)])
            if hh >= 2:
                wait(v, ck[("gmm", hh - 2)])
            ck[("wgc", hh)] = v_inc(v.tensor_copy(out=wgt_b[hh % 2][:],
                                                  in_=wgi_b[hh % 2][:]))
            wait(v, ck[("ldwu", hh)])
            if hh >= 2:
                wait(v, ck[("umm", hh - 2)])
            ck[("wuc", hh)] = v_inc(v.tensor_copy(out=wut_b[hh % 2][:],
                                                  in_=wui_b[hh % 2][:]))

        @block.tensor
        def _(t, hh=hh):
            wait(t, ck[("wgc", hh)])
            if hh == 0:
                wait(t, ck["d_csh"])
            if hh >= 2:
                wait(t, ck[("silu", hh - 2)])
            last = None
            for cd in range(DC):
                for (lo, hi) in groups(TOK):
                    last = t.matmul(
                        p_g[hh % 2][:, lo:hi],
                        wgt_b[hh % 2][:, cd * 128:(cd + 1) * 128],
                        h2[:, cd * TOK + lo:cd * TOK + hi],
                        start=(cd == 0), stop=False)
            for (lo, hi) in groups(TOK):
                last = t.matmul(p_g[hh % 2][:, lo:hi], neg128b[:],
                                cs_rows[:, lo:hi], start=False, stop=True)
            ck[("gmm", hh)] = t_inc(last)
            wait(t, ck[("wuc", hh)])
            if hh >= 2:
                wait(t, ck[("gu", hh - 2)])
            for cd in range(DC):
                for (lo, hi) in groups(TOK):
                    last = t.matmul(
                        p_u[hh % 2][:, lo:hi],
                        wut_b[hh % 2][:, cd * 128:(cd + 1) * 128],
                        h2[:, cd * TOK + lo:cd * TOK + hi],
                        start=(cd == 0), stop=False)
            for (lo, hi) in groups(TOK):
                last = t.matmul(p_u[hh % 2][:, lo:hi], neg128b[:],
                                cs_rows[:, lo:hi], start=False, stop=True)
            ck[("umm", hh)] = t_inc(last)

        @block.scalar
        def _(s_, hh=hh):
            wait(s_, ck[("gmm", hh)])
            if hh >= 2:
                wait(s_, ck[("gu", hh - 2)])
            ck[("silu", hh)] = s_inc(s_.activation(
                gate_b[:], p_g[hh % 2][:], ACT.Silu,
                scale=sg_c[:, hh:hh + 1]))

        @block.vector
        def _(v, hh=hh):
            wait(v, ck[("silu", hh)])
            wait(v, ck[("umm", hh)])
            v.tensor_scalar(up_scr[:], p_u[hh % 2][:], su_c[:, hh:hh + 1],
                            None, ALU.mult)
            v.drain()
            ck[("gu", hh)] = v_inc(v.tensor_mul(
                gu[:, hh * TOK:(hh + 1) * TOK], up_scr[:], gate_b[:]))

    @block.tensor
    def _(t):
        wait(t, ck[("gu", HC - 1)])
        wait(t, ck[("gu", HC - 2)])
        last = None
        for hh in range(HC):
            for (lo, hi) in groups(TOK):
                last = t.matmul(p_u[0][0:1, lo:hi], ones_cb[:],
                                gu[:, hh * TOK + lo:hh * TOK + hi],
                                start=(hh == 0), stop=(hh == HC - 1))
        ck["csg_t"] = t_inc(last)

    @block.vector
    def _(v):
        wait(v, ck["csg_t"])
        v.tensor_copy(out=cs2_rows[0:1, :], in_=p_u[0][0:1, 0:TOK])
        v.drain()
        ck["csg_v"] = v_inc(v.tensor_sub(csscr[:], p_u[0][0:1, 0:TOK],
                                         cs2_rows[0:1, :]))

    @block.gpsimd
    def _(g):
        wait(g, ck["csg_v"])
        ck["d_csg"] = dma(g, "csd", cs2_rows[1:2, :], csscr[:])

    p8.close()
    pH.close()
    wdi_b = [sbuf(f"wdi{i}", [128, HC * 128], I32, st=p8d, side="right")
             for i in range(2)]
    wdt_b = [sbuf(f"wdt{i}", [128, HC * 128], BF16, st=p8d, side="right")
             for i in range(2)]
    x1l = [sbuf(f"x1l{i}", [128, TOK], st=p8d, side="right")
           for i in range(2)]
    x2r = [sbuf(f"x2r{i}", [128, TOK], st=p8d, side="right")
           for i in range(2)]

    for o in range(DC):
        @block.sync
        def _(sync, o=o):
            if o >= 2:
                wait(sync, ck[("wdc", o - 2)])
            else:
                wait(sync, ck[("gu", HC - 1)])
                wait(sync, ck["x1s_fin"][0])
                wait(sync, ck["x1s_fin"][1])
            ck[("ldwd", o)] = dma(
                sync, f"wdl{o % 2}", wdi_b[o % 2][:],
                wd_i[:, o * HC * 128:(o + 1) * HC * 128])
            wait(sync, ck["x1s_fin"][0])
            wait(sync, ck["x1s_fin"][1])
            if o >= 2:
                wait(sync, ck[("dnev", o - 2)])
            ck[("ldx1", o)] = dma(sync, f"lx{o % 2}", x1l[o % 2][:],
                                  x1_sp.ap()[:, o * TOK:(o + 1) * TOK])

        @block.vector
        def _(v, o=o):
            wait(v, ck[("ldwd", o)])
            if o >= 2:
                wait(v, ck[("dmm", o - 2)])
            ck[("wdc", o)] = v_inc(v.tensor_copy(out=wdt_b[o % 2][:],
                                                 in_=wdi_b[o % 2][:]))

        @block.tensor
        def _(t, o=o):
            wait(t, ck[("wdc", o)])
            if o == 0:
                wait(t, ck["d_csg"])
            if o >= 2:
                wait(t, ck[("dnev", o - 2)])
            last = None
            for hh in range(HC):
                for (lo, hi) in groups(TOK):
                    last = t.matmul(
                        p_g[o % 2][:, lo:hi],
                        wdt_b[o % 2][:, hh * 128:(hh + 1) * 128],
                        gu[:, hh * TOK + lo:hh * TOK + hi],
                        start=(hh == 0), stop=False)
            for (lo, hi) in groups(TOK):
                last = t.matmul(p_g[o % 2][:, lo:hi], neg128b[:],
                                cs2_rows[:, lo:hi], start=False, stop=True)
            ck[("dmm", o)] = t_inc(last)

        @block.vector
        def _(v, o=o):
            wait(v, ck[("dmm", o)])
            wait(v, ck[("ldx1", o)])
            if o >= 2:
                wait(v, ck[("ost", o - 2)])
            v.tensor_scalar(x2r[o % 2][:], p_g[o % 2][:],
                            gbsd_c[:, o:o + 1], None, ALU.mult)
            v.drain()
            ck[("dnev", o)] = v_inc(v.tensor_add(
                x2r[o % 2][:], x2r[o % 2][:], x1l[o % 2][:]))

        @block.sync
        def _(sync, o=o):
            wait(sync, ck[("dnev", o)])
            ck[("ost", o)] = dma(sync, f"st{o % 2}",
                                 out_t[:, o * TOK:(o + 1) * TOK],
                                 x2r[o % 2][:])

    p8d.close()
    pG.close()

    @block.sync
    def _(sync):
        sync.wait_ge(sems["st0"], cnt["st0"])
        sync.wait_ge(sems["st1"], cnt["st1"])
        if dbg:
            sync.wait_ge(sems["dbgq"], cnt["dbgq"])

    es.close()
    return nc


S_FULL = 8192
HID_FULL = 5632


def _chunk_major(a):
    Dn, N = a.shape
    return np.ascontiguousarray(
        a.reshape(Dn // 128, 128, N).transpose(1, 0, 2).reshape(128, -1))


def prep_inputs(x, w_qkv, b_qkv, w_out, b_out, wq_g, s_g, wq_u, s_u,
                wq_d, s_d, cvec, wmod, S, HID):
    import ml_dtypes
    HC = HID // 128
    TOKC = S // 16
    xf = np.ascontiguousarray(x[0].T.astype(np.float32))
    xfull = _chunk_major(xf)
    woutT = _chunk_major(np.ascontiguousarray(w_out.T.astype(np.float32)))
    bout_col = np.ascontiguousarray(
        b_out.astype(np.float32).reshape(DC, 128).T)
    wg = np.ascontiguousarray(
        wq_g.reshape(HC, 128, DC, 128).transpose(3, 0, 2, 1)
        .reshape(128, -1).astype(np.int32))
    wu = np.ascontiguousarray(
        wq_u.reshape(HC, 128, DC, 128).transpose(3, 0, 2, 1)
        .reshape(128, -1).astype(np.int32))
    wd = np.ascontiguousarray(
        wq_d.reshape(DC, 128, HC, 128).transpose(3, 0, 2, 1)
        .reshape(128, -1).astype(np.int32))
    sg_col = np.ascontiguousarray(s_g.astype(np.float32).reshape(HC, 128).T)
    su_col = np.ascontiguousarray(s_u.astype(np.float32).reshape(HC, 128).T)
    sd_col = np.ascontiguousarray(s_d.astype(np.float32).reshape(DC, 128).T)
    c_col = np.ascontiguousarray(cvec[0].astype(np.float32).reshape(DC, 128).T)
    tri = (np.arange(128)[:, None] <= np.arange(128)[None, :]
           ).astype(ml_dtypes.bfloat16)
    SW = S // 8
    pr16 = np.empty((128, DC), np.float32)
    for rr in range(DC):
        pr16[:, rr] = (128 * rr + np.arange(128)) * SW

    ins = []
    for c in range(NCORES):
        Ka, Kb = c, 15 - c
        xo = np.empty((128, DC * 2 * TOKC), np.float32)
        for cd in range(DC):
            blk = np.empty((128, 2 * TOKC), np.float32)
            blk[:, :TOKC] = xfull[:, cd * S + Ka * TOKC:cd * S + (Ka + 1) * TOKC]
            blk[:, TOKC:] = xfull[:, cd * S + Kb * TOKC:cd * S + (Kb + 1) * TOKC]
            xo[:, cd * 2 * TOKC:(cd + 1) * 2 * TOKC] = blk
        wsl = np.concatenate([
            w_qkv[256 * c:256 * (c + 1)],
            w_qkv[D + 256 * c:D + 256 * (c + 1)],
            w_qkv[2 * D + 256 * c:2 * D + 256 * (c + 1)]], axis=0)
        wqkv = _chunk_major(np.ascontiguousarray(wsl.T.astype(np.float32)))
        bql = np.concatenate([
            b_qkv[256 * c:256 * (c + 1)],
            b_qkv[D + 256 * c:D + 256 * (c + 1)],
            b_qkv[2 * D + 256 * c:2 * D + 256 * (c + 1)]])[None, :]
        wmc = np.empty((D, 12, 128), np.float32)
        for j in range(12):
            vv, ii = j // 2, j % 2
            col0 = 2048 * vv + 128 * (2 * c + ii)
            wmc[:, j, :] = wmod[:, col0:col0 + 128]
        wm = np.ascontiguousarray(
            wmc.reshape(DC, 128, 12 * 128).transpose(1, 0, 2).reshape(128, -1))
        gb = np.empty((128, 2), np.float32)
        CPS = SW // TOKC
        NSTRIP = S // SW
        for gi, K in enumerate((Ka, Kb)):
            srel = (K // CPS) - (NSTRIP // 2 * gi)
            gb[:, gi] = srel * D * SW + (K % CPS) * TOKC
        ins.append(dict(
            xfull=xfull, xown=np.ascontiguousarray(xo), wm=wm,
            c_col=c_col, wqkv=wqkv,
            bqkv_row=np.ascontiguousarray(bql.astype(np.float32)),
            wout=woutT, bout_col=bout_col, wg=wg, wu=wu, wd=wd,
            sg_col=sg_col, su_col=su_col, sd_col=sd_col, tri=tri,
            pramp16=pr16, gbase=gb))
    return ins


def assemble_out(results, S):
    TOKC = S // 16
    out = np.empty((1, S, D), np.float32)
    for c in range(NCORES):
        r = results[c]["out_t"].reshape(128, DC, 2 * TOKC)
        for gi, K in enumerate((c, 15 - c)):
            blk = r[:, :, gi * TOKC:(gi + 1) * TOKC]
            out[0, K * TOKC:(K + 1) * TOKC, :] = (
                blk.transpose(2, 1, 0).reshape(TOKC, D))
    return out


def kernel(L0_attn_x, L0_attn_w_qkv, L0_attn_b_qkv, L0_attn_w_out,
           L0_attn_b_out, L0_ffn_wq_gate, L0_ffn_s_gate, L0_ffn_wq_up,
           L0_ffn_s_up, L0_ffn_wq_down, L0_ffn_s_down, L0_c, L0_wmod):
    S = int(np.asarray(L0_attn_x).shape[1])
    HID = int(np.asarray(L0_ffn_wq_gate).shape[0])
    nc = build_nc(S, HID)
    ins = prep_inputs(
        np.asarray(L0_attn_x), np.asarray(L0_attn_w_qkv),
        np.asarray(L0_attn_b_qkv), np.asarray(L0_attn_w_out),
        np.asarray(L0_attn_b_out), np.asarray(L0_ffn_wq_gate),
        np.asarray(L0_ffn_s_gate), np.asarray(L0_ffn_wq_up),
        np.asarray(L0_ffn_s_up), np.asarray(L0_ffn_wq_down),
        np.asarray(L0_ffn_s_down), np.asarray(L0_c), np.asarray(L0_wmod),
        S, HID)
    res = run_bass_kernel_spmd(nc, ins, core_ids=list(range(NCORES)))
    return assemble_out(res.results, S)


# version 12
# speedup vs baseline: 1.2504x; 1.1335x over previous
import math
import numpy as np
from contextlib import ExitStack

import concourse.bass as bass
import concourse.mybir as mybir
from concourse.bass import IndirectOffsetOnAxis
from concourse.bass_utils import run_bass_kernel_spmd

F32 = mybir.dt.float32
BF16 = mybir.dt.bfloat16
I32 = mybir.dt.int32
ALU = mybir.AluOpType
ACT = mybir.ActivationFunctionType
AXX = mybir.AxisListType.X

NCORES = 8
D = 2048
DC = D // 128
HPC = 2
HD = 128
ESL = 3 * HPC * HD
ISQ = 1.0 / math.sqrt(HD)
LN_EPS = 1e-5


def groups(total, start=0, maxn=512):
    out = []
    lo = start
    while lo < total:
        hi = min(total, (lo // maxn + 1) * maxn)
        out.append((lo, hi))
        lo = hi
    return out


def build_nc(S, HID, dbg=False):
    SW = S // 8
    NSTRIP = S // SW
    SKT = SW // 128
    NB = max(1, SW // 512)
    CPS = SW // (S // 16)
    TOKC = S // 16
    TOK = 2 * TOKC
    HC = HID // 128
    WIN = 512
    NWIN = S // WIN

    nc = bass.Bass("TRN2", target_bir_lowering=False, debug=False)
    es = ExitStack()
    cnt = {}
    sems = {}

    def getsem(name):
        if name not in sems:
            sems[name] = es.enter_context(nc.semaphore(name))
        return sems[name]

    for sn in ("TS", "VS", "SS", "CC"):
        getsem(sn)

    def bump(name, by):
        cnt[name] = cnt.get(name, 0) + by
        return cnt[name]

    def wait(eng, ckv):
        eng.wait_ge(sems[ckv[0]], ckv[1])

    def sbuf(name, shape, dt=F32, st=None, side=None):
        return (st or es).enter_context(
            nc.sbuf_tensor(name, shape, dt, side=side))

    def psum(name, shape, st=None):
        return (st or es).enter_context(nc.psum_tensor(name, shape, F32))

    DI = lambda n, s, d: nc.dram_tensor(n, s, d, kind="ExternalInput").ap()
    xfull = DI("xfull", [128, DC * S], F32)
    xown_i = DI("xown", [128, DC * TOK], F32)
    wm_i = DI("wm", [128, DC * 12 * 128], F32)
    c_i = DI("c_col", [128, DC], F32)
    wqkv_i = DI("wqkv", [128, DC * ESL], F32)
    bqkv_i = DI("bqkv_row", [1, ESL], F32)
    wout_i = DI("wout", [128, DC * D], F32)
    bout_i = DI("bout_col", [128, DC], F32)
    wg_i = DI("wg", [128, HC * DC * 128], I32)
    wu_i = DI("wu", [128, HC * DC * 128], I32)
    wd_i = DI("wd", [128, DC * HC * 128], I32)
    sg_i = DI("sg_col", [128, HC], F32)
    su_i = DI("su_col", [128, HC], F32)
    sd_i = DI("sd_col", [128, DC], F32)
    tri_i = DI("tri", [128, 128], BF16)
    pr16_i = DI("pramp16", [128, DC], F32)
    gbase_i = DI("gbase", [128, 2], F32)
    out_t = nc.dram_tensor("out_t", [128, DC * TOK], F32,
                           kind="ExternalOutput").ap()
    if dbg:
        dbg_mod = nc.dram_tensor("dbg_mod", [128, NCORES * 12], F32,
                                 kind="ExternalOutput").ap()
        dbg_st = nc.dram_tensor("dbg_st", [NCORES, 2 * TOK], F32,
                                kind="ExternalOutput").ap()
        dbg_rows = nc.dram_tensor("dbg_rows", [34, S], BF16,
                                  kind="ExternalOutput").ap()
        dbg_corr = nc.dram_tensor("dbg_corr", [34, ESL], BF16,
                                  kind="ExternalOutput").ap()
        dbg_qf = nc.dram_tensor("dbg_qf", [128, HPC * S], BF16,
                                kind="ExternalOutput").ap()
        dbg_kf = nc.dram_tensor("dbg_kf", [128, HPC * S], BF16,
                                kind="ExternalOutput").ap()
        dbg_vt = nc.dram_tensor("dbg_vt", [128, HPC * S], BF16,
                                kind="ExternalOutput").ap()
        dbg_amg = nc.dram_tensor("dbg_amg", [128, DC * TOKC], BF16,
                                 kind="ExternalOutput").ap()
        dbg_h2 = nc.dram_tensor("dbg_h2", [128, DC * TOK], BF16,
                                kind="ExternalOutput").ap()
        dbg_agi = nc.dram_tensor("dbg_agi", [NSTRIP * HPC * HD, SW], BF16,
                                 kind="ExternalOutput").ap()
        dbg_m0 = nc.dram_tensor("dbg_m0", [128, HPC * 4], F32,
                                kind="ExternalOutput").ap()
        dbg_nrm = nc.dram_tensor("dbg_nrm", [2, HPC * NWIN], F32,
                                 kind="ExternalOutput").ap()
        dbg_pacc = nc.dram_tensor("dbg_pacc", [128, SW], F32,
                                  kind="ExternalOutput").ap()
        dbg_rinv = nc.dram_tensor("dbg_rinv", [1, SW], BF16,
                                  kind="ExternalOutput").ap()
        dbg_agab = nc.dram_tensor("dbg_agab", [NSTRIP // 2 * D, SW], BF16,
                                  kind="ExternalOutput").ap()

    agma_i = nc.dram_tensor("agma_i", [128, 4], F32)
    agma_o = nc.dram_tensor("agma_o", [NCORES * 128, 4], F32)
    agmb_i = nc.dram_tensor("agmb_i", [128, 8], F32)
    agmb_o = nc.dram_tensor("agmb_o", [NCORES * 128, 8], F32)
    agst_i = nc.dram_tensor("agst_i", [1, 2 * TOK], F32)
    agst_o = nc.dram_tensor("agst_o", [NCORES, 2 * TOK], F32)
    ago_in = nc.dram_tensor("ago_in", [NSTRIP * HPC * HD, SW], BF16)
    ago_ab = nc.dram_tensor("ago_ab", [NSTRIP // 2 * D, SW], BF16,
                            addr_space="Shared")
    ago_cd = nc.dram_tensor("ago_cd", [NSTRIP // 2 * D, SW], BF16,
                            addr_space="Shared")
    ago_flat_a = ago_ab.ap().rearrange("a b -> (a b)")[:, None]
    ago_flat_b = ago_cd.ap().rearrange("a b -> (a b)")[:, None]
    x1_sp = nc.dram_tensor("x1_sp", [128, DC * TOK], F32)

    ones_cf = sbuf("ones_cf", [128, 1], F32)
    ones_cb = sbuf("ones_cb", [128, 1], BF16)
    ones2b = sbuf("ones2b", [2, 128], BF16)
    neg128b = sbuf("neg128b", [2, 128], BF16)
    ones_rf = sbuf("ones_rf", [1, 128], F32)
    ones_rb = sbuf("ones_rb", [1, 128], BF16)
    tri01 = sbuf("tri01", [128, 128], BF16)
    pr16 = sbuf("pr16", [128, DC])
    gbase = sbuf("gbase_s", [128, 2])
    c_sb = sbuf("c_sb", [128, DC])
    csilu = sbuf("csilu", [128, DC])
    sa_c = sbuf("sa_c", [128, DC])
    sab_c = sbuf("sab_c", [128, DC], BF16)
    ca1_c = sbuf("ca1_c", [128, DC])
    ga_c = sbuf("ga_c", [128, DC])
    sb_c = sbuf("sb_c", [128, DC])
    cb1_c = sbuf("cb1_c", [128, DC])
    gb_c = sbuf("gb_c", [128, DC])
    gabout_c = sbuf("gabout_c", [128, DC])
    gbsd_c = sbuf("gbsd_c", [128, DC])
    sd_c = sbuf("sd_c", [128, DC])
    bout_c = sbuf("bout_c", [128, DC])
    sg_c = sbuf("sg_c", [128, HC])
    su_c = sbuf("su_c", [128, HC])
    negm0 = sbuf("negm0", [128, HPC * NSTRIP])
    m0s = sbuf("m0s", [1, 4])
    idx16f = sbuf("idx16f", [128, DC])
    idx16 = [sbuf(f"idx16_{g}", [128, DC], I32) for g in range(2)]

    block = es.enter_context(nc.Block())
    ck = {}

    def dma(eng, stream, out, in_, **kw):
        sem = getsem(stream)
        eng.dma_start(out=out, in_=in_, **kw).then_inc(sem, 16)
        return (stream, bump(stream, 16))

    def t_inc(inst):
        inst.then_inc(sems["TS"], 1)
        return ("TS", bump("TS", 1))

    def v_inc(inst):
        inst.then_inc(sems["VS"], 1)
        return ("VS", bump("VS", 1))

    def s_inc(inst):
        inst.then_inc(sems["SS"], 1)
        return ("SS", bump("SS", 1))

    def c_inc(inst):
        inst.then_inc(sems["CC"], 1)
        return ("CC", bump("CC", 1))

    RG = [list(range(NCORES))]

    p23 = ExitStack()
    wq_sb = sbuf("wq_sb", [128, DC * ESL], BF16, st=p23, side="right")
    corr4 = sbuf("corr4", [34, ESL], BF16, st=p23, side="right")
    corr_l = corr4[:][32:34, :]
    rows4 = sbuf("rows4", [34, S], BF16, st=p23, side="right")
    xw = [sbuf(f"xw{i}", [128, DC * WIN], BF16, st=p23, side="right")
          for i in range(2)]
    v_win = [sbuf(f"vwin{i}", [128, HPC * WIN], BF16, st=p23, side="right")
             for i in range(2)]
    sq_scr = sbuf("sq_scr", [128, WIN], BF16, st=p23, side="right")
    rep_sb = sbuf("rep_sb", [128, WIN], st=p23, side="right")
    rrows = rows4[:][0:2, :]
    crows = rows4[:][32:34, :]

    p01 = ExitStack()
    brow_f = sbuf("brow_f", [1, ESL], st=p01, side="right")
    wm_t = [sbuf(f"wm_t{i}", [128, 12 * 128], st=p01, side="right")
            for i in range(2)]
    xo_b = [sbuf(f"xo_b{i}", [128, TOK], BF16, st=p01, side="right")
            for i in range(2)]
    xo_s = [sbuf(f"xo_s{i}", [128, TOK], BF16, st=p01, side="right")
            for i in range(2)]
    stat8 = sbuf("stat8", [NCORES, 2 * TOK], st=p01, side="right")
    r8 = sbuf("r8", [NCORES, TOK], st=p01, side="right")
    m8n = sbuf("m8n", [NCORES, TOK], st=p01, side="right")
    lnscr8 = sbuf("lnscr8", [NCORES, TOK], st=p01, side="right")
    r8hi = sbuf("r8hi", [NCORES, TOK], BF16, st=p01, side="right")
    r8lo = sbuf("r8lo", [NCORES, TOK], BF16, st=p01, side="right")
    st_m = sbuf("st_m", [1, TOK], st=p01, side="right")
    st_v = sbuf("st_v", [1, TOK], st=p01, side="right")
    st_sd = sbuf("st_sd", [1, TOK], st=p01, side="right")
    modc = sbuf("modc", [128, NCORES * 12], st=p01, side="right")
    modacc = sbuf("modacc", [128, 12], st=p01, side="right")
    g_scr = sbuf("g_scr", [128, DC], st=p01, side="right")
    p_mod = psum("p_mod", [128, 12], st=p01)
    SQO = max(TOK, 512)
    p_st = psum("p_st", [1, SQO + TOK], st=p01)
    p_gemv = psum("p_gemv", [64, ESL], st=p01)

    @block.sync
    def _(sync):
        dma(sync, "cst", c_sb[:], c_i)
        dma(sync, "cst", tri01[:], tri_i)
        dma(sync, "cst", pr16[:], pr16_i)
        dma(sync, "cst", gbase[:], gbase_i)
        dma(sync, "cst", sg_c[:], sg_i)
        dma(sync, "cst", su_c[:], su_i)
        dma(sync, "cst", sd_c[:], sd_i)
        dma(sync, "cst", bout_c[:], bout_i)
        ck["ld_consts"] = dma(sync, "cst", brow_f[0:1, :], bqkv_i)

    @block.vector
    def _(v):
        v.memset(ones_cf[:], 1.0)
        v.memset(ones_cb[:], 1.0)
        v.memset(ones2b[:], 1.0)
        v.memset(neg128b[:], -128.0)
        v.memset(ones_rf[:], 1.0)
        ck["v_consts"] = v_inc(v.memset(ones_rb[:], 1.0))

    @block.scalar
    def _(s):
        wait(s, ck["ld_consts"])
        ck["s_csilu"] = s_inc(s.activation(csilu[:], c_sb[:], ACT.Silu))

    ck["xo_ld"] = {}
    ck["xo_sq"] = {}
    ck["xo_mm"] = {}
    for cd in range(DC):
        @block.gpsimd
        def _(g, cd=cd):
            if cd >= 2:
                wait(g, ck["xo_mm"][cd - 2])
            ck["xo_ld"][cd] = dma(g, f"xo{cd % 2}", xo_b[cd % 2][:],
                                  xown_i[:, cd * TOK:(cd + 1) * TOK])

        @block.vector
        def _(v, cd=cd):
            wait(v, ck["xo_ld"][cd])
            ck["xo_sq"][cd] = v_inc(v.tensor_mul(
                xo_s[cd % 2][:], xo_b[cd % 2][:], xo_b[cd % 2][:]))

        @block.tensor
        def _(t, cd=cd):
            wait(t, ck["xo_ld"][cd])
            if cd == 0:
                wait(t, ck["v_consts"])
            last = None
            for (lo, hi) in groups(TOK):
                last = t.matmul(p_st[0:1, lo:hi], ones_cb[:],
                                xo_b[cd % 2][:, lo:hi],
                                start=(cd == 0), stop=(cd == DC - 1),
                                skip_group_check=True)
            wait(t, ck["xo_sq"][cd])
            for (lo, hi) in groups(TOK):
                last = t.matmul(p_st[0:1, SQO + lo:SQO + hi], ones_cb[:],
                                xo_s[cd % 2][:, lo:hi],
                                start=(cd == 0), stop=(cd == DC - 1),
                                skip_group_check=True)
            ck["xo_mm"][cd] = t_inc(last)
    ck["mm_sq"] = ck["xo_mm"][DC - 1]

    ck["ld_wm"] = {}


    ck["mm_mod"] = []
    ck["v_macc"] = {}

    @block.vector
    def _(v):
        ck["v_macc0"] = v_inc(v.memset(modacc[:], 0.0))

    for cd in range(DC):
        @block.sync
        def _(sync, cd=cd):
            if cd >= 2:
                wait(sync, ck["mm_mod"][cd - 2])
            ck["ld_wm"][cd] = dma(sync, f"wm{cd % 2}", wm_t[cd % 2][:],
                                  wm_i[:, cd * 1536:(cd + 1) * 1536])

        @block.tensor
        def _(t, cd=cd):
            if cd == 0:
                wait(t, ck["s_csilu"])
            wait(t, ck["ld_wm"][cd])
            if cd >= 1:
                wait(t, ck["v_macc"][cd - 1])
            last = None
            for j in range(12):
                last = t.matmul(p_mod[:, j:j + 1],
                                wm_t[cd % 2][:, j * 128:(j + 1) * 128],
                                csilu[:, cd:cd + 1],
                                start=True, stop=True,
                                skip_group_check=True)
            ck["mm_mod"].append(t_inc(last))

        @block.vector
        def _(v, cd=cd):
            wait(v, ck["mm_mod"][cd])
            if cd == 0:
                wait(v, ck["v_macc0"])
            ck["v_macc"][cd] = v_inc(v.tensor_add(modacc[:], modacc[:],
                                                   p_mod[:]))
            v.drain()

    @block.vector
    def _(v):
        wait(v, ck["v_macc"][DC - 1])
        ck["v_moda"] = v_inc(v.tensor_copy(out=modc[:, 0:4],
                                           in_=modacc[:, 0:4]))
        ck["v_modb"] = v_inc(v.tensor_copy(out=modc[:, 4:12],
                                           in_=modacc[:, 4:12]))

    @block.sync
    def _(sync):
        wait(sync, ck["v_moda"])
        ck["st_moda"] = dma(sync, "stma", agma_i.ap(), modc[:, 0:4])
        wait(sync, ck["v_modb"])
        ck["st_modb"] = dma(sync, "stmb", agmb_i.ap(), modc[:, 4:12])

    @block.vector
    def _(v):
        wait(v, ck["mm_sq"])
        v.tensor_scalar_mul(st_m[:], p_st[0:1, 0:TOK], 1.0 / D)
        v.tensor_scalar_mul(st_v[:], p_st[0:1, SQO:SQO + TOK], 1.0 / D)
        v.drain()
        v.tensor_mul(lnscr8[0:1, :], st_m[:], st_m[:])
        v.drain()
        v.tensor_sub(st_v[:], st_v[:], lnscr8[0:1, :])
        v.drain()
        ck["v_var"] = v_inc(v.tensor_scalar_add(st_v[:], st_v[:], LN_EPS))

    @block.scalar
    def _(s):
        wait(s, ck["v_var"])
        ck["s_sd"] = s_inc(s.activation(st_sd[:], st_v[:], ACT.Sqrt))

    @block.sync
    def _(sync):
        wait(sync, ck["v_var"])
        dma(sync, "stst", agst_i.ap()[0:1, 0:TOK], st_m[:])
        wait(sync, ck["s_sd"])
        ck["st_stats"] = dma(sync, "stst", agst_i.ap()[0:1, TOK:2 * TOK],
                             st_sd[:])

    def emit_xwin(g, w):
        r = None
        for cd in range(DC):
            r = dma(g, f"xw{w % 2}", xw[w % 2][:, cd * WIN:(cd + 1) * WIN],
                    xfull[:, cd * S + w * WIN:cd * S + (w + 1) * WIN])
        return r

    ck["xwin"] = {}

    @block.gpsimd
    def _(g):
        ck["xwin"][0] = emit_xwin(g, 0)
        ck["ld_wq"] = []
        for cd in range(DC):
            ck["ld_wq"].append(
                dma(g, "wqc", wq_sb[:, cd * ESL:(cd + 1) * ESL],
                    wqkv_i[:, cd * ESL:(cd + 1) * ESL]))
        ck["xwin"][1] = emit_xwin(g, 1)
        wait(g, ck["st_moda"])
        ck["cc_moda"] = c_inc(g.collective_compute(
            "AllGather", ALU.bypass, replica_groups=RG,
            ins=[agma_i.ap().opt()], outs=[agma_o.ap().opt()]))
        wait(g, ck["st_modb"])
        ck["cc_modb"] = c_inc(g.collective_compute(
            "AllGather", ALU.bypass, replica_groups=RG,
            ins=[agmb_i.ap().opt()], outs=[agmb_o.ap().opt()]))
        wait(g, ck["st_stats"])
        ck["cc_stats"] = c_inc(g.collective_compute(
            "AllGather", ALU.bypass, replica_groups=RG,
            ins=[agst_i.ap().opt()], outs=[agst_o.ap().opt()]))

    @block.sync
    def _(sync):
        wait(sync, ck["cc_moda"])
        for b in range(NCORES):
            ck["ld_mca"] = dma(sync, "mbk", modc[:, b * 12:b * 12 + 4],
                               agma_o.ap()[b * 128:(b + 1) * 128, :])
        wait(sync, ck["cc_modb"])
        for b in range(NCORES):
            ck["ld_mcb"] = dma(sync, "mbk", modc[:, b * 12 + 4:b * 12 + 12],
                               agmb_o.ap()[b * 128:(b + 1) * 128, :])
        wait(sync, ck["cc_stats"])
        ck["ld_rows"] = dma(sync, "mbk", stat8[:], agst_o.ap())

    @block.vector
    def _(v):
        wait(v, ck["ld_rows"])
        mv = modc[:].rearrange("p (b j) -> p b j", j=12)
        v.tensor_copy(out=sa_c[:].rearrange("p (b j) -> p b j", j=2),
                      in_=mv[:, :, 0:2])
        v.tensor_copy(out=ca1_c[:].rearrange("p (b j) -> p b j", j=2),
                      in_=mv[:, :, 2:4])
        v.drain()
        v.tensor_copy(out=sab_c[:], in_=sa_c[:])
        ck["v_saca"] = v_inc(v.tensor_scalar_add(ca1_c[:], ca1_c[:], 1.0))
        v.tensor_copy(out=ga_c[:].rearrange("p (b j) -> p b j", j=2),
                      in_=mv[:, :, 4:6])
        v.tensor_copy(out=sb_c[:].rearrange("p (b j) -> p b j", j=2),
                      in_=mv[:, :, 6:8])
        v.tensor_copy(out=cb1_c[:].rearrange("p (b j) -> p b j", j=2),
                      in_=mv[:, :, 8:10])
        v.tensor_copy(out=gb_c[:].rearrange("p (b j) -> p b j", j=2),
                      in_=mv[:, :, 10:12])
        v.drain()
        v.tensor_scalar_add(cb1_c[:], cb1_c[:], 1.0)
        v.drain()
        for gt in (ga_c, gb_c):
            v.tensor_scalar(g_scr[:], gt[:], 0.0, 2.0, ALU.is_ge, ALU.mult)
            v.drain()
            v.tensor_scalar_add(g_scr[:], g_scr[:], -1.0)
            v.drain()
            v.tensor_mul(gt[:], gt[:], g_scr[:])
            v.drain()
            v.tensor_scalar_max(gt[:], gt[:], 0.25)
            v.drain()
            v.tensor_mul(gt[:], gt[:], g_scr[:])
            v.drain()
        v.tensor_mul(gabout_c[:], ga_c[:], bout_c[:])
        ck["v_mod"] = v_inc(v.tensor_mul(gbsd_c[:], gb_c[:], sd_c[:]))
        v.tensor_scalar(idx16f[:], pr16[:], gbase[:, 0:1], None, ALU.add)
        v.drain()
        v.tensor_copy(out=idx16[0][:], in_=idx16f[:])
        v.drain()
        v.tensor_scalar(idx16f[:], pr16[:], gbase[:, 1:2], None, ALU.add)
        v.drain()
        ck["v_idx"] = v_inc(v.tensor_copy(out=idx16[1][:], in_=idx16f[:]))

    @block.scalar
    def _(s):
        wait(s, ck["ld_rows"])
        s.activation(lnscr8[:], stat8[:, TOK:2 * TOK], ACT.Ln)
        s.drain()
        ck["s_rall"] = s_inc(s.activation(r8[:], lnscr8[:], ACT.Exp,
                                          scale=-1.0))

    @block.vector
    def _(v):
        wait(v, ck["s_rall"])
        v.tensor_copy(out=r8hi[:], in_=r8[:])
        v.drain()
        v.tensor_sub(lnscr8[:], r8[:], r8hi[:])
        v.drain()
        v.tensor_copy(out=r8lo[:], in_=lnscr8[:])
        ck["v_rows"] = v_inc(v.tensor_scalar_mul(m8n[:], stat8[:, 0:TOK],
                                                 -1.0))

    @block.sync
    def _(sync):
        wait(sync, ck["v_rows"])
        for K in range(16):
            sK = K if K <= 7 else 15 - K
            off = 0 if K <= 7 else TOKC
            cs = slice(K * TOKC, (K + 1) * TOKC)
            dma(sync, "rw2", rows4[0:1, cs], r8hi[sK:sK + 1, off:off + TOKC])
            dma(sync, "rw2", rows4[1:2, cs], r8lo[sK:sK + 1, off:off + TOKC])
        ck["d_rw2"] = ("rw2", cnt["rw2"])

    @block.gpsimd
    def _(g):
        wait(g, ck["v_rows"])
        for K in range(16):
            sK = K if K <= 7 else 15 - K
            off = 0 if K <= 7 else TOKC
            cs = slice(K * TOKC, (K + 1) * TOKC)
            dma(g, "rwd", rows4[32:33, cs], m8n[sK:sK + 1, off:off + TOKC])
            ck["d_sd"] = dma(g, "rwd", rows4[33:34, cs],
                             stat8[sK:sK + 1, TOK + off:TOK + off + TOKC])

    ck["wp_gemv"] = []
    ck["wp_scale"] = []
    ck["wp_sum"] = []
    for cd in range(DC):
        @block.tensor
        def _(t, cd=cd):
            if cd == 0:
                wait(t, ck["ld_wq"][DC - 1])
                wait(t, ck["v_saca"])
            last = None
            for (lo, hi) in groups(ESL):
                last = t.matmul(p_gemv[0:1, lo:hi], sab_c[:, cd:cd + 1],
                                wq_sb[:, cd * ESL + lo:cd * ESL + hi],
                                start=(cd == 0), stop=(cd == DC - 1))
            ck["wp_gemv"].append(t_inc(last))

        @block.vector
        def _(v, cd=cd):
            wait(v, ck["wp_gemv"][cd])
            ck["wp_scale"].append(v_inc(v.tensor_scalar(
                wq_sb[:, cd * ESL:(cd + 1) * ESL],
                wq_sb[:, cd * ESL:(cd + 1) * ESL],
                ca1_c[:, cd:cd + 1], None, ALU.mult)))

        @block.tensor
        def _(t, cd=cd):
            wait(t, ck["wp_scale"][cd])
            last = None
            for (lo, hi) in groups(ESL):
                last = t.matmul(p_gemv[32:33, lo:hi], ones_cb[:],
                                wq_sb[:, cd * ESL + lo:cd * ESL + hi],
                                start=(cd == 0), stop=(cd == DC - 1))
            ck["wp_sum"].append(t_inc(last))

    @block.vector
    def _(v):
        wait(v, ck["wp_sum"][-1])
        v.tensor_add(brow_f[0:1, :], brow_f[0:1, :], p_gemv[0:1, :])
        v.drain()
        ck["v_corr"] = v_inc(v.tensor_copy(out=corr4[32:33, :],
                                           in_=p_gemv[32:33, :]))

    @block.gpsimd
    def _(g):
        wait(g, ck["v_corr"])
        ck["d_corr"] = dma(g, "rwd", corr4[33:34, :], brow_f[0:1, :])

    p01.close()

    p34 = ExitStack()
    q_fm = sbuf("q_fm", [128, HPC * S], BF16, st=p34)
    k_fm = sbuf("k_fm", [128, HPC * S], BF16, st=p34)
    v_tok = sbuf("v_tok", [128, HPC * S], BF16, st=p34)
    nrm_q = sbuf("nrm_q", [1, HPC * NWIN], st=p34)
    nrm_k = sbuf("nrm_k", [1, HPC * NWIN], st=p34)
    p3p = ExitStack()
    p_qkv = [psum(f"p_qkv{i}", [128, WIN], st=p3p) for i in range(2)]
    p_rep = psum("p_rep", [128, WIN], st=p3p)
    p_nrm = psum("p_nrm", [1, WIN], st=p3p)

    def qkv_dst(o, w):
        hl = o % 2
        if o < 2:
            return q_fm[:, hl * S + w * WIN:hl * S + (w + 1) * WIN]
        if o < 4:
            return k_fm[:, hl * S + w * WIN:hl * S + (w + 1) * WIN]
        return v_win[w % 2][:, hl * WIN:(hl + 1) * WIN]

    ck["win_done"] = {}
    ck["win_evac"] = {}
    ck["rep_cp"] = {}
    ck["win_tr"] = {}
    last_evac_buf = {0: None, 1: None}
    last_vwin_tr = {0: None, 1: None}
    last_nrm_v = None

    for w in range(NWIN):
        if w >= 2:
            @block.gpsimd
            def _(g, w=w):
                wait(g, ck["win_done"][w - 2])
                ck["xwin"][w] = emit_xwin(g, w)

        @block.tensor
        def _(t, w=w):
            wait(t, ck["xwin"][w])
            if w == 0:
                wait(t, ck["v_corr"])
                wait(t, ck["v_rows"])
                wait(t, ck["d_corr"])
                wait(t, ck["d_rw2"])
            if w >= 1:
                wait(t, ck["rep_cp"][w - 1])
            ck[("rep_mm", w)] = t_inc(t.matmul(
                p_rep[:], ones2b[:],
                rrows[:, w * WIN:(w + 1) * WIN], start=True, stop=True))

        @block.vector
        def _(v, w=w):
            wait(v, ck[("rep_mm", w)])
            ck["rep_cp"][w] = v_inc(v.tensor_copy(out=rep_sb[:], in_=p_rep[:]))

        for o in range(6):
            buf = o % 2

            @block.tensor
            def _(t, w=w, o=o, buf=buf):
                if last_evac_buf[buf] is not None:
                    wait(t, last_evac_buf[buf])
                for cd in range(DC):
                    t.matmul(p_qkv[buf][:],
                             wq_sb[:, cd * ESL + o * 128:cd * ESL + (o + 1) * 128],
                             xw[w % 2][:, cd * WIN:(cd + 1) * WIN],
                             start=(cd == 0), stop=False)
                last = t.matmul(p_qkv[buf][:],
                                corr_l[:, o * 128:(o + 1) * 128],
                                crows[:, w * WIN:(w + 1) * WIN],
                                start=False, stop=True)
                ck[("qkv_t", w, o)] = t_inc(last)

            @block.vector
            def _(v, w=w, o=o, buf=buf):
                wait(v, ck[("qkv_t", w, o)])
                if o == 0:
                    wait(v, ck["rep_cp"][w])
                if o >= 4 and last_vwin_tr[w % 2] is not None:
                    wait(v, last_vwin_tr[w % 2])
                last_evac_buf[buf] = v_inc(v.tensor_mul(
                    qkv_dst(o, w), p_qkv[buf][:], rep_sb[:]))

        ck["win_done"][w] = ("TS", cnt["TS"])
        ck["win_evac"][w] = ("VS", cnt["VS"])

        for i, (tens, nrm) in enumerate(((q_fm, nrm_q), (k_fm, nrm_k))):
            for hl in range(HPC):
                @block.vector
                def _(v, w=w, i=i, hl=hl, tens=tens):
                    wait(v, ck["win_evac"][w])
                    ck[("nrm_sq", w, i, hl)] = v_inc(v.tensor_mul(
                        sq_scr[:],
                        tens[:, hl * S + w * WIN:hl * S + (w + 1) * WIN],
                        tens[:, hl * S + w * WIN:hl * S + (w + 1) * WIN]))

                @block.tensor
                def _(t, w=w, i=i, hl=hl):
                    wait(t, ck[("nrm_sq", w, i, hl)])
                    if last_nrm_v is not None:
                        wait(t, last_nrm_v)
                    ck[("nrm_t", w, i, hl)] = t_inc(t.matmul(
                        p_nrm[:], ones_cb[:], sq_scr[:],
                        start=True, stop=True))

                @block.vector
                def _(v, w=w, i=i, hl=hl, nrm=nrm):
                    wait(v, ck[("nrm_t", w, i, hl)])
                    ck[("nrm_v", w, i, hl)] = v_inc(v.reduce_max(
                        nrm[0:1, hl * NWIN + w:hl * NWIN + w + 1],
                        p_nrm[:], AXX))
                last_nrm_v = ck[("nrm_v", w, i, hl)]

        @block.sync
        def _(sync, w=w):
            wait(sync, ck["win_evac"][w])
            r = None
            for hl in range(HPC):
                for j in range(WIN // 128):
                    tok0 = w * WIN + j * 128
                    r = dma(sync, f"vt{w % 2}",
                            v_tok[:, hl * S + tok0:hl * S + tok0 + 128],
                            v_win[w % 2][:, hl * WIN + j * 128:hl * WIN + (j + 1) * 128],
                            transpose=True)
            last_vwin_tr[w % 2] = r
            ck["win_tr"][w] = r

    ck["vt_all"] = [("vt0", cnt.get("vt0", 0)), ("vt1", cnt.get("vt1", 0))]

    if dbg:
        @block.sync
        def _(sync):
            wait(sync, ck["d_corr"])
            wait(sync, ck["d_rw2"])
            dma(sync, "dbgq", dbg_mod, modc[:])
            dma(sync, "dbgq", dbg_st, stat8[:])
            dma(sync, "dbgq", dbg_rows, rows4[:])
            dma(sync, "dbgq", dbg_corr, corr4[:])
            wait(sync, ck["win_evac"][NWIN - 1])
            wait(sync, ck["win_tr"][NWIN - 1])
            dma(sync, "dbgq", dbg_qf, q_fm[:])
            dma(sync, "dbgq", dbg_kf, k_fm[:])
            ck["dbg1"] = dma(sync, "dbgq", dbg_vt, v_tok[:])

    p23.close()
    p3p.close()

    pW = ExitStack()
    amg = sbuf("amg", [128, DC * TOKC], BF16, st=pW, side="right")
    wo_raw = [sbuf(f"wo_raw{i}", [128, DC * 128], F32, st=pW, side="right")
              for i in range(2)]
    wo_buf = [sbuf(f"wo_buf{i}", [128, DC * 128], BF16, st=pW, side="right")
              for i in range(2)]

    p4s = ExitStack()
    ptile = [sbuf(f"pt{i}", [128, SW], BF16, st=p4s, side="right")
             for i in range(3)]
    pacc = sbuf("pacc", [128, SW], st=p4s, side="right")
    sum_row = sbuf("sum_row", [1, SW], st=p4s, side="right")
    lnrow = sbuf("lnrow", [1, SW], st=p4s, side="right")
    rinv_row = sbuf("rinv_row", [1, SW], BF16, st=p4s, side="right")
    astrip = [sbuf(f"astrip{i}", [128, SW], BF16, st=p4s, side="right")
              for i in range(2)]
    nrep_sb = sbuf("nrep_sb", [128, SW], st=p4s, side="right")
    p_sc = [psum(f"p_sc{i}", [128, SW], st=p4s) for i in range(3)]
    p_out = psum("p_out", [128, SW], st=p4s)

    @block.vector
    def _(v):
        ck["v_pacc0"] = v_inc(v.memset(pacc[:], 0.0))

    last_strip_evac = None
    last_pacc_reset = ck["v_pacc0"]
    ck["cc_strip"] = {}

    for s in range(NSTRIP):
        for hl in range(HPC):
            sidx = s * HPC + hl
            wq0 = (s * SW) // WIN
            wq1 = ((s + 1) * SW + WIN - 1) // WIN

            @block.vector
            def _(v, hl=hl, wq0=wq0, wq1=wq1, sidx=sidx):
                if sidx > 0:
                    wait(v, ck[("m0", sidx - 1)])
                v.reduce_max(m0s[0:1, 0:1],
                             nrm_q[0:1, hl * NWIN + wq0:hl * NWIN + wq1], AXX)
                v.reduce_max(m0s[0:1, 1:2],
                             nrm_k[0:1, hl * NWIN:hl * NWIN + wq1], AXX)
                v.drain()
                ck["m0a"] = v_inc(v.tensor_mul(m0s[0:1, 2:3], m0s[0:1, 0:1],
                                               m0s[0:1, 1:2]))

            @block.scalar
            def _(s_):
                wait(s_, ck["m0a"])
                s_.activation(m0s[0:1, 3:4], m0s[0:1, 2:3], ACT.Ln)
                s_.drain()
                ck["m0b"] = s_inc(s_.activation(m0s[0:1, 2:3], m0s[0:1, 3:4],
                                                ACT.Exp, scale=0.5))

            @block.vector
            def _(v):
                wait(v, ck["m0b"])
                ck["m0c"] = v_inc(v.tensor_scalar(
                    m0s[0:1, 0:1], m0s[0:1, 2:3], -ISQ, -1.0,
                    ALU.mult, ALU.add))

            @block.tensor
            def _(t, sidx=sidx):
                wait(t, ck["m0c"])
                ck["m0t"] = t_inc(t.matmul(p_sc[0][:, 0:1], ones_rf[:],
                                           m0s[0:1, 0:1],
                                           start=True, stop=True))

            @block.vector
            def _(v, sidx=sidx):
                wait(v, ck["m0t"])
                ck[("m0", sidx)] = v_inc(v.tensor_copy(
                    out=negm0[:, sidx:sidx + 1], in_=p_sc[0][:, 0:1]))

    for s in range(NSTRIP):
        for hl in range(HPC):
            sidx = s * HPC + hl
            nkt = (s + 1) * SKT

            exp_ck = {}
            mask_ck = {}
            pacc_ck = {}
            attnv_ck = {}

            def emit_attnv(kt):
                qlo = max(0, 128 * kt - s * SW)
                diag = 128 * kt >= s * SW
                buf = kt % 3

                @block.tensor
                def _(t, s=s, hl=hl, kt=kt, qlo=qlo, buf=buf, nkt=nkt,
                      diag=diag):
                    if diag:
                        wait(t, mask_ck[kt])
                    else:
                        wait(t, exp_ck[kt])
                    last = None
                    for b in range(qlo // 512, NB):
                        lo = max(qlo, 512 * b)
                        hi = min(SW, 512 * (b + 1))
                        ktl = min(nkt - 1, s * SKT + 4 * (b + 1) - 1)
                        last = t.matmul(
                            p_out[:, lo:hi],
                            v_tok[:, hl * S + kt * 128:hl * S + (kt + 1) * 128],
                            ptile[buf][:, lo:hi],
                            start=(kt == 0), stop=(kt == ktl))
                    attnv_ck[kt] = t_inc(last)

            for kt in range(nkt):
                qlo = max(0, 128 * kt - s * SW)
                diag = 128 * kt >= s * SW
                buf = kt % 3

                @block.tensor
                def _(t, s=s, hl=hl, kt=kt, qlo=qlo, buf=buf, sidx=sidx,
                      wq1=wq1):
                    if kt == 0:
                        wait(t, ck[("m0", NSTRIP * HPC - 1)])
                        wait(t, ck["vt_all"][0])
                        wait(t, ck["vt_all"][1])
                        if last_strip_evac is not None:
                            wait(t, last_strip_evac)
                    if kt >= 3:
                        wait(t, exp_ck[kt - 3])
                    last = None
                    for (lo, hi) in groups(SW, qlo):
                        last = t.matmul(
                            p_sc[buf][:, lo:hi],
                            k_fm[:, hl * S + kt * 128:hl * S + (kt + 1) * 128],
                            q_fm[:, hl * S + s * SW + lo:hl * S + s * SW + hi],
                            start=True, stop=True)
                    ck[("sc", sidx, kt)] = t_inc(last)

                @block.scalar
                def _(s_, kt=kt, qlo=qlo, buf=buf, sidx=sidx):
                    wait(s_, ck[("sc", sidx, kt)])
                    if kt >= 3:
                        wait(s_, attnv_ck[kt - 3])
                        wait(s_, pacc_ck[kt - 3])
                    exp_ck[kt] = s_inc(s_.activation(
                        ptile[buf][:, qlo:SW], p_sc[buf][:, qlo:SW], ACT.Exp,
                        bias=negm0[:, sidx:sidx + 1], scale=ISQ))

                @block.vector
                def _(v, kt=kt, qlo=qlo, buf=buf, diag=diag):
                    wait(v, exp_ck[kt])
                    if diag:
                        mask_ck[kt] = v_inc(v.tensor_mul(
                            ptile[buf][:, qlo:qlo + 128],
                            ptile[buf][:, qlo:qlo + 128], tri01[:]))
                        v.drain()
                    if kt == 0:
                        wait(v, last_pacc_reset)
                    pacc_ck[kt] = v_inc(v.tensor_add(
                        pacc[:, qlo:SW], pacc[:, qlo:SW],
                        ptile[buf][:, qlo:SW]))
                    v.drain()

                if kt >= 1:
                    emit_attnv(kt - 1)
            emit_attnv(nkt - 1)

            @block.tensor
            def _(t, nkt=nkt, sidx=sidx):
                wait(t, pacc_ck[nkt - 1])
                for back in range(1, min(3, nkt) + 1):
                    wait(t, exp_ck[nkt - back])
                last = None
                for (lo, hi) in groups(SW):
                    last = t.matmul(p_sc[0][0:1, lo:hi], ones_cf[:],
                                    pacc[:, lo:hi], start=True, stop=True)
                ck[("sums", sidx)] = t_inc(last)

            @block.vector
            def _(v, sidx=sidx):
                wait(v, ck[("sums", sidx)])
                ck[("sumr", sidx)] = v_inc(v.tensor_copy(
                    out=sum_row[:], in_=p_sc[0][0:1, 0:SW]))
                v.drain()
                ck[("paccr", sidx)] = v_inc(v.memset(pacc[:], 0.0))

            if dbg and sidx == 0:
                @block.sync
                def _(sync):
                    wait(sync, ck[("sumr", 0)])
                    dma(sync, "dbgq", dbg_pacc, pacc[:])

            last_pacc_reset = ck[("paccr", sidx)]

            @block.tensor
            def _(t, sidx=sidx, nkt=nkt):
                wait(t, ck[("sumr", sidx)])
                wait(t, attnv_ck[nkt - 1])
                last = None
                for (lo, hi) in groups(SW):
                    last = t.matmul(p_sc[0][:, lo:hi], ones_rf[:],
                                    sum_row[0:1, lo:hi],
                                    start=True, stop=True)
                ck[("repn", sidx)] = t_inc(last)

            @block.vector
            def _(v, sidx=sidx):
                wait(v, ck[("repn", sidx)])
                v.reciprocal(out=nrep_sb[:], in_=p_sc[0][:])
                v.drain()
                if sidx >= 2:
                    wait(v, ck[("agst", sidx - 2)])
                ck[("evac", sidx)] = v_inc(v.tensor_mul(
                    astrip[sidx % 2][:], p_out[:], nrep_sb[:]))

            last_strip_evac = ck[("evac", sidx)]

            @block.gpsimd
            def _(g, s=s, hl=hl, sidx=sidx):
                wait(g, ck[("evac", sidx)])
                ck[("agst", sidx)] = dma(
                    g, f"ag{sidx % 2}",
                    ago_in.ap()[sidx * 128:(sidx + 1) * 128, :],
                    astrip[sidx % 2][:])

        @block.gpsimd
        def _(g, s=s):
            for hl in range(HPC):
                wait(g, ck[("agst", s * HPC + hl)])
            ck["cc_strip"][s] = c_inc(g.collective_compute(
                "AllGather", ALU.bypass, replica_groups=RG,
                ins=[ago_in.ap()[s * HPC * 128:(s + 1) * HPC * 128, :].opt()],
                outs=[(ago_ab if s < NSTRIP // 2 else ago_cd).ap()
                      [(s % (NSTRIP // 2)) * D:
                       (s % (NSTRIP // 2) + 1) * D, :].opt()]))

        if s == NSTRIP // 2 - 1:
            @block.gpsimd
            def _(g):
                wait(g, ck["cc_strip"][NSTRIP // 2 - 1])
                wait(g, ck["v_idx"])
                r = None
                for rr in range(DC):
                    g.indirect_dma_start(
                        amg[:, rr * TOKC:(rr + 1) * TOKC], None,
                        ago_flat_a,
                        IndirectOffsetOnAxis(idx16[0][:, rr:rr + 1], 0),
                    ).then_inc(getsem("gth"), 16)
                    r = ("gth", bump("gth", 16))
                ck[("gather", 0)] = r

    p4s.close()
    p34.close()

    pH = ExitStack()
    h2 = sbuf("h2", [128, DC * TOK], BF16, st=pH)
    p67 = ExitStack()
    xr_roll = [sbuf(f"xr{i}", [128, TOKC], st=p67, side="right")
               for i in range(2)]
    x1_roll = [sbuf(f"x1r{i}", [128, TOKC], st=p67, side="right")
               for i in range(2)]
    x1c_bf = sbuf("x1c_bf", [128, DC * TOKC], BF16, st=p67, side="right")
    x1s_bf = sbuf("x1s_bf", [128, DC * TOKC], BF16, st=p67, side="right")
    ln2row = sbuf("ln2row", [1, 2 * TOKC], st=p67, side="right")
    ln2m = sbuf("ln2m", [1, TOKC], st=p67, side="right")
    ln2v = sbuf("ln2v", [1, TOKC], st=p67, side="right")
    ln2sd = sbuf("ln2sd", [1, TOKC], st=p67, side="right")
    ln2r = sbuf("ln2r", [1, TOKC], st=p67, side="right")
    apply_scr = sbuf("apply_scr", [128, TOKC], st=p67, side="right")
    p_op = [psum(f"p_op{i}", [128, TOKC], st=p67) for i in range(2)]
    SQO2 = max(TOKC, 512)
    p_st2 = psum("p_st2", [1, SQO2 + TOKC], st=p67)
    p_r2m = psum("p_r2m", [128, TOKC], st=p67)
    p_r2r = psum("p_r2r", [128, TOKC], st=p67)

    wov = wout_i.rearrange("p (cd e) -> p cd e", e=D)
    h2_ck = []
    prv = {k: {0: None, 1: None} for k in
           ("woc", "opt", "opx1", "opev", "x1sp")}

    for g_ in range(2):
        if g_ == 1:
            @block.gpsimd
            def _(g):
                wait(g, ck["cc_strip"][NSTRIP - 1])
                wait(g, ck[("opt", 0, DC - 1)])
                r = None
                for rr in range(DC):
                    g.indirect_dma_start(
                        amg[:, rr * TOKC:(rr + 1) * TOKC], None,
                        ago_flat_b,
                        IndirectOffsetOnAxis(idx16[1][:, rr:rr + 1], 0),
                    ).then_inc(getsem("gth"), 16)
                    r = ("gth", bump("gth", 16))
                ck[("gather", 1)] = r

        for o in range(DC):
            bf = o % 2

            @block.sync
            def _(sync, g_=g_, o=o, bf=bf):
                if prv["woc"][bf] is not None:
                    wait(sync, prv["woc"][bf])
                else:
                    wait(sync, ck["win_evac"][NWIN - 1])
                ck[("ld_wor", g_, o)] = dma(
                    sync, f"wor{bf}",
                    wo_raw[bf][:].rearrange("p (cd e) -> p cd e", e=128),
                    wov[:, :, o * 128:(o + 1) * 128])
                if prv["opx1"][bf] is not None:
                    wait(sync, prv["opx1"][bf])
                else:
                    wait(sync, ck[("agst", NSTRIP * HPC - 2)])
                    wait(sync, ck[("agst", NSTRIP * HPC - 1)])
                ck[("ld_xr", g_, o)] = dma(
                    sync, f"xr{bf}", xr_roll[bf][:],
                    xown_i[:, o * TOK + g_ * TOKC:o * TOK + g_ * TOKC + TOKC])

            @block.vector
            def _(v, g_=g_, o=o, bf=bf):
                wait(v, ck[("ld_wor", g_, o)])
                if prv["opt"][bf] is not None:
                    wait(v, prv["opt"][bf])
                ck[("woc", g_, o)] = v_inc(v.tensor_copy(
                    out=wo_buf[bf][:], in_=wo_raw[bf][:]))
            prv["woc"][bf] = ck[("woc", g_, o)]

            @block.tensor
            def _(t, g_=g_, o=o, bf=bf):
                wait(t, ck[("woc", g_, o)])
                if o == 0:
                    wait(t, ck[("gather", g_)])
                if prv["opx1"][bf] is not None:
                    wait(t, prv["opx1"][bf])
                last = None
                for cd in range(DC):
                    last = t.matmul(p_op[bf][:],
                                    wo_buf[bf][:, cd * 128:(cd + 1) * 128],
                                    amg[:, cd * TOKC:(cd + 1) * TOKC],
                                    start=(cd == 0), stop=(cd == DC - 1))
                ck[("opt", g_, o)] = t_inc(last)
            prv["opt"][bf] = ck[("opt", g_, o)]

            @block.vector
            def _(v, g_=g_, o=o, bf=bf):
                wait(v, ck[("opt", g_, o)])
                wait(v, ck[("ld_xr", g_, o)])
                if g_ == 0 and o == 0:
                    wait(v, ck["v_mod"])
                if prv["x1sp"][bf] is not None:
                    wait(v, prv["x1sp"][bf])
                v.tensor_scalar(x1_roll[bf][:], p_op[bf][:],
                                ga_c[:, o:o + 1], gabout_c[:, o:o + 1],
                                ALU.mult, ALU.add)
                v.drain()
                ck[("opx1", g_, o)] = v_inc(v.tensor_add(
                    x1_roll[bf][:], x1_roll[bf][:], xr_roll[bf][:]))
                v.drain()
                ck[("opev", g_, o)] = v_inc(v.tensor_copy(
                    out=x1c_bf[:, o * TOKC:(o + 1) * TOKC],
                    in_=x1_roll[bf][:]))
            prv["opx1"][bf] = ck[("opx1", g_, o)]
            prv["opev"][bf] = ck[("opev", g_, o)]

            @block.sync
            def _(sync, g_=g_, o=o, bf=bf):
                wait(sync, ck[("opev", g_, o)])
                ck[("x1sp", g_, o)] = dma(
                    sync, f"x1s{bf}",
                    x1_sp.ap()[:, o * TOK + g_ * TOKC:o * TOK + g_ * TOKC + TOKC],
                    x1_roll[bf][:])
            prv["x1sp"][bf] = ck[("x1sp", g_, o)]

        if dbg and g_ == 0:
            @block.sync
            def _(sync):
                wait(sync, ck[("m0", NSTRIP * HPC - 1)])
                dma(sync, "dbgq", dbg_m0, negm0[:])
                dma(sync, "dbgq", dbg_nrm[0:1, :], nrm_q[:])
                dma(sync, "dbgq", dbg_nrm[1:2, :], nrm_k[:])
                wait(sync, ck["cc_strip"][NSTRIP - 1])
                dma(sync, "dbgq", dbg_agi, ago_in.ap())
                dma(sync, "dbgq", dbg_agab, ago_ab.ap())
                wait(sync, ck[("opt", 0, DC - 1)])
                ck["dbg2"] = dma(sync, "dbgq", dbg_amg, amg[:])

        @block.vector
        def _(v, g_=g_):
            wait(v, ck[("opev", g_, DC - 1)])
            v.drain()
            ck[("l2sq", g_)] = v_inc(v.tensor_mul(x1s_bf[:], x1c_bf[:],
                                                  x1c_bf[:]))

        @block.tensor
        def _(t, g_=g_):
            wait(t, ck[("l2sq", g_)])
            last = None
            for cd in range(DC):
                for (lo, hi) in groups(TOKC):
                    last = t.matmul(p_st2[0:1, lo:hi], ones_cb[:],
                                    x1c_bf[:, cd * TOKC + lo:cd * TOKC + hi],
                                    start=(cd == 0), stop=(cd == DC - 1),
                                    skip_group_check=True)
            for cd in range(DC):
                for (lo, hi) in groups(TOKC):
                    last = t.matmul(p_st2[0:1, SQO2 + lo:SQO2 + hi], ones_cb[:],
                                    x1s_bf[:, cd * TOKC + lo:cd * TOKC + hi],
                                    start=(cd == 0), stop=(cd == DC - 1),
                                    skip_group_check=True)
            ck[("l2mm", g_)] = t_inc(last)

        @block.vector
        def _(v, g_=g_):
            wait(v, ck[("l2mm", g_)])
            v.tensor_scalar_mul(ln2m[:], p_st2[0:1, 0:TOKC], 1.0 / D)
            v.tensor_scalar_mul(ln2v[:], p_st2[0:1, SQO2:SQO2 + TOKC], 1.0 / D)
            v.drain()
            v.tensor_mul(ln2row[0:1, 0:TOKC], ln2m[:], ln2m[:])
            v.drain()
            v.tensor_sub(ln2v[:], ln2v[:], ln2row[0:1, 0:TOKC])
            v.drain()
            ck[("l2var", g_)] = v_inc(v.tensor_scalar_add(ln2v[:], ln2v[:],
                                                          LN_EPS))

        @block.scalar
        def _(s_, g_=g_):
            wait(s_, ck[("l2var", g_)])
            s_.activation(ln2sd[:], ln2v[:], ACT.Sqrt)
            s_.drain()
            s_.activation(ln2row[0:1, TOKC:2 * TOKC], ln2sd[:], ACT.Ln)
            s_.drain()
            ck[("l2r", g_)] = s_inc(s_.activation(
                ln2r[:], ln2row[0:1, TOKC:2 * TOKC], ACT.Exp, scale=-1.0))

        @block.tensor
        def _(t, g_=g_):
            wait(t, ck[("l2r", g_)])
            if g_ == 1:
                wait(t, ck[("l2apply", 0)])
            last = None
            for (lo, hi) in groups(TOKC):
                t.matmul(p_r2m[:, lo:hi], ones_rf[:], ln2m[0:1, lo:hi],
                         start=True, stop=True)
                last = t.matmul(p_r2r[:, lo:hi], ones_rf[:], ln2r[0:1, lo:hi],
                                start=True, stop=True)
            ck[("l2rep", g_)] = t_inc(last)

        @block.vector
        def _(v, g_=g_):
            wait(v, ck[("l2rep", g_)])
            r = None
            for cd in range(DC):
                v.tensor_sub(apply_scr[:], x1c_bf[:, cd * TOKC:(cd + 1) * TOKC],
                             p_r2m[:])
                v.drain()
                v.tensor_mul(apply_scr[:], apply_scr[:], p_r2r[:])
                v.drain()
                r = v_inc(v.tensor_scalar(
                    h2[:, cd * TOK + g_ * TOKC:cd * TOK + g_ * TOKC + TOKC],
                    apply_scr[:], cb1_c[:, cd:cd + 1], sb_c[:, cd:cd + 1],
                    ALU.mult, ALU.add))
                v.drain()
            ck[("l2apply", g_)] = r

        h2_ck.append(ck[("l2apply", g_)])

    p67.close()
    pW.close()

    pG = ExitStack()
    gu = sbuf("gu", [128, HC * TOK], BF16, st=pG, side="right")
    cs2_rows = sbuf("cs2_rows", [2, TOK], BF16, st=pG, side="right")
    csscr = sbuf("csscr", [1, TOK], st=pG, side="right")
    amr_scr = sbuf("amr_scr", [128, 1], st=pG, side="right")
    up_scr = sbuf("up_scr", [128, TOK], st=pG, side="right")
    p8 = ExitStack()
    gate_b = sbuf("gate_b", [128, TOK], BF16, st=p8, side="right")
    wgi_b = [sbuf(f"wgi{i}", [128, DC * 128], I32, st=p8, side="right")
             for i in range(2)]
    wui_b = [sbuf(f"wui{i}", [128, DC * 128], I32, st=p8, side="right")
             for i in range(2)]
    wgt_b = [sbuf(f"wgt{i}", [128, DC * 128], BF16, st=p8, side="right")
             for i in range(2)]
    wut_b = [sbuf(f"wut{i}", [128, DC * 128], BF16, st=p8, side="right")
             for i in range(2)]
    cs_rows = sbuf("cs_rows", [2, TOK], BF16, st=p8, side="right")
    p_g = [psum(f"p_g{i}", [128, TOK], st=p8) for i in range(2)]
    p_u = [psum(f"p_u{i}", [128, TOK], st=p8) for i in range(2)]

    if dbg:
        @block.sync
        def _(sync):
            wait(sync, h2_ck[1])
            ck["dbg3"] = dma(sync, "dbgq", dbg_h2, h2[:])

    ck["x1s_fin"] = {0: ("x1s0", cnt.get("x1s0", 0)),
                     1: ("x1s1", cnt.get("x1s1", 0))}
    p8d = ExitStack()

    @block.tensor
    def _(t):
        wait(t, h2_ck[1])
        last = None
        for cd in range(DC):
            for (lo, hi) in groups(TOK):
                last = t.matmul(p_g[0][0:1, lo:hi], ones_cb[:],
                                h2[:, cd * TOK + lo:cd * TOK + hi],
                                start=(cd == 0), stop=(cd == DC - 1))
        ck["csh_t"] = t_inc(last)

    @block.vector
    def _(v):
        wait(v, ck["csh_t"])
        v.tensor_copy(out=cs_rows[0:1, :], in_=p_g[0][0:1, 0:TOK])
        v.drain()
        ck["csh_v"] = v_inc(v.tensor_sub(csscr[:], p_g[0][0:1, 0:TOK],
                                         cs_rows[0:1, :]))

    @block.gpsimd
    def _(g):
        wait(g, ck["csh_v"])
        ck["d_csh"] = dma(g, "csd", cs_rows[1:2, :], csscr[:])

    for hh in range(HC):
        @block.scalar
        def _(s_, hh=hh):
            if hh >= 2:
                wait(s_, ck[("wgc", hh - 2)])
            else:
                wait(s_, h2_ck[1])
                wait(s_, ck["x1s_fin"][0])
                wait(s_, ck["x1s_fin"][1])
            ck[("ldwg", hh)] = dma(
                s_, f"wg{hh % 2}", wgi_b[hh % 2][:],
                wg_i[:, hh * DC * 128:(hh + 1) * DC * 128])

        @block.sync
        def _(sync, hh=hh):
            if hh >= 2:
                wait(sync, ck[("wuc", hh - 2)])
            else:
                wait(sync, h2_ck[1])
                wait(sync, ck["x1s_fin"][0])
                wait(sync, ck["x1s_fin"][1])
            ck[("ldwu", hh)] = dma(
                sync, f"wu{hh % 2}", wui_b[hh % 2][:],
                wu_i[:, hh * DC * 128:(hh + 1) * DC * 128])

        @block.vector
        def _(v, hh=hh):
            wait(v, ck[("ldwg", hh)])
            if hh >= 2:
                wait(v, ck[("gmm", hh - 2)])
            ck[("wgc", hh)] = v_inc(v.tensor_copy(out=wgt_b[hh % 2][:],
                                                  in_=wgi_b[hh % 2][:]))
            wait(v, ck[("ldwu", hh)])
            if hh >= 2:
                wait(v, ck[("umm", hh - 2)])
            ck[("wuc", hh)] = v_inc(v.tensor_copy(out=wut_b[hh % 2][:],
                                                  in_=wui_b[hh % 2][:]))

        @block.tensor
        def _(t, hh=hh):
            wait(t, ck[("wgc", hh)])
            if hh == 0:
                wait(t, ck["d_csh"])
            if hh >= 2:
                wait(t, ck[("silu", hh - 2)])
            last = None
            for cd in range(DC):
                for (lo, hi) in groups(TOK):
                    last = t.matmul(
                        p_g[hh % 2][:, lo:hi],
                        wgt_b[hh % 2][:, cd * 128:(cd + 1) * 128],
                        h2[:, cd * TOK + lo:cd * TOK + hi],
                        start=(cd == 0), stop=False)
            for (lo, hi) in groups(TOK):
                last = t.matmul(p_g[hh % 2][:, lo:hi], neg128b[:],
                                cs_rows[:, lo:hi], start=False, stop=True)
            ck[("gmm", hh)] = t_inc(last)
            wait(t, ck[("wuc", hh)])
            if hh >= 2:
                wait(t, ck[("gu", hh - 2)])
            for cd in range(DC):
                for (lo, hi) in groups(TOK):
                    last = t.matmul(
                        p_u[hh % 2][:, lo:hi],
                        wut_b[hh % 2][:, cd * 128:(cd + 1) * 128],
                        h2[:, cd * TOK + lo:cd * TOK + hi],
                        start=(cd == 0), stop=False)
            for (lo, hi) in groups(TOK):
                last = t.matmul(p_u[hh % 2][:, lo:hi], neg128b[:],
                                cs_rows[:, lo:hi], start=False, stop=True)
            ck[("umm", hh)] = t_inc(last)

        @block.scalar
        def _(s_, hh=hh):
            wait(s_, ck[("gmm", hh)])
            if hh >= 2:
                wait(s_, ck[("gu", hh - 2)])
            ck[("silu", hh)] = s_inc(s_.activation(
                gate_b[:], p_g[hh % 2][:], ACT.Silu,
                scale=sg_c[:, hh:hh + 1]))

        @block.vector
        def _(v, hh=hh):
            wait(v, ck[("silu", hh)])
            wait(v, ck[("umm", hh)])
            v.tensor_scalar(up_scr[:], p_u[hh % 2][:], su_c[:, hh:hh + 1],
                            None, ALU.mult)
            v.drain()
            ck[("gu", hh)] = v_inc(v.tensor_mul(
                gu[:, hh * TOK:(hh + 1) * TOK], up_scr[:], gate_b[:]))

    @block.tensor
    def _(t):
        wait(t, ck[("gu", HC - 1)])
        wait(t, ck[("gu", HC - 2)])
        last = None
        for hh in range(HC):
            for (lo, hi) in groups(TOK):
                last = t.matmul(p_u[0][0:1, lo:hi], ones_cb[:],
                                gu[:, hh * TOK + lo:hh * TOK + hi],
                                start=(hh == 0), stop=(hh == HC - 1))
        ck["csg_t"] = t_inc(last)

    @block.vector
    def _(v):
        wait(v, ck["csg_t"])
        v.tensor_copy(out=cs2_rows[0:1, :], in_=p_u[0][0:1, 0:TOK])
        v.drain()
        ck["csg_v"] = v_inc(v.tensor_sub(csscr[:], p_u[0][0:1, 0:TOK],
                                         cs2_rows[0:1, :]))

    @block.gpsimd
    def _(g):
        wait(g, ck["csg_v"])
        ck["d_csg"] = dma(g, "csd", cs2_rows[1:2, :], csscr[:])

    p8.close()
    pH.close()
    wdi_b = [sbuf(f"wdi{i}", [128, HC * 128], I32, st=p8d, side="right")
             for i in range(2)]
    wdt_b = [sbuf(f"wdt{i}", [128, HC * 128], BF16, st=p8d, side="right")
             for i in range(2)]
    x1l = [sbuf(f"x1l{i}", [128, TOK], st=p8d, side="right")
           for i in range(2)]
    x2r = [sbuf(f"x2r{i}", [128, TOK], st=p8d, side="right")
           for i in range(2)]

    for o in range(DC):
        @block.sync
        def _(sync, o=o):
            if o >= 2:
                wait(sync, ck[("wdc", o - 2)])
            else:
                wait(sync, ck[("gu", HC - 1)])
                wait(sync, ck["x1s_fin"][0])
                wait(sync, ck["x1s_fin"][1])
            ck[("ldwd", o)] = dma(
                sync, f"wdl{o % 2}", wdi_b[o % 2][:],
                wd_i[:, o * HC * 128:(o + 1) * HC * 128])
            wait(sync, ck["x1s_fin"][0])
            wait(sync, ck["x1s_fin"][1])
            if o >= 2:
                wait(sync, ck[("dnev", o - 2)])
            ck[("ldx1", o)] = dma(sync, f"lx{o % 2}", x1l[o % 2][:],
                                  x1_sp.ap()[:, o * TOK:(o + 1) * TOK])

        @block.vector
        def _(v, o=o):
            wait(v, ck[("ldwd", o)])
            if o >= 2:
                wait(v, ck[("dmm", o - 2)])
            ck[("wdc", o)] = v_inc(v.tensor_copy(out=wdt_b[o % 2][:],
                                                 in_=wdi_b[o % 2][:]))

        @block.tensor
        def _(t, o=o):
            wait(t, ck[("wdc", o)])
            if o == 0:
                wait(t, ck["d_csg"])
            if o >= 2:
                wait(t, ck[("dnev", o - 2)])
            last = None
            for hh in range(HC):
                for (lo, hi) in groups(TOK):
                    last = t.matmul(
                        p_g[o % 2][:, lo:hi],
                        wdt_b[o % 2][:, hh * 128:(hh + 1) * 128],
                        gu[:, hh * TOK + lo:hh * TOK + hi],
                        start=(hh == 0), stop=False)
            for (lo, hi) in groups(TOK):
                last = t.matmul(p_g[o % 2][:, lo:hi], neg128b[:],
                                cs2_rows[:, lo:hi], start=False, stop=True)
            ck[("dmm", o)] = t_inc(last)

        @block.vector
        def _(v, o=o):
            wait(v, ck[("dmm", o)])
            wait(v, ck[("ldx1", o)])
            if o >= 2:
                wait(v, ck[("ost", o - 2)])
            v.tensor_scalar(x2r[o % 2][:], p_g[o % 2][:],
                            gbsd_c[:, o:o + 1], None, ALU.mult)
            v.drain()
            ck[("dnev", o)] = v_inc(v.tensor_add(
                x2r[o % 2][:], x2r[o % 2][:], x1l[o % 2][:]))

        @block.sync
        def _(sync, o=o):
            wait(sync, ck[("dnev", o)])
            ck[("ost", o)] = dma(sync, f"st{o % 2}",
                                 out_t[:, o * TOK:(o + 1) * TOK],
                                 x2r[o % 2][:])

    p8d.close()
    pG.close()

    @block.sync
    def _(sync):
        sync.wait_ge(sems["st0"], cnt["st0"])
        sync.wait_ge(sems["st1"], cnt["st1"])
        if dbg:
            sync.wait_ge(sems["dbgq"], cnt["dbgq"])

    es.close()
    return nc


S_FULL = 8192
HID_FULL = 5632


def _chunk_major(a):
    Dn, N = a.shape
    return np.ascontiguousarray(
        a.reshape(Dn // 128, 128, N).transpose(1, 0, 2).reshape(128, -1))


def prep_inputs(x, w_qkv, b_qkv, w_out, b_out, wq_g, s_g, wq_u, s_u,
                wq_d, s_d, cvec, wmod, S, HID):
    import ml_dtypes
    HC = HID // 128
    TOKC = S // 16
    xf = np.ascontiguousarray(x[0].T.astype(np.float32))
    xfull = _chunk_major(xf)
    woutT = _chunk_major(np.ascontiguousarray(w_out.T.astype(np.float32)))
    bout_col = np.ascontiguousarray(
        b_out.astype(np.float32).reshape(DC, 128).T)
    wg = np.ascontiguousarray(
        wq_g.reshape(HC, 128, DC, 128).transpose(3, 0, 2, 1)
        .reshape(128, -1).astype(np.int32))
    wu = np.ascontiguousarray(
        wq_u.reshape(HC, 128, DC, 128).transpose(3, 0, 2, 1)
        .reshape(128, -1).astype(np.int32))
    wd = np.ascontiguousarray(
        wq_d.reshape(DC, 128, HC, 128).transpose(3, 0, 2, 1)
        .reshape(128, -1).astype(np.int32))
    sg_col = np.ascontiguousarray(s_g.astype(np.float32).reshape(HC, 128).T)
    su_col = np.ascontiguousarray(s_u.astype(np.float32).reshape(HC, 128).T)
    sd_col = np.ascontiguousarray(s_d.astype(np.float32).reshape(DC, 128).T)
    c_col = np.ascontiguousarray(cvec[0].astype(np.float32).reshape(DC, 128).T)
    tri = (np.arange(128)[:, None] <= np.arange(128)[None, :]
           ).astype(ml_dtypes.bfloat16)
    SW = S // 8
    pr16 = np.empty((128, DC), np.float32)
    for rr in range(DC):
        pr16[:, rr] = (128 * rr + np.arange(128)) * SW

    ins = []
    for c in range(NCORES):
        Ka, Kb = c, 15 - c
        xo = np.empty((128, DC * 2 * TOKC), np.float32)
        for cd in range(DC):
            blk = np.empty((128, 2 * TOKC), np.float32)
            blk[:, :TOKC] = xfull[:, cd * S + Ka * TOKC:cd * S + (Ka + 1) * TOKC]
            blk[:, TOKC:] = xfull[:, cd * S + Kb * TOKC:cd * S + (Kb + 1) * TOKC]
            xo[:, cd * 2 * TOKC:(cd + 1) * 2 * TOKC] = blk
        wsl = np.concatenate([
            w_qkv[256 * c:256 * (c + 1)],
            w_qkv[D + 256 * c:D + 256 * (c + 1)],
            w_qkv[2 * D + 256 * c:2 * D + 256 * (c + 1)]], axis=0)
        wqkv = _chunk_major(np.ascontiguousarray(wsl.T.astype(np.float32)))
        bql = np.concatenate([
            b_qkv[256 * c:256 * (c + 1)],
            b_qkv[D + 256 * c:D + 256 * (c + 1)],
            b_qkv[2 * D + 256 * c:2 * D + 256 * (c + 1)]])[None, :]
        wmc = np.empty((D, 12, 128), np.float32)
        for j in range(12):
            vv, ii = j // 2, j % 2
            col0 = 2048 * vv + 128 * (2 * c + ii)
            wmc[:, j, :] = wmod[:, col0:col0 + 128]
        wm = np.ascontiguousarray(
            wmc.reshape(DC, 128, 12 * 128).transpose(1, 0, 2).reshape(128, -1))
        gb = np.empty((128, 2), np.float32)
        CPS = SW // TOKC
        NSTRIP = S // SW
        for gi, K in enumerate((Ka, Kb)):
            srel = (K // CPS) - (NSTRIP // 2 * gi)
            gb[:, gi] = srel * D * SW + (K % CPS) * TOKC
        ins.append(dict(
            xfull=xfull, xown=np.ascontiguousarray(xo), wm=wm,
            c_col=c_col, wqkv=wqkv,
            bqkv_row=np.ascontiguousarray(bql.astype(np.float32)),
            wout=woutT, bout_col=bout_col, wg=wg, wu=wu, wd=wd,
            sg_col=sg_col, su_col=su_col, sd_col=sd_col, tri=tri,
            pramp16=pr16, gbase=gb))
    return ins


def assemble_out(results, S):
    TOKC = S // 16
    out = np.empty((1, S, D), np.float32)
    for c in range(NCORES):
        r = results[c]["out_t"].reshape(128, DC, 2 * TOKC)
        for gi, K in enumerate((c, 15 - c)):
            blk = r[:, :, gi * TOKC:(gi + 1) * TOKC]
            out[0, K * TOKC:(K + 1) * TOKC, :] = (
                blk.transpose(2, 1, 0).reshape(TOKC, D))
    return out


def kernel(L0_attn_x, L0_attn_w_qkv, L0_attn_b_qkv, L0_attn_w_out,
           L0_attn_b_out, L0_ffn_wq_gate, L0_ffn_s_gate, L0_ffn_wq_up,
           L0_ffn_s_up, L0_ffn_wq_down, L0_ffn_s_down, L0_c, L0_wmod):
    S = int(np.asarray(L0_attn_x).shape[1])
    HID = int(np.asarray(L0_ffn_wq_gate).shape[0])
    nc = build_nc(S, HID)
    ins = prep_inputs(
        np.asarray(L0_attn_x), np.asarray(L0_attn_w_qkv),
        np.asarray(L0_attn_b_qkv), np.asarray(L0_attn_w_out),
        np.asarray(L0_attn_b_out), np.asarray(L0_ffn_wq_gate),
        np.asarray(L0_ffn_s_gate), np.asarray(L0_ffn_wq_up),
        np.asarray(L0_ffn_s_up), np.asarray(L0_ffn_wq_down),
        np.asarray(L0_ffn_s_down), np.asarray(L0_c), np.asarray(L0_wmod),
        S, HID)
    res = run_bass_kernel_spmd(nc, ins, core_ids=list(range(NCORES)))
    return assemble_out(res.results, S)


# version 13 (final)
# speedup vs baseline: 1.2723x; 1.0175x over previous
import math
import numpy as np
from contextlib import ExitStack

import concourse.bass as bass
import concourse.mybir as mybir
from concourse.bass import IndirectOffsetOnAxis
from concourse.bass_utils import run_bass_kernel_spmd

F32 = mybir.dt.float32
BF16 = mybir.dt.bfloat16
I32 = mybir.dt.int32
ALU = mybir.AluOpType
ACT = mybir.ActivationFunctionType
AXX = mybir.AxisListType.X

NCORES = 8
D = 2048
DC = D // 128
HPC = 2
HD = 128
ESL = 3 * HPC * HD
ISQ = 1.0 / math.sqrt(HD)
LN_EPS = 1e-5


def groups(total, start=0, maxn=512):
    out = []
    lo = start
    while lo < total:
        hi = min(total, (lo // maxn + 1) * maxn)
        out.append((lo, hi))
        lo = hi
    return out


def build_nc(S, HID, dbg=False):
    SW = S // 8
    NSTRIP = S // SW
    SKT = SW // 128
    NB = max(1, SW // 512)
    CPS = SW // (S // 16)
    TOKC = S // 16
    TOK = 2 * TOKC
    HC = HID // 128
    WIN = 512
    NWIN = S // WIN

    nc = bass.Bass("TRN2", target_bir_lowering=False, debug=False)
    es = ExitStack()
    cnt = {}
    sems = {}

    def getsem(name):
        if name not in sems:
            sems[name] = es.enter_context(nc.semaphore(name))
        return sems[name]

    for sn in ("TS", "VS", "SS", "CC"):
        getsem(sn)

    def bump(name, by):
        cnt[name] = cnt.get(name, 0) + by
        return cnt[name]

    def wait(eng, ckv):
        eng.wait_ge(sems[ckv[0]], ckv[1])

    def sbuf(name, shape, dt=F32, st=None, side=None):
        return (st or es).enter_context(
            nc.sbuf_tensor(name, shape, dt, side=side))

    def psum(name, shape, st=None):
        return (st or es).enter_context(nc.psum_tensor(name, shape, F32))

    DI = lambda n, s, d: nc.dram_tensor(n, s, d, kind="ExternalInput").ap()
    xfull = DI("xfull", [128, DC * S], F32)
    xown_i = DI("xown", [128, DC * TOK], F32)
    wm_i = DI("wm", [128, DC * 12 * 128], F32)
    c_i = DI("c_col", [128, DC], F32)
    wqkv_i = DI("wqkv", [128, DC * ESL], F32)
    bqkv_i = DI("bqkv_row", [1, ESL], F32)
    wout_i = DI("wout", [128, DC * D], F32)
    bout_i = DI("bout_col", [128, DC], F32)
    wg_i = DI("wg", [128, HC * DC * 128], I32)
    wu_i = DI("wu", [128, HC * DC * 128], I32)
    wd_i = DI("wd", [128, DC * HC * 128], I32)
    sg_i = DI("sg_col", [128, HC], F32)
    su_i = DI("su_col", [128, HC], F32)
    sd_i = DI("sd_col", [128, DC], F32)
    tri_i = DI("tri", [128, 128], BF16)
    pr16_i = DI("pramp16", [128, DC], F32)
    gbase_i = DI("gbase", [128, 2], F32)
    out_t = nc.dram_tensor("out_t", [128, DC * TOK], F32,
                           kind="ExternalOutput").ap()
    if dbg:
        dbg_mod = nc.dram_tensor("dbg_mod", [128, NCORES * 12], F32,
                                 kind="ExternalOutput").ap()
        dbg_st = nc.dram_tensor("dbg_st", [NCORES, 2 * TOK], F32,
                                kind="ExternalOutput").ap()
        dbg_rows = nc.dram_tensor("dbg_rows", [34, S], BF16,
                                  kind="ExternalOutput").ap()
        dbg_corr = nc.dram_tensor("dbg_corr", [34, ESL], BF16,
                                  kind="ExternalOutput").ap()
        dbg_qf = nc.dram_tensor("dbg_qf", [128, HPC * S], BF16,
                                kind="ExternalOutput").ap()
        dbg_kf = nc.dram_tensor("dbg_kf", [128, HPC * S], BF16,
                                kind="ExternalOutput").ap()
        dbg_vt = nc.dram_tensor("dbg_vt", [128, HPC * S], BF16,
                                kind="ExternalOutput").ap()
        dbg_amg = nc.dram_tensor("dbg_amg", [128, DC * TOKC], BF16,
                                 kind="ExternalOutput").ap()
        dbg_h2 = nc.dram_tensor("dbg_h2", [128, DC * TOK], BF16,
                                kind="ExternalOutput").ap()
        dbg_agi = nc.dram_tensor("dbg_agi", [NSTRIP * HPC * HD, SW], BF16,
                                 kind="ExternalOutput").ap()
        dbg_m0 = nc.dram_tensor("dbg_m0", [128, HPC * 4], F32,
                                kind="ExternalOutput").ap()
        dbg_nrm = nc.dram_tensor("dbg_nrm", [2, HPC * NWIN], F32,
                                 kind="ExternalOutput").ap()
        dbg_pacc = nc.dram_tensor("dbg_pacc", [128, SW], F32,
                                  kind="ExternalOutput").ap()
        dbg_rinv = nc.dram_tensor("dbg_rinv", [1, SW], BF16,
                                  kind="ExternalOutput").ap()
        dbg_agab = nc.dram_tensor("dbg_agab", [NSTRIP // 2 * D, SW], BF16,
                                  kind="ExternalOutput").ap()

    agma_i = nc.dram_tensor("agma_i", [128, 4], F32)
    agma_o = nc.dram_tensor("agma_o", [NCORES * 128, 4], F32)
    agmb_i = nc.dram_tensor("agmb_i", [128, 8], F32)
    agmb_o = nc.dram_tensor("agmb_o", [NCORES * 128, 8], F32)
    agst_i = nc.dram_tensor("agst_i", [1, 2 * TOK], F32)
    agst_o = nc.dram_tensor("agst_o", [NCORES, 2 * TOK], F32)
    ago_in = nc.dram_tensor("ago_in", [NSTRIP * HPC * HD, SW], BF16)
    ago_ab = nc.dram_tensor("ago_ab", [NSTRIP // 2 * D, SW], BF16,
                            addr_space="Shared")
    ago_cd = nc.dram_tensor("ago_cd", [NSTRIP // 2 * D, SW], BF16,
                            addr_space="Shared")
    ago_flat_a = ago_ab.ap().rearrange("a b -> (a b)")[:, None]
    ago_flat_b = ago_cd.ap().rearrange("a b -> (a b)")[:, None]
    x1_sp = nc.dram_tensor("x1_sp", [128, DC * TOK], F32)

    ones_cf = sbuf("ones_cf", [128, 1], F32)
    ones_cb = sbuf("ones_cb", [128, 1], BF16)
    ones2b = sbuf("ones2b", [2, 128], BF16)
    neg128b = sbuf("neg128b", [2, 128], BF16)
    ones_rf = sbuf("ones_rf", [1, 128], F32)
    ones_rb = sbuf("ones_rb", [1, 128], BF16)
    tri01 = sbuf("tri01", [128, 128], BF16)
    pr16 = sbuf("pr16", [128, DC])
    gbase = sbuf("gbase_s", [128, 2])
    c_sb = sbuf("c_sb", [128, DC])
    csilu = sbuf("csilu", [128, DC])
    sa_c = sbuf("sa_c", [128, DC])
    sab_c = sbuf("sab_c", [128, DC], BF16)
    ca1_c = sbuf("ca1_c", [128, DC])
    ga_c = sbuf("ga_c", [128, DC])
    sb_c = sbuf("sb_c", [128, DC])
    cb1_c = sbuf("cb1_c", [128, DC])
    gb_c = sbuf("gb_c", [128, DC])
    gabout_c = sbuf("gabout_c", [128, DC])
    gbsd_c = sbuf("gbsd_c", [128, DC])
    sd_c = sbuf("sd_c", [128, DC])
    bout_c = sbuf("bout_c", [128, DC])
    sg_c = sbuf("sg_c", [128, HC])
    su_c = sbuf("su_c", [128, HC])
    negm0 = sbuf("negm0", [128, HPC * NSTRIP])
    m0s = sbuf("m0s", [1, 4])
    idx16f = sbuf("idx16f", [128, DC])
    idx16 = [sbuf(f"idx16_{g}", [128, DC], I32) for g in range(2)]

    block = es.enter_context(nc.Block())
    ck = {}

    def dma(eng, stream, out, in_, **kw):
        sem = getsem(stream)
        eng.dma_start(out=out, in_=in_, **kw).then_inc(sem, 16)
        return (stream, bump(stream, 16))

    def t_inc(inst):
        inst.then_inc(sems["TS"], 1)
        return ("TS", bump("TS", 1))

    def v_inc(inst):
        inst.then_inc(sems["VS"], 1)
        return ("VS", bump("VS", 1))

    def s_inc(inst):
        inst.then_inc(sems["SS"], 1)
        return ("SS", bump("SS", 1))

    def c_inc(inst):
        inst.then_inc(sems["CC"], 1)
        return ("CC", bump("CC", 1))

    RG = [list(range(NCORES))]

    p23 = ExitStack()
    wq_sb = sbuf("wq_sb", [128, DC * ESL], BF16, st=p23, side="right")
    corr4 = sbuf("corr4", [34, ESL], BF16, st=p23, side="right")
    corr_l = corr4[:][32:34, :]
    rows4 = sbuf("rows4", [34, S], BF16, st=p23, side="right")
    xw = [sbuf(f"xw{i}", [128, DC * WIN], BF16, st=p23, side="right")
          for i in range(2)]
    v_win = [sbuf(f"vwin{i}", [128, HPC * WIN], BF16, st=p23, side="right")
             for i in range(2)]
    sq_scr = sbuf("sq_scr", [128, WIN], BF16, st=p23, side="right")
    rep_sb = sbuf("rep_sb", [128, WIN], st=p23, side="right")
    rrows = rows4[:][0:2, :]
    crows = rows4[:][32:34, :]

    p01 = ExitStack()
    brow_f = sbuf("brow_f", [1, ESL], st=p01, side="right")
    wm_t = [sbuf(f"wm_t{i}", [128, 12 * 128], st=p01, side="right")
            for i in range(2)]
    xo_b = [sbuf(f"xo_b{i}", [128, TOK], BF16, st=p01, side="right")
            for i in range(2)]
    xo_s = [sbuf(f"xo_s{i}", [128, TOK], BF16, st=p01, side="right")
            for i in range(2)]
    stat8 = sbuf("stat8", [NCORES, 2 * TOK], st=p01, side="right")
    r8 = sbuf("r8", [NCORES, TOK], st=p01, side="right")
    m8n = sbuf("m8n", [NCORES, TOK], st=p01, side="right")
    lnscr8 = sbuf("lnscr8", [NCORES, TOK], st=p01, side="right")
    r8hi = sbuf("r8hi", [NCORES, TOK], BF16, st=p01, side="right")
    r8lo = sbuf("r8lo", [NCORES, TOK], BF16, st=p01, side="right")
    st_m = sbuf("st_m", [1, TOK], st=p01, side="right")
    st_v = sbuf("st_v", [1, TOK], st=p01, side="right")
    st_sd = sbuf("st_sd", [1, TOK], st=p01, side="right")
    modc = sbuf("modc", [128, NCORES * 12], st=p01, side="right")
    modacc = sbuf("modacc", [128, 12], st=p01, side="right")
    g_scr = sbuf("g_scr", [128, DC], st=p01, side="right")
    p_mod = psum("p_mod", [128, 12], st=p01)
    SQO = max(TOK, 512)
    p_st = psum("p_st", [1, SQO + TOK], st=p01)
    p_gemv = psum("p_gemv", [64, ESL], st=p01)

    @block.sync
    def _(sync):
        dma(sync, "cst", c_sb[:], c_i)
        dma(sync, "cst", tri01[:], tri_i)
        dma(sync, "cst", pr16[:], pr16_i)
        dma(sync, "cst", gbase[:], gbase_i)
        dma(sync, "cst", sg_c[:], sg_i)
        dma(sync, "cst", su_c[:], su_i)
        dma(sync, "cst", sd_c[:], sd_i)
        dma(sync, "cst", bout_c[:], bout_i)
        ck["ld_consts"] = dma(sync, "cst", brow_f[0:1, :], bqkv_i)

    @block.vector
    def _(v):
        v.memset(ones_cf[:], 1.0)
        v.memset(ones_cb[:], 1.0)
        v.memset(ones2b[:], 1.0)
        v.memset(neg128b[:], -128.0)
        v.memset(ones_rf[:], 1.0)
        ck["v_consts"] = v_inc(v.memset(ones_rb[:], 1.0))

    @block.scalar
    def _(s):
        wait(s, ck["ld_consts"])
        ck["s_csilu"] = s_inc(s.activation(csilu[:], c_sb[:], ACT.Silu))

    ck["xo_ld"] = {}
    ck["xo_sq"] = {}
    ck["xo_mm"] = {}
    for cd in range(DC):
        @block.gpsimd
        def _(g, cd=cd):
            if cd >= 2:
                wait(g, ck["xo_mm"][cd - 2])
            ck["xo_ld"][cd] = dma(g, f"xo{cd % 2}", xo_b[cd % 2][:],
                                  xown_i[:, cd * TOK:(cd + 1) * TOK])

        @block.vector
        def _(v, cd=cd):
            wait(v, ck["xo_ld"][cd])
            ck["xo_sq"][cd] = v_inc(v.tensor_mul(
                xo_s[cd % 2][:], xo_b[cd % 2][:], xo_b[cd % 2][:]))

        @block.tensor
        def _(t, cd=cd):
            wait(t, ck["xo_ld"][cd])
            if cd == 0:
                wait(t, ck["v_consts"])
            last = None
            for (lo, hi) in groups(TOK):
                last = t.matmul(p_st[0:1, lo:hi], ones_cb[:],
                                xo_b[cd % 2][:, lo:hi],
                                start=(cd == 0), stop=(cd == DC - 1),
                                skip_group_check=True)
            wait(t, ck["xo_sq"][cd])
            for (lo, hi) in groups(TOK):
                last = t.matmul(p_st[0:1, SQO + lo:SQO + hi], ones_cb[:],
                                xo_s[cd % 2][:, lo:hi],
                                start=(cd == 0), stop=(cd == DC - 1),
                                skip_group_check=True)
            ck["xo_mm"][cd] = t_inc(last)
    ck["mm_sq"] = ck["xo_mm"][DC - 1]

    ck["ld_wm"] = {}


    ck["mm_mod"] = []
    ck["v_macc"] = {}

    @block.vector
    def _(v):
        ck["v_macc0"] = v_inc(v.memset(modacc[:], 0.0))

    for cd in range(DC):
        @block.sync
        def _(sync, cd=cd):
            if cd >= 2:
                wait(sync, ck["mm_mod"][cd - 2])
            ck["ld_wm"][cd] = dma(sync, f"wm{cd % 2}", wm_t[cd % 2][:],
                                  wm_i[:, cd * 1536:(cd + 1) * 1536])

        @block.tensor
        def _(t, cd=cd):
            if cd == 0:
                wait(t, ck["s_csilu"])
            wait(t, ck["ld_wm"][cd])
            if cd >= 1:
                wait(t, ck["v_macc"][cd - 1])
            last = None
            for j in range(12):
                last = t.matmul(p_mod[:, j:j + 1],
                                wm_t[cd % 2][:, j * 128:(j + 1) * 128],
                                csilu[:, cd:cd + 1],
                                start=True, stop=True,
                                skip_group_check=True)
            ck["mm_mod"].append(t_inc(last))

        @block.vector
        def _(v, cd=cd):
            wait(v, ck["mm_mod"][cd])
            if cd == 0:
                wait(v, ck["v_macc0"])
            ck["v_macc"][cd] = v_inc(v.tensor_add(modacc[:], modacc[:],
                                                   p_mod[:]))
            v.drain()

    @block.vector
    def _(v):
        wait(v, ck["v_macc"][DC - 1])
        ck["v_moda"] = v_inc(v.tensor_copy(out=modc[:, 0:4],
                                           in_=modacc[:, 0:4]))
        ck["v_modb"] = v_inc(v.tensor_copy(out=modc[:, 4:12],
                                           in_=modacc[:, 4:12]))

    @block.sync
    def _(sync):
        wait(sync, ck["v_moda"])
        ck["st_moda"] = dma(sync, "stma", agma_i.ap(), modc[:, 0:4])
        wait(sync, ck["v_modb"])
        ck["st_modb"] = dma(sync, "stmb", agmb_i.ap(), modc[:, 4:12])

    @block.vector
    def _(v):
        wait(v, ck["mm_sq"])
        v.tensor_scalar_mul(st_m[:], p_st[0:1, 0:TOK], 1.0 / D)
        v.tensor_scalar_mul(st_v[:], p_st[0:1, SQO:SQO + TOK], 1.0 / D)
        v.drain()
        v.tensor_mul(lnscr8[0:1, :], st_m[:], st_m[:])
        v.drain()
        v.tensor_sub(st_v[:], st_v[:], lnscr8[0:1, :])
        v.drain()
        ck["v_var"] = v_inc(v.tensor_scalar_add(st_v[:], st_v[:], LN_EPS))

    @block.scalar
    def _(s):
        wait(s, ck["v_var"])
        ck["s_sd"] = s_inc(s.activation(st_sd[:], st_v[:], ACT.Sqrt))

    @block.sync
    def _(sync):
        wait(sync, ck["v_var"])
        dma(sync, "stst", agst_i.ap()[0:1, 0:TOK], st_m[:])
        wait(sync, ck["s_sd"])
        ck["st_stats"] = dma(sync, "stst", agst_i.ap()[0:1, TOK:2 * TOK],
                             st_sd[:])

    def emit_xwin(g, w):
        r = None
        for cd in range(DC):
            r = dma(g, f"xw{w % 2}", xw[w % 2][:, cd * WIN:(cd + 1) * WIN],
                    xfull[:, cd * S + w * WIN:cd * S + (w + 1) * WIN])
        return r

    ck["xwin"] = {}

    @block.gpsimd
    def _(g):
        ck["xwin"][0] = emit_xwin(g, 0)
        ck["ld_wq"] = []
        for cd in range(DC):
            ck["ld_wq"].append(
                dma(g, "wqc", wq_sb[:, cd * ESL:(cd + 1) * ESL],
                    wqkv_i[:, cd * ESL:(cd + 1) * ESL]))
        ck["xwin"][1] = emit_xwin(g, 1)
        wait(g, ck["st_stats"])
        ck["cc_stats"] = c_inc(g.collective_compute(
            "AllGather", ALU.bypass, replica_groups=RG,
            ins=[agst_i.ap().opt()], outs=[agst_o.ap().opt()]))
        wait(g, ck["st_moda"])
        ck["cc_moda"] = c_inc(g.collective_compute(
            "AllGather", ALU.bypass, replica_groups=RG,
            ins=[agma_i.ap().opt()], outs=[agma_o.ap().opt()]))
        wait(g, ck["st_modb"])
        ck["cc_modb"] = c_inc(g.collective_compute(
            "AllGather", ALU.bypass, replica_groups=RG,
            ins=[agmb_i.ap().opt()], outs=[agmb_o.ap().opt()]))

    @block.sync
    def _(sync):
        wait(sync, ck["cc_moda"])
        for b in range(NCORES):
            ck["ld_mca"] = dma(sync, "mbk", modc[:, b * 12:b * 12 + 4],
                               agma_o.ap()[b * 128:(b + 1) * 128, :])
        wait(sync, ck["cc_modb"])
        for b in range(NCORES):
            ck["ld_mcb"] = dma(sync, "mbk", modc[:, b * 12 + 4:b * 12 + 12],
                               agmb_o.ap()[b * 128:(b + 1) * 128, :])
        wait(sync, ck["cc_stats"])
        ck["ld_rows"] = dma(sync, "mbk", stat8[:], agst_o.ap())

    @block.vector
    def _(v):
        wait(v, ck["ld_rows"])
        mv = modc[:].rearrange("p (b j) -> p b j", j=12)
        v.tensor_copy(out=sa_c[:].rearrange("p (b j) -> p b j", j=2),
                      in_=mv[:, :, 0:2])
        v.tensor_copy(out=ca1_c[:].rearrange("p (b j) -> p b j", j=2),
                      in_=mv[:, :, 2:4])
        v.drain()
        v.tensor_copy(out=sab_c[:], in_=sa_c[:])
        ck["v_saca"] = v_inc(v.tensor_scalar_add(ca1_c[:], ca1_c[:], 1.0))
        v.tensor_copy(out=ga_c[:].rearrange("p (b j) -> p b j", j=2),
                      in_=mv[:, :, 4:6])
        v.tensor_copy(out=sb_c[:].rearrange("p (b j) -> p b j", j=2),
                      in_=mv[:, :, 6:8])
        v.tensor_copy(out=cb1_c[:].rearrange("p (b j) -> p b j", j=2),
                      in_=mv[:, :, 8:10])
        v.tensor_copy(out=gb_c[:].rearrange("p (b j) -> p b j", j=2),
                      in_=mv[:, :, 10:12])
        v.drain()
        v.tensor_scalar_add(cb1_c[:], cb1_c[:], 1.0)
        v.drain()
        for gt in (ga_c, gb_c):
            v.tensor_scalar(g_scr[:], gt[:], 0.0, 2.0, ALU.is_ge, ALU.mult)
            v.drain()
            v.tensor_scalar_add(g_scr[:], g_scr[:], -1.0)
            v.drain()
            v.tensor_mul(gt[:], gt[:], g_scr[:])
            v.drain()
            v.tensor_scalar_max(gt[:], gt[:], 0.25)
            v.drain()
            v.tensor_mul(gt[:], gt[:], g_scr[:])
            v.drain()
        v.tensor_mul(gabout_c[:], ga_c[:], bout_c[:])
        ck["v_mod"] = v_inc(v.tensor_mul(gbsd_c[:], gb_c[:], sd_c[:]))
        v.tensor_scalar(idx16f[:], pr16[:], gbase[:, 0:1], None, ALU.add)
        v.drain()
        v.tensor_copy(out=idx16[0][:], in_=idx16f[:])
        v.drain()
        v.tensor_scalar(idx16f[:], pr16[:], gbase[:, 1:2], None, ALU.add)
        v.drain()
        ck["v_idx"] = v_inc(v.tensor_copy(out=idx16[1][:], in_=idx16f[:]))

    @block.scalar
    def _(s):
        wait(s, ck["ld_rows"])
        s.activation(lnscr8[:], stat8[:, TOK:2 * TOK], ACT.Ln)
        s.drain()
        ck["s_rall"] = s_inc(s.activation(r8[:], lnscr8[:], ACT.Exp,
                                          scale=-1.0))

    @block.vector
    def _(v):
        wait(v, ck["s_rall"])
        v.tensor_copy(out=r8hi[:], in_=r8[:])
        v.drain()
        v.tensor_sub(lnscr8[:], r8[:], r8hi[:])
        v.drain()
        v.tensor_copy(out=r8lo[:], in_=lnscr8[:])
        ck["v_rows"] = v_inc(v.tensor_scalar_mul(m8n[:], stat8[:, 0:TOK],
                                                 -1.0))

    @block.sync
    def _(sync):
        wait(sync, ck["v_rows"])
        for K in range(16):
            sK = K if K <= 7 else 15 - K
            off = 0 if K <= 7 else TOKC
            cs = slice(K * TOKC, (K + 1) * TOKC)
            dma(sync, "rw2", rows4[0:1, cs], r8hi[sK:sK + 1, off:off + TOKC])
            dma(sync, "rw2", rows4[1:2, cs], r8lo[sK:sK + 1, off:off + TOKC])
        ck["d_rw2"] = ("rw2", cnt["rw2"])

    @block.gpsimd
    def _(g):
        wait(g, ck["v_rows"])
        for K in range(16):
            sK = K if K <= 7 else 15 - K
            off = 0 if K <= 7 else TOKC
            cs = slice(K * TOKC, (K + 1) * TOKC)
            dma(g, "rwd", rows4[32:33, cs], m8n[sK:sK + 1, off:off + TOKC])
            ck["d_sd"] = dma(g, "rwd", rows4[33:34, cs],
                             stat8[sK:sK + 1, TOK + off:TOK + off + TOKC])

    ck["wp_gemv"] = []
    ck["wp_scale"] = []
    ck["wp_sum"] = []
    for cd in range(DC):
        @block.tensor
        def _(t, cd=cd):
            if cd == 0:
                wait(t, ck["ld_wq"][DC - 1])
                wait(t, ck["v_saca"])
            last = None
            for (lo, hi) in groups(ESL):
                last = t.matmul(p_gemv[0:1, lo:hi], sab_c[:, cd:cd + 1],
                                wq_sb[:, cd * ESL + lo:cd * ESL + hi],
                                start=(cd == 0), stop=(cd == DC - 1))
            ck["wp_gemv"].append(t_inc(last))

        @block.vector
        def _(v, cd=cd):
            wait(v, ck["wp_gemv"][cd])
            ck["wp_scale"].append(v_inc(v.tensor_scalar(
                wq_sb[:, cd * ESL:(cd + 1) * ESL],
                wq_sb[:, cd * ESL:(cd + 1) * ESL],
                ca1_c[:, cd:cd + 1], None, ALU.mult)))

        @block.tensor
        def _(t, cd=cd):
            wait(t, ck["wp_scale"][cd])
            last = None
            for (lo, hi) in groups(ESL):
                last = t.matmul(p_gemv[32:33, lo:hi], ones_cb[:],
                                wq_sb[:, cd * ESL + lo:cd * ESL + hi],
                                start=(cd == 0), stop=(cd == DC - 1))
            ck["wp_sum"].append(t_inc(last))

    @block.vector
    def _(v):
        wait(v, ck["wp_sum"][-1])
        v.tensor_add(brow_f[0:1, :], brow_f[0:1, :], p_gemv[0:1, :])
        v.drain()
        ck["v_corr"] = v_inc(v.tensor_copy(out=corr4[32:33, :],
                                           in_=p_gemv[32:33, :]))

    @block.gpsimd
    def _(g):
        wait(g, ck["v_corr"])
        ck["d_corr"] = dma(g, "rwd", corr4[33:34, :], brow_f[0:1, :])

    p01.close()

    p34 = ExitStack()
    q_fm = sbuf("q_fm", [128, HPC * S], BF16, st=p34)
    k_fm = sbuf("k_fm", [128, HPC * S], BF16, st=p34)
    v_tok = sbuf("v_tok", [128, HPC * S], BF16, st=p34)
    nrm_q = sbuf("nrm_q", [1, HPC * NWIN], st=p34)
    nrm_k = sbuf("nrm_k", [1, HPC * NWIN], st=p34)
    p3p = ExitStack()
    p_qkv = [psum(f"p_qkv{i}", [128, WIN], st=p3p) for i in range(2)]
    p_rep = psum("p_rep", [128, WIN], st=p3p)
    p_nrm = psum("p_nrm", [1, WIN], st=p3p)

    def qkv_dst(o, w):
        hl = o % 2
        if o < 2:
            return q_fm[:, hl * S + w * WIN:hl * S + (w + 1) * WIN]
        if o < 4:
            return k_fm[:, hl * S + w * WIN:hl * S + (w + 1) * WIN]
        return v_win[w % 2][:, hl * WIN:(hl + 1) * WIN]

    ck["win_done"] = {}
    ck["win_evac"] = {}
    ck["rep_cp"] = {}
    ck["win_tr"] = {}
    last_evac_buf = {0: None, 1: None}
    last_vwin_tr = {0: None, 1: None}
    last_nrm_v = None

    for w in range(NWIN):
        if w >= 2:
            @block.gpsimd
            def _(g, w=w):
                wait(g, ck["win_done"][w - 2])
                ck["xwin"][w] = emit_xwin(g, w)

        @block.tensor
        def _(t, w=w):
            wait(t, ck["xwin"][w])
            if w == 0:
                wait(t, ck["v_corr"])
                wait(t, ck["v_rows"])
                wait(t, ck["d_corr"])
                wait(t, ck["d_rw2"])
            if w >= 1:
                wait(t, ck["rep_cp"][w - 1])
            ck[("rep_mm", w)] = t_inc(t.matmul(
                p_rep[:], ones2b[:],
                rrows[:, w * WIN:(w + 1) * WIN], start=True, stop=True))

        @block.vector
        def _(v, w=w):
            wait(v, ck[("rep_mm", w)])
            ck["rep_cp"][w] = v_inc(v.tensor_copy(out=rep_sb[:], in_=p_rep[:]))

        for o in range(6):
            buf = o % 2

            @block.tensor
            def _(t, w=w, o=o, buf=buf):
                if last_evac_buf[buf] is not None:
                    wait(t, last_evac_buf[buf])
                for cd in range(DC):
                    t.matmul(p_qkv[buf][:],
                             wq_sb[:, cd * ESL + o * 128:cd * ESL + (o + 1) * 128],
                             xw[w % 2][:, cd * WIN:(cd + 1) * WIN],
                             start=(cd == 0), stop=False)
                last = t.matmul(p_qkv[buf][:],
                                corr_l[:, o * 128:(o + 1) * 128],
                                crows[:, w * WIN:(w + 1) * WIN],
                                start=False, stop=True)
                ck[("qkv_t", w, o)] = t_inc(last)

            @block.vector
            def _(v, w=w, o=o, buf=buf):
                wait(v, ck[("qkv_t", w, o)])
                if o == 0:
                    wait(v, ck["rep_cp"][w])
                if o >= 4 and last_vwin_tr[w % 2] is not None:
                    wait(v, last_vwin_tr[w % 2])
                last_evac_buf[buf] = v_inc(v.tensor_mul(
                    qkv_dst(o, w), p_qkv[buf][:], rep_sb[:]))

        ck["win_done"][w] = ("TS", cnt["TS"])
        ck["win_evac"][w] = ("VS", cnt["VS"])

        for i, (tens, nrm) in enumerate(((q_fm, nrm_q), (k_fm, nrm_k))):
            for hl in range(HPC):
                @block.vector
                def _(v, w=w, i=i, hl=hl, tens=tens):
                    wait(v, ck["win_evac"][w])
                    ck[("nrm_sq", w, i, hl)] = v_inc(v.tensor_mul(
                        sq_scr[:],
                        tens[:, hl * S + w * WIN:hl * S + (w + 1) * WIN],
                        tens[:, hl * S + w * WIN:hl * S + (w + 1) * WIN]))

                @block.tensor
                def _(t, w=w, i=i, hl=hl):
                    wait(t, ck[("nrm_sq", w, i, hl)])
                    if last_nrm_v is not None:
                        wait(t, last_nrm_v)
                    ck[("nrm_t", w, i, hl)] = t_inc(t.matmul(
                        p_nrm[:], ones_cb[:], sq_scr[:],
                        start=True, stop=True))

                @block.vector
                def _(v, w=w, i=i, hl=hl, nrm=nrm):
                    wait(v, ck[("nrm_t", w, i, hl)])
                    ck[("nrm_v", w, i, hl)] = v_inc(v.reduce_max(
                        nrm[0:1, hl * NWIN + w:hl * NWIN + w + 1],
                        p_nrm[:], AXX))
                last_nrm_v = ck[("nrm_v", w, i, hl)]

        @block.sync
        def _(sync, w=w):
            wait(sync, ck["win_evac"][w])
            r = None
            for hl in range(HPC):
                for j in range(WIN // 128):
                    tok0 = w * WIN + j * 128
                    r = dma(sync, f"vt{w % 2}",
                            v_tok[:, hl * S + tok0:hl * S + tok0 + 128],
                            v_win[w % 2][:, hl * WIN + j * 128:hl * WIN + (j + 1) * 128],
                            transpose=True)
            last_vwin_tr[w % 2] = r
            ck["win_tr"][w] = r

    ck["vt_all"] = [("vt0", cnt.get("vt0", 0)), ("vt1", cnt.get("vt1", 0))]

    if dbg:
        @block.sync
        def _(sync):
            wait(sync, ck["d_corr"])
            wait(sync, ck["d_rw2"])
            dma(sync, "dbgq", dbg_mod, modc[:])
            dma(sync, "dbgq", dbg_st, stat8[:])
            dma(sync, "dbgq", dbg_rows, rows4[:])
            dma(sync, "dbgq", dbg_corr, corr4[:])
            wait(sync, ck["win_evac"][NWIN - 1])
            wait(sync, ck["win_tr"][NWIN - 1])
            dma(sync, "dbgq", dbg_qf, q_fm[:])
            dma(sync, "dbgq", dbg_kf, k_fm[:])
            ck["dbg1"] = dma(sync, "dbgq", dbg_vt, v_tok[:])

    p23.close()
    p3p.close()

    pW = ExitStack()
    amg = sbuf("amg", [128, DC * TOKC], BF16, st=pW, side="right")
    wo_raw = [sbuf(f"wo_raw{i}", [128, DC * 128], F32, st=pW, side="right")
              for i in range(2)]
    wo_buf = [sbuf(f"wo_buf{i}", [128, DC * 128], BF16, st=pW, side="right")
              for i in range(2)]

    p4s = ExitStack()
    ptile = [sbuf(f"pt{i}", [128, SW], BF16, st=p4s, side="right")
             for i in range(3)]
    pacc = sbuf("pacc", [128, SW], st=p4s, side="right")
    sum_row = sbuf("sum_row", [1, SW], st=p4s, side="right")
    lnrow = sbuf("lnrow", [1, SW], st=p4s, side="right")
    rinv_row = sbuf("rinv_row", [1, SW], BF16, st=p4s, side="right")
    astrip = [sbuf(f"astrip{i}", [128, SW], BF16, st=p4s, side="right")
              for i in range(2)]
    nrep_sb = sbuf("nrep_sb", [128, SW], st=p4s, side="right")
    p_sc = [psum(f"p_sc{i}", [128, SW], st=p4s) for i in range(3)]
    p_out = psum("p_out", [128, SW], st=p4s)

    @block.vector
    def _(v):
        ck["v_pacc0"] = v_inc(v.memset(pacc[:], 0.0))

    last_strip_evac = None
    last_strip_recip = None
    last_pacc_reset = ck["v_pacc0"]
    ck["cc_strip"] = {}

    for s in range(NSTRIP):
        for hl in range(HPC):
            sidx = s * HPC + hl
            wq0 = (s * SW) // WIN
            wq1 = ((s + 1) * SW + WIN - 1) // WIN

            @block.vector
            def _(v, hl=hl, wq0=wq0, wq1=wq1, sidx=sidx):
                if sidx > 0:
                    wait(v, ck[("m0", sidx - 1)])
                v.reduce_max(m0s[0:1, 0:1],
                             nrm_q[0:1, hl * NWIN + wq0:hl * NWIN + wq1], AXX)
                v.reduce_max(m0s[0:1, 1:2],
                             nrm_k[0:1, hl * NWIN:hl * NWIN + wq1], AXX)
                v.drain()
                ck["m0a"] = v_inc(v.tensor_mul(m0s[0:1, 2:3], m0s[0:1, 0:1],
                                               m0s[0:1, 1:2]))

            @block.scalar
            def _(s_):
                wait(s_, ck["m0a"])
                s_.activation(m0s[0:1, 3:4], m0s[0:1, 2:3], ACT.Ln)
                s_.drain()
                ck["m0b"] = s_inc(s_.activation(m0s[0:1, 2:3], m0s[0:1, 3:4],
                                                ACT.Exp, scale=0.5))

            @block.vector
            def _(v):
                wait(v, ck["m0b"])
                ck["m0c"] = v_inc(v.tensor_scalar(
                    m0s[0:1, 0:1], m0s[0:1, 2:3], -ISQ, -1.0,
                    ALU.mult, ALU.add))

            @block.tensor
            def _(t, sidx=sidx):
                wait(t, ck["m0c"])
                ck["m0t"] = t_inc(t.matmul(p_sc[0][:, 0:1], ones_rf[:],
                                           m0s[0:1, 0:1],
                                           start=True, stop=True))

            @block.vector
            def _(v, sidx=sidx):
                wait(v, ck["m0t"])
                ck[("m0", sidx)] = v_inc(v.tensor_copy(
                    out=negm0[:, sidx:sidx + 1], in_=p_sc[0][:, 0:1]))

    for s in range(NSTRIP):
        for hl in range(HPC):
            sidx = s * HPC + hl
            nkt = (s + 1) * SKT

            exp_ck = {}
            mask_ck = {}
            pacc_ck = {}
            attnv_ck = {}

            def emit_attnv(kt):
                qlo = max(0, 128 * kt - s * SW)
                diag = 128 * kt >= s * SW
                buf = kt % 3

                @block.tensor
                def _(t, s=s, hl=hl, kt=kt, qlo=qlo, buf=buf, nkt=nkt,
                      diag=diag):
                    if kt == 0 and last_strip_evac is not None:
                        wait(t, last_strip_evac)
                    if diag:
                        wait(t, mask_ck[kt])
                    else:
                        wait(t, exp_ck[kt])
                    last = None
                    for b in range(qlo // 512, NB):
                        lo = max(qlo, 512 * b)
                        hi = min(SW, 512 * (b + 1))
                        ktl = min(nkt - 1, s * SKT + 4 * (b + 1) - 1)
                        last = t.matmul(
                            p_out[:, lo:hi],
                            v_tok[:, hl * S + kt * 128:hl * S + (kt + 1) * 128],
                            ptile[buf][:, lo:hi],
                            start=(kt == 0), stop=(kt == ktl))
                    attnv_ck[kt] = t_inc(last)

            for kt in range(nkt):
                qlo = max(0, 128 * kt - s * SW)
                diag = 128 * kt >= s * SW
                buf = kt % 3

                @block.tensor
                def _(t, s=s, hl=hl, kt=kt, qlo=qlo, buf=buf, sidx=sidx,
                      wq1=wq1):
                    if kt == 0:
                        wait(t, ck[("m0", NSTRIP * HPC - 1)])
                        wait(t, ck["vt_all"][0])
                        wait(t, ck["vt_all"][1])
                        if last_strip_recip is not None:
                            wait(t, last_strip_recip)
                    if kt >= 3:
                        wait(t, exp_ck[kt - 3])
                    last = None
                    for (lo, hi) in groups(SW, qlo):
                        last = t.matmul(
                            p_sc[buf][:, lo:hi],
                            k_fm[:, hl * S + kt * 128:hl * S + (kt + 1) * 128],
                            q_fm[:, hl * S + s * SW + lo:hl * S + s * SW + hi],
                            start=True, stop=True)
                    ck[("sc", sidx, kt)] = t_inc(last)

                @block.scalar
                def _(s_, kt=kt, qlo=qlo, buf=buf, sidx=sidx):
                    wait(s_, ck[("sc", sidx, kt)])
                    if kt >= 3:
                        wait(s_, attnv_ck[kt - 3])
                        wait(s_, pacc_ck[kt - 3])
                    exp_ck[kt] = s_inc(s_.activation(
                        ptile[buf][:, qlo:SW], p_sc[buf][:, qlo:SW], ACT.Exp,
                        bias=negm0[:, sidx:sidx + 1], scale=ISQ))

                @block.vector
                def _(v, kt=kt, qlo=qlo, buf=buf, diag=diag):
                    wait(v, exp_ck[kt])
                    if diag:
                        mask_ck[kt] = v_inc(v.tensor_mul(
                            ptile[buf][:, qlo:qlo + 128],
                            ptile[buf][:, qlo:qlo + 128], tri01[:]))
                        v.drain()
                    if kt == 0:
                        wait(v, last_pacc_reset)
                    pacc_ck[kt] = v_inc(v.tensor_add(
                        pacc[:, qlo:SW], pacc[:, qlo:SW],
                        ptile[buf][:, qlo:SW]))
                    v.drain()

                if kt >= 1:
                    emit_attnv(kt - 1)
            emit_attnv(nkt - 1)

            @block.tensor
            def _(t, nkt=nkt, sidx=sidx):
                wait(t, pacc_ck[nkt - 1])
                for back in range(1, min(3, nkt) + 1):
                    wait(t, exp_ck[nkt - back])
                last = None
                for (lo, hi) in groups(SW):
                    last = t.matmul(p_sc[0][0:1, lo:hi], ones_cf[:],
                                    pacc[:, lo:hi], start=True, stop=True)
                ck[("sums", sidx)] = t_inc(last)

            @block.vector
            def _(v, sidx=sidx):
                wait(v, ck[("sums", sidx)])
                ck[("sumr", sidx)] = v_inc(v.tensor_copy(
                    out=sum_row[:], in_=p_sc[0][0:1, 0:SW]))
                v.drain()
                ck[("paccr", sidx)] = v_inc(v.memset(pacc[:], 0.0))

            if dbg and sidx == 0:
                @block.sync
                def _(sync):
                    wait(sync, ck[("sumr", 0)])
                    dma(sync, "dbgq", dbg_pacc, pacc[:])

            last_pacc_reset = ck[("paccr", sidx)]

            @block.tensor
            def _(t, sidx=sidx, nkt=nkt):
                wait(t, ck[("sumr", sidx)])
                wait(t, attnv_ck[nkt - 1])
                last = None
                for (lo, hi) in groups(SW):
                    last = t.matmul(p_sc[0][:, lo:hi], ones_rf[:],
                                    sum_row[0:1, lo:hi],
                                    start=True, stop=True)
                ck[("repn", sidx)] = t_inc(last)

            @block.vector
            def _(v, sidx=sidx):
                wait(v, ck[("repn", sidx)])
                ck[("recip", sidx)] = v_inc(v.reciprocal(out=nrep_sb[:],
                                                         in_=p_sc[0][:]))
                v.drain()
                if sidx >= 2:
                    wait(v, ck[("agst", sidx - 2)])
                ck[("evac", sidx)] = v_inc(v.tensor_mul(
                    astrip[sidx % 2][:], p_out[:], nrep_sb[:]))

            last_strip_evac = ck[("evac", sidx)]
            last_strip_recip = ck[("recip", sidx)]

            @block.gpsimd
            def _(g, s=s, hl=hl, sidx=sidx):
                wait(g, ck[("evac", sidx)])
                ck[("agst", sidx)] = dma(
                    g, f"ag{sidx % 2}",
                    ago_in.ap()[sidx * 128:(sidx + 1) * 128, :],
                    astrip[sidx % 2][:])

        @block.gpsimd
        def _(g, s=s):
            for hl in range(HPC):
                wait(g, ck[("agst", s * HPC + hl)])
            ck["cc_strip"][s] = c_inc(g.collective_compute(
                "AllGather", ALU.bypass, replica_groups=RG,
                ins=[ago_in.ap()[s * HPC * 128:(s + 1) * HPC * 128, :].opt()],
                outs=[(ago_ab if s < NSTRIP // 2 else ago_cd).ap()
                      [(s % (NSTRIP // 2)) * D:
                       (s % (NSTRIP // 2) + 1) * D, :].opt()]))

        if s == NSTRIP // 2 - 1:
            @block.gpsimd
            def _(g):
                wait(g, ck["cc_strip"][NSTRIP // 2 - 1])
                wait(g, ck["v_idx"])
                r = None
                for rr in range(DC):
                    g.indirect_dma_start(
                        amg[:, rr * TOKC:(rr + 1) * TOKC], None,
                        ago_flat_a,
                        IndirectOffsetOnAxis(idx16[0][:, rr:rr + 1], 0),
                    ).then_inc(getsem("gth"), 16)
                    r = ("gth", bump("gth", 16))
                ck[("gather", 0)] = r

    p4s.close()
    p34.close()

    pH = ExitStack()
    h2 = sbuf("h2", [128, DC * TOK], BF16, st=pH)
    p67 = ExitStack()
    xr_roll = [sbuf(f"xr{i}", [128, TOKC], st=p67, side="right")
               for i in range(2)]
    x1_roll = [sbuf(f"x1r{i}", [128, TOKC], st=p67, side="right")
               for i in range(2)]
    x1c_bf = sbuf("x1c_bf", [128, DC * TOKC], BF16, st=p67, side="right")
    x1s_bf = sbuf("x1s_bf", [128, DC * TOKC], BF16, st=p67, side="right")
    ln2row = sbuf("ln2row", [1, 2 * TOKC], st=p67, side="right")
    ln2m = sbuf("ln2m", [1, TOKC], st=p67, side="right")
    ln2v = sbuf("ln2v", [1, TOKC], st=p67, side="right")
    ln2sd = sbuf("ln2sd", [1, TOKC], st=p67, side="right")
    ln2r = sbuf("ln2r", [1, TOKC], st=p67, side="right")
    apply_scr = sbuf("apply_scr", [128, TOKC], st=p67, side="right")
    p_op = [psum(f"p_op{i}", [128, TOKC], st=p67) for i in range(2)]
    SQO2 = max(TOKC, 512)
    p_st2 = psum("p_st2", [1, SQO2 + TOKC], st=p67)
    p_r2m = psum("p_r2m", [128, TOKC], st=p67)
    p_r2r = psum("p_r2r", [128, TOKC], st=p67)

    wov = wout_i.rearrange("p (cd e) -> p cd e", e=D)
    h2_ck = []
    prv = {k: {0: None, 1: None} for k in
           ("woc", "opt", "opx1", "opev", "x1sp")}

    for g_ in range(2):
        if g_ == 1:
            @block.gpsimd
            def _(g):
                wait(g, ck["cc_strip"][NSTRIP - 1])
                wait(g, ck[("opt", 0, DC - 1)])
                r = None
                for rr in range(DC):
                    g.indirect_dma_start(
                        amg[:, rr * TOKC:(rr + 1) * TOKC], None,
                        ago_flat_b,
                        IndirectOffsetOnAxis(idx16[1][:, rr:rr + 1], 0),
                    ).then_inc(getsem("gth"), 16)
                    r = ("gth", bump("gth", 16))
                ck[("gather", 1)] = r

        for o in range(DC):
            bf = o % 2

            @block.sync
            def _(sync, g_=g_, o=o, bf=bf):
                if prv["woc"][bf] is not None:
                    wait(sync, prv["woc"][bf])
                else:
                    wait(sync, ck["win_evac"][NWIN - 1])
                ck[("ld_wor", g_, o)] = dma(
                    sync, f"wor{bf}",
                    wo_raw[bf][:].rearrange("p (cd e) -> p cd e", e=128),
                    wov[:, :, o * 128:(o + 1) * 128])
                if prv["opx1"][bf] is not None:
                    wait(sync, prv["opx1"][bf])
                else:
                    wait(sync, ck[("agst", NSTRIP * HPC - 2)])
                    wait(sync, ck[("agst", NSTRIP * HPC - 1)])
                ck[("ld_xr", g_, o)] = dma(
                    sync, f"xr{bf}", xr_roll[bf][:],
                    xown_i[:, o * TOK + g_ * TOKC:o * TOK + g_ * TOKC + TOKC])

            @block.vector
            def _(v, g_=g_, o=o, bf=bf):
                wait(v, ck[("ld_wor", g_, o)])
                if prv["opt"][bf] is not None:
                    wait(v, prv["opt"][bf])
                ck[("woc", g_, o)] = v_inc(v.tensor_copy(
                    out=wo_buf[bf][:], in_=wo_raw[bf][:]))
            prv["woc"][bf] = ck[("woc", g_, o)]

            @block.tensor
            def _(t, g_=g_, o=o, bf=bf):
                wait(t, ck[("woc", g_, o)])
                if o == 0:
                    wait(t, ck[("gather", g_)])
                if prv["opx1"][bf] is not None:
                    wait(t, prv["opx1"][bf])
                last = None
                for cd in range(DC):
                    last = t.matmul(p_op[bf][:],
                                    wo_buf[bf][:, cd * 128:(cd + 1) * 128],
                                    amg[:, cd * TOKC:(cd + 1) * TOKC],
                                    start=(cd == 0), stop=(cd == DC - 1))
                ck[("opt", g_, o)] = t_inc(last)
            prv["opt"][bf] = ck[("opt", g_, o)]

            @block.vector
            def _(v, g_=g_, o=o, bf=bf):
                wait(v, ck[("opt", g_, o)])
                wait(v, ck[("ld_xr", g_, o)])
                if g_ == 0 and o == 0:
                    wait(v, ck["v_mod"])
                if prv["x1sp"][bf] is not None:
                    wait(v, prv["x1sp"][bf])
                v.tensor_scalar(x1_roll[bf][:], p_op[bf][:],
                                ga_c[:, o:o + 1], gabout_c[:, o:o + 1],
                                ALU.mult, ALU.add)
                v.drain()
                ck[("opx1", g_, o)] = v_inc(v.tensor_add(
                    x1_roll[bf][:], x1_roll[bf][:], xr_roll[bf][:]))
                v.drain()
                ck[("opev", g_, o)] = v_inc(v.tensor_copy(
                    out=x1c_bf[:, o * TOKC:(o + 1) * TOKC],
                    in_=x1_roll[bf][:]))
            prv["opx1"][bf] = ck[("opx1", g_, o)]
            prv["opev"][bf] = ck[("opev", g_, o)]

            @block.sync
            def _(sync, g_=g_, o=o, bf=bf):
                wait(sync, ck[("opev", g_, o)])
                ck[("x1sp", g_, o)] = dma(
                    sync, f"x1s{bf}",
                    x1_sp.ap()[:, o * TOK + g_ * TOKC:o * TOK + g_ * TOKC + TOKC],
                    x1_roll[bf][:])
            prv["x1sp"][bf] = ck[("x1sp", g_, o)]

        if dbg and g_ == 0:
            @block.sync
            def _(sync):
                wait(sync, ck[("m0", NSTRIP * HPC - 1)])
                dma(sync, "dbgq", dbg_m0, negm0[:])
                dma(sync, "dbgq", dbg_nrm[0:1, :], nrm_q[:])
                dma(sync, "dbgq", dbg_nrm[1:2, :], nrm_k[:])
                wait(sync, ck["cc_strip"][NSTRIP - 1])
                dma(sync, "dbgq", dbg_agi, ago_in.ap())
                dma(sync, "dbgq", dbg_agab, ago_ab.ap())
                wait(sync, ck[("opt", 0, DC - 1)])
                ck["dbg2"] = dma(sync, "dbgq", dbg_amg, amg[:])

        @block.vector
        def _(v, g_=g_):
            wait(v, ck[("opev", g_, DC - 1)])
            v.drain()
            ck[("l2sq", g_)] = v_inc(v.tensor_mul(x1s_bf[:], x1c_bf[:],
                                                  x1c_bf[:]))

        @block.tensor
        def _(t, g_=g_):
            wait(t, ck[("l2sq", g_)])
            last = None
            for cd in range(DC):
                for (lo, hi) in groups(TOKC):
                    last = t.matmul(p_st2[0:1, lo:hi], ones_cb[:],
                                    x1c_bf[:, cd * TOKC + lo:cd * TOKC + hi],
                                    start=(cd == 0), stop=(cd == DC - 1),
                                    skip_group_check=True)
            for cd in range(DC):
                for (lo, hi) in groups(TOKC):
                    last = t.matmul(p_st2[0:1, SQO2 + lo:SQO2 + hi], ones_cb[:],
                                    x1s_bf[:, cd * TOKC + lo:cd * TOKC + hi],
                                    start=(cd == 0), stop=(cd == DC - 1),
                                    skip_group_check=True)
            ck[("l2mm", g_)] = t_inc(last)

        @block.vector
        def _(v, g_=g_):
            wait(v, ck[("l2mm", g_)])
            v.tensor_scalar_mul(ln2m[:], p_st2[0:1, 0:TOKC], 1.0 / D)
            v.tensor_scalar_mul(ln2v[:], p_st2[0:1, SQO2:SQO2 + TOKC], 1.0 / D)
            v.drain()
            v.tensor_mul(ln2row[0:1, 0:TOKC], ln2m[:], ln2m[:])
            v.drain()
            v.tensor_sub(ln2v[:], ln2v[:], ln2row[0:1, 0:TOKC])
            v.drain()
            ck[("l2var", g_)] = v_inc(v.tensor_scalar_add(ln2v[:], ln2v[:],
                                                          LN_EPS))

        @block.scalar
        def _(s_, g_=g_):
            wait(s_, ck[("l2var", g_)])
            s_.activation(ln2sd[:], ln2v[:], ACT.Sqrt)
            s_.drain()
            s_.activation(ln2row[0:1, TOKC:2 * TOKC], ln2sd[:], ACT.Ln)
            s_.drain()
            ck[("l2r", g_)] = s_inc(s_.activation(
                ln2r[:], ln2row[0:1, TOKC:2 * TOKC], ACT.Exp, scale=-1.0))

        @block.tensor
        def _(t, g_=g_):
            wait(t, ck[("l2r", g_)])
            if g_ == 1:
                wait(t, ck[("l2apply", 0)])
            last = None
            for (lo, hi) in groups(TOKC):
                t.matmul(p_r2m[:, lo:hi], ones_rf[:], ln2m[0:1, lo:hi],
                         start=True, stop=True)
                last = t.matmul(p_r2r[:, lo:hi], ones_rf[:], ln2r[0:1, lo:hi],
                                start=True, stop=True)
            ck[("l2rep", g_)] = t_inc(last)

        @block.vector
        def _(v, g_=g_):
            wait(v, ck[("l2rep", g_)])
            r = None
            for cd in range(DC):
                v.tensor_sub(apply_scr[:], x1c_bf[:, cd * TOKC:(cd + 1) * TOKC],
                             p_r2m[:])
                v.drain()
                v.tensor_mul(apply_scr[:], apply_scr[:], p_r2r[:])
                v.drain()
                r = v_inc(v.tensor_scalar(
                    h2[:, cd * TOK + g_ * TOKC:cd * TOK + g_ * TOKC + TOKC],
                    apply_scr[:], cb1_c[:, cd:cd + 1], sb_c[:, cd:cd + 1],
                    ALU.mult, ALU.add))
                v.drain()
            ck[("l2apply", g_)] = r

        h2_ck.append(ck[("l2apply", g_)])

    p67.close()
    pW.close()

    pG = ExitStack()
    gu = sbuf("gu", [128, HC * TOK], BF16, st=pG, side="right")
    cs2_rows = sbuf("cs2_rows", [2, TOK], BF16, st=pG, side="right")
    csscr = sbuf("csscr", [1, TOK], st=pG, side="right")
    amr_scr = sbuf("amr_scr", [128, 1], st=pG, side="right")
    up_scr = sbuf("up_scr", [128, TOK], st=pG, side="right")
    p8 = ExitStack()
    gate_b = sbuf("gate_b", [128, TOK], BF16, st=p8, side="right")
    wgi_b = [sbuf(f"wgi{i}", [128, DC * 128], I32, st=p8, side="right")
             for i in range(2)]
    wui_b = [sbuf(f"wui{i}", [128, DC * 128], I32, st=p8, side="right")
             for i in range(2)]
    wgt_b = [sbuf(f"wgt{i}", [128, DC * 128], BF16, st=p8, side="right")
             for i in range(2)]
    wut_b = [sbuf(f"wut{i}", [128, DC * 128], BF16, st=p8, side="right")
             for i in range(2)]
    cs_rows = sbuf("cs_rows", [2, TOK], BF16, st=p8, side="right")
    p_g = [psum(f"p_g{i}", [128, TOK], st=p8) for i in range(2)]
    p_u = [psum(f"p_u{i}", [128, TOK], st=p8) for i in range(2)]

    if dbg:
        @block.sync
        def _(sync):
            wait(sync, h2_ck[1])
            ck["dbg3"] = dma(sync, "dbgq", dbg_h2, h2[:])

    ck["x1s_fin"] = {0: ("x1s0", cnt.get("x1s0", 0)),
                     1: ("x1s1", cnt.get("x1s1", 0))}
    p8d = ExitStack()

    @block.tensor
    def _(t):
        wait(t, h2_ck[1])
        last = None
        for cd in range(DC):
            for (lo, hi) in groups(TOK):
                last = t.matmul(p_g[0][0:1, lo:hi], ones_cb[:],
                                h2[:, cd * TOK + lo:cd * TOK + hi],
                                start=(cd == 0), stop=(cd == DC - 1))
        ck["csh_t"] = t_inc(last)

    @block.vector
    def _(v):
        wait(v, ck["csh_t"])
        v.tensor_copy(out=cs_rows[0:1, :], in_=p_g[0][0:1, 0:TOK])
        v.drain()
        ck["csh_v"] = v_inc(v.tensor_sub(csscr[:], p_g[0][0:1, 0:TOK],
                                         cs_rows[0:1, :]))

    @block.gpsimd
    def _(g):
        wait(g, ck["csh_v"])
        ck["d_csh"] = dma(g, "csd", cs_rows[1:2, :], csscr[:])

    for hh in range(HC):
        @block.scalar
        def _(s_, hh=hh):
            if hh >= 2:
                wait(s_, ck[("wgc", hh - 2)])
            else:
                wait(s_, h2_ck[1])
                wait(s_, ck["x1s_fin"][0])
                wait(s_, ck["x1s_fin"][1])
            ck[("ldwg", hh)] = dma(
                s_, f"wg{hh % 2}", wgi_b[hh % 2][:],
                wg_i[:, hh * DC * 128:(hh + 1) * DC * 128])

        @block.sync
        def _(sync, hh=hh):
            if hh >= 2:
                wait(sync, ck[("wuc", hh - 2)])
            else:
                wait(sync, h2_ck[1])
                wait(sync, ck["x1s_fin"][0])
                wait(sync, ck["x1s_fin"][1])
            ck[("ldwu", hh)] = dma(
                sync, f"wu{hh % 2}", wui_b[hh % 2][:],
                wu_i[:, hh * DC * 128:(hh + 1) * DC * 128])

        @block.vector
        def _(v, hh=hh):
            wait(v, ck[("ldwg", hh)])
            if hh >= 2:
                wait(v, ck[("gmm", hh - 2)])
            ck[("wgc", hh)] = v_inc(v.tensor_copy(out=wgt_b[hh % 2][:],
                                                  in_=wgi_b[hh % 2][:]))
            wait(v, ck[("ldwu", hh)])
            if hh >= 2:
                wait(v, ck[("umm", hh - 2)])
            ck[("wuc", hh)] = v_inc(v.tensor_copy(out=wut_b[hh % 2][:],
                                                  in_=wui_b[hh % 2][:]))

        @block.tensor
        def _(t, hh=hh):
            wait(t, ck[("wgc", hh)])
            if hh == 0:
                wait(t, ck["d_csh"])
            if hh >= 2:
                wait(t, ck[("silu", hh - 2)])
            last = None
            for cd in range(DC):
                for (lo, hi) in groups(TOK):
                    last = t.matmul(
                        p_g[hh % 2][:, lo:hi],
                        wgt_b[hh % 2][:, cd * 128:(cd + 1) * 128],
                        h2[:, cd * TOK + lo:cd * TOK + hi],
                        start=(cd == 0), stop=False)
            for (lo, hi) in groups(TOK):
                last = t.matmul(p_g[hh % 2][:, lo:hi], neg128b[:],
                                cs_rows[:, lo:hi], start=False, stop=True)
            ck[("gmm", hh)] = t_inc(last)
            wait(t, ck[("wuc", hh)])
            if hh >= 2:
                wait(t, ck[("gu", hh - 2)])
            for cd in range(DC):
                for (lo, hi) in groups(TOK):
                    last = t.matmul(
                        p_u[hh % 2][:, lo:hi],
                        wut_b[hh % 2][:, cd * 128:(cd + 1) * 128],
                        h2[:, cd * TOK + lo:cd * TOK + hi],
                        start=(cd == 0), stop=False)
            for (lo, hi) in groups(TOK):
                last = t.matmul(p_u[hh % 2][:, lo:hi], neg128b[:],
                                cs_rows[:, lo:hi], start=False, stop=True)
            ck[("umm", hh)] = t_inc(last)

        @block.scalar
        def _(s_, hh=hh):
            wait(s_, ck[("gmm", hh)])
            if hh >= 2:
                wait(s_, ck[("gu", hh - 2)])
            ck[("silu", hh)] = s_inc(s_.activation(
                gate_b[:], p_g[hh % 2][:], ACT.Silu,
                scale=sg_c[:, hh:hh + 1]))

        @block.vector
        def _(v, hh=hh):
            wait(v, ck[("silu", hh)])
            wait(v, ck[("umm", hh)])
            v.tensor_scalar(up_scr[:], p_u[hh % 2][:], su_c[:, hh:hh + 1],
                            None, ALU.mult)
            v.drain()
            ck[("gu", hh)] = v_inc(v.tensor_mul(
                gu[:, hh * TOK:(hh + 1) * TOK], up_scr[:], gate_b[:]))

    @block.tensor
    def _(t):
        wait(t, ck[("gu", HC - 1)])
        wait(t, ck[("gu", HC - 2)])
        last = None
        for hh in range(HC):
            for (lo, hi) in groups(TOK):
                last = t.matmul(p_u[0][0:1, lo:hi], ones_cb[:],
                                gu[:, hh * TOK + lo:hh * TOK + hi],
                                start=(hh == 0), stop=(hh == HC - 1))
        ck["csg_t"] = t_inc(last)

    @block.vector
    def _(v):
        wait(v, ck["csg_t"])
        v.tensor_copy(out=cs2_rows[0:1, :], in_=p_u[0][0:1, 0:TOK])
        v.drain()
        ck["csg_v"] = v_inc(v.tensor_sub(csscr[:], p_u[0][0:1, 0:TOK],
                                         cs2_rows[0:1, :]))

    @block.gpsimd
    def _(g):
        wait(g, ck["csg_v"])
        ck["d_csg"] = dma(g, "csd", cs2_rows[1:2, :], csscr[:])

    p8.close()
    pH.close()
    wdi_b = [sbuf(f"wdi{i}", [128, HC * 128], I32, st=p8d, side="right")
             for i in range(2)]
    wdt_b = [sbuf(f"wdt{i}", [128, HC * 128], BF16, st=p8d, side="right")
             for i in range(2)]
    x1l = [sbuf(f"x1l{i}", [128, TOK], st=p8d, side="right")
           for i in range(2)]
    x2r = [sbuf(f"x2r{i}", [128, TOK], st=p8d, side="right")
           for i in range(2)]

    for o in range(DC):
        @block.sync
        def _(sync, o=o):
            if o >= 2:
                wait(sync, ck[("wdc", o - 2)])
            else:
                wait(sync, ck[("gu", HC - 1)])
                wait(sync, ck["x1s_fin"][0])
                wait(sync, ck["x1s_fin"][1])
            ck[("ldwd", o)] = dma(
                sync, f"wdl{o % 2}", wdi_b[o % 2][:],
                wd_i[:, o * HC * 128:(o + 1) * HC * 128])
            wait(sync, ck["x1s_fin"][0])
            wait(sync, ck["x1s_fin"][1])
            if o >= 2:
                wait(sync, ck[("dnev", o - 2)])
            ck[("ldx1", o)] = dma(sync, f"lx{o % 2}", x1l[o % 2][:],
                                  x1_sp.ap()[:, o * TOK:(o + 1) * TOK])

        @block.vector
        def _(v, o=o):
            wait(v, ck[("ldwd", o)])
            if o >= 2:
                wait(v, ck[("dmm", o - 2)])
            ck[("wdc", o)] = v_inc(v.tensor_copy(out=wdt_b[o % 2][:],
                                                 in_=wdi_b[o % 2][:]))

        @block.tensor
        def _(t, o=o):
            wait(t, ck[("wdc", o)])
            if o == 0:
                wait(t, ck["d_csg"])
            if o >= 2:
                wait(t, ck[("dnev", o - 2)])
            last = None
            for hh in range(HC):
                for (lo, hi) in groups(TOK):
                    last = t.matmul(
                        p_g[o % 2][:, lo:hi],
                        wdt_b[o % 2][:, hh * 128:(hh + 1) * 128],
                        gu[:, hh * TOK + lo:hh * TOK + hi],
                        start=(hh == 0), stop=False)
            for (lo, hi) in groups(TOK):
                last = t.matmul(p_g[o % 2][:, lo:hi], neg128b[:],
                                cs2_rows[:, lo:hi], start=False, stop=True)
            ck[("dmm", o)] = t_inc(last)

        @block.vector
        def _(v, o=o):
            wait(v, ck[("dmm", o)])
            wait(v, ck[("ldx1", o)])
            if o >= 2:
                wait(v, ck[("ost", o - 2)])
            v.tensor_scalar(x2r[o % 2][:], p_g[o % 2][:],
                            gbsd_c[:, o:o + 1], None, ALU.mult)
            v.drain()
            ck[("dnev", o)] = v_inc(v.tensor_add(
                x2r[o % 2][:], x2r[o % 2][:], x1l[o % 2][:]))

        @block.sync
        def _(sync, o=o):
            wait(sync, ck[("dnev", o)])
            ck[("ost", o)] = dma(sync, f"st{o % 2}",
                                 out_t[:, o * TOK:(o + 1) * TOK],
                                 x2r[o % 2][:])

    p8d.close()
    pG.close()

    @block.sync
    def _(sync):
        sync.wait_ge(sems["st0"], cnt["st0"])
        sync.wait_ge(sems["st1"], cnt["st1"])
        if dbg:
            sync.wait_ge(sems["dbgq"], cnt["dbgq"])

    es.close()
    return nc


S_FULL = 8192
HID_FULL = 5632


def _chunk_major(a):
    Dn, N = a.shape
    return np.ascontiguousarray(
        a.reshape(Dn // 128, 128, N).transpose(1, 0, 2).reshape(128, -1))


def prep_inputs(x, w_qkv, b_qkv, w_out, b_out, wq_g, s_g, wq_u, s_u,
                wq_d, s_d, cvec, wmod, S, HID):
    import ml_dtypes
    HC = HID // 128
    TOKC = S // 16
    xf = np.ascontiguousarray(x[0].T.astype(np.float32))
    xfull = _chunk_major(xf)
    woutT = _chunk_major(np.ascontiguousarray(w_out.T.astype(np.float32)))
    bout_col = np.ascontiguousarray(
        b_out.astype(np.float32).reshape(DC, 128).T)
    wg = np.ascontiguousarray(
        wq_g.reshape(HC, 128, DC, 128).transpose(3, 0, 2, 1)
        .reshape(128, -1).astype(np.int32))
    wu = np.ascontiguousarray(
        wq_u.reshape(HC, 128, DC, 128).transpose(3, 0, 2, 1)
        .reshape(128, -1).astype(np.int32))
    wd = np.ascontiguousarray(
        wq_d.reshape(DC, 128, HC, 128).transpose(3, 0, 2, 1)
        .reshape(128, -1).astype(np.int32))
    sg_col = np.ascontiguousarray(s_g.astype(np.float32).reshape(HC, 128).T)
    su_col = np.ascontiguousarray(s_u.astype(np.float32).reshape(HC, 128).T)
    sd_col = np.ascontiguousarray(s_d.astype(np.float32).reshape(DC, 128).T)
    c_col = np.ascontiguousarray(cvec[0].astype(np.float32).reshape(DC, 128).T)
    tri = (np.arange(128)[:, None] <= np.arange(128)[None, :]
           ).astype(ml_dtypes.bfloat16)
    SW = S // 8
    pr16 = np.empty((128, DC), np.float32)
    for rr in range(DC):
        pr16[:, rr] = (128 * rr + np.arange(128)) * SW

    ins = []
    for c in range(NCORES):
        Ka, Kb = c, 15 - c
        xo = np.empty((128, DC * 2 * TOKC), np.float32)
        for cd in range(DC):
            blk = np.empty((128, 2 * TOKC), np.float32)
            blk[:, :TOKC] = xfull[:, cd * S + Ka * TOKC:cd * S + (Ka + 1) * TOKC]
            blk[:, TOKC:] = xfull[:, cd * S + Kb * TOKC:cd * S + (Kb + 1) * TOKC]
            xo[:, cd * 2 * TOKC:(cd + 1) * 2 * TOKC] = blk
        wsl = np.concatenate([
            w_qkv[256 * c:256 * (c + 1)],
            w_qkv[D + 256 * c:D + 256 * (c + 1)],
            w_qkv[2 * D + 256 * c:2 * D + 256 * (c + 1)]], axis=0)
        wqkv = _chunk_major(np.ascontiguousarray(wsl.T.astype(np.float32)))
        bql = np.concatenate([
            b_qkv[256 * c:256 * (c + 1)],
            b_qkv[D + 256 * c:D + 256 * (c + 1)],
            b_qkv[2 * D + 256 * c:2 * D + 256 * (c + 1)]])[None, :]
        wmc = np.empty((D, 12, 128), np.float32)
        for j in range(12):
            vv, ii = j // 2, j % 2
            col0 = 2048 * vv + 128 * (2 * c + ii)
            wmc[:, j, :] = wmod[:, col0:col0 + 128]
        wm = np.ascontiguousarray(
            wmc.reshape(DC, 128, 12 * 128).transpose(1, 0, 2).reshape(128, -1))
        gb = np.empty((128, 2), np.float32)
        CPS = SW // TOKC
        NSTRIP = S // SW
        for gi, K in enumerate((Ka, Kb)):
            srel = (K // CPS) - (NSTRIP // 2 * gi)
            gb[:, gi] = srel * D * SW + (K % CPS) * TOKC
        ins.append(dict(
            xfull=xfull, xown=np.ascontiguousarray(xo), wm=wm,
            c_col=c_col, wqkv=wqkv,
            bqkv_row=np.ascontiguousarray(bql.astype(np.float32)),
            wout=woutT, bout_col=bout_col, wg=wg, wu=wu, wd=wd,
            sg_col=sg_col, su_col=su_col, sd_col=sd_col, tri=tri,
            pramp16=pr16, gbase=gb))
    return ins


def assemble_out(results, S):
    TOKC = S // 16
    out = np.empty((1, S, D), np.float32)
    for c in range(NCORES):
        r = results[c]["out_t"].reshape(128, DC, 2 * TOKC)
        for gi, K in enumerate((c, 15 - c)):
            blk = r[:, :, gi * TOKC:(gi + 1) * TOKC]
            out[0, K * TOKC:(K + 1) * TOKC, :] = (
                blk.transpose(2, 1, 0).reshape(TOKC, D))
    return out


def kernel(L0_attn_x, L0_attn_w_qkv, L0_attn_b_qkv, L0_attn_w_out,
           L0_attn_b_out, L0_ffn_wq_gate, L0_ffn_s_gate, L0_ffn_wq_up,
           L0_ffn_s_up, L0_ffn_wq_down, L0_ffn_s_down, L0_c, L0_wmod):
    S = int(np.asarray(L0_attn_x).shape[1])
    HID = int(np.asarray(L0_ffn_wq_gate).shape[0])
    nc = build_nc(S, HID)
    ins = prep_inputs(
        np.asarray(L0_attn_x), np.asarray(L0_attn_w_qkv),
        np.asarray(L0_attn_b_qkv), np.asarray(L0_attn_w_out),
        np.asarray(L0_attn_b_out), np.asarray(L0_ffn_wq_gate),
        np.asarray(L0_ffn_s_gate), np.asarray(L0_ffn_wq_up),
        np.asarray(L0_ffn_s_up), np.asarray(L0_ffn_wq_down),
        np.asarray(L0_ffn_s_down), np.asarray(L0_c), np.asarray(L0_wmod),
        S, HID)
    res = run_bass_kernel_spmd(nc, ins, core_ids=list(range(NCORES)))
    return assemble_out(res.results, S)
